# version 09
# baseline (speedup 1.0000x reference)
import numpy as np
import concourse.bass as bass
import concourse.mybir as mybir
from concourse.bass_utils import run_bass_kernel_spmd
from contextlib import ExitStack

F32 = mybir.dt.float32
BF16 = mybir.dt.bfloat16
AF = mybir.ActivationFunctionType
ALU = mybir.AluOpType
AX = mybir.AxisListType

T = 2048
NKC = 8
DM = 1024
NCH = 99
ICH = 11
NEXP = 9
NH = 16
NKV = 4
BIG = 30000.0
PART = 6
FT = 4
NTC = 16


class FakeInst:
    def then_inc(self, sem, k):
        return self


_FI = FakeInst()


class FakeEng:
    def wait_ge(self, *a, **k):
        pass

    def __getattr__(self, name):
        def op(*a, **k):
            return _FI
        return op


_FE = FakeEng()


class SC:
    __slots__ = ("h", "n")

    def __init__(self, h):
        self.h = h
        self.n = 0

    def inc(self, inst, k=1):
        inst.then_inc(self.h, k)
        self.n += k
        return self.n


def build_program(upto="all", taps=()):
    nc = bass.Bass("TRN2", target_bir_lowering=False)
    x_h = nc.dram_tensor("x", [T, DM], F32, kind="ExternalInput").ap()
    wq_h = nc.dram_tensor("wq", [DM, DM], F32, kind="ExternalInput").ap()
    wkv_h = nc.dram_tensor("wkv", [DM, 512], F32, kind="ExternalInput").ap()
    ks_h = nc.dram_tensor("ks", [8, 128], F32, kind="ExternalInput").ap()
    gw_h = nc.dram_tensor("gw", [8, DM], F32, kind="ExternalInput").ap()
    sgw_h = nc.dram_tensor("sgw", [1, DM], F32, kind="ExternalInput").ap()
    wg_h = nc.dram_tensor("wg", [8, DM, 1408], F32, kind="ExternalInput").ap()
    wu_h = nc.dram_tensor("wu", [8, DM, 1408], F32, kind="ExternalInput").ap()
    wd_h = nc.dram_tensor("wd", [8, 1408, DM], F32, kind="ExternalInput").ap()
    swg_h = nc.dram_tensor("swg", [DM, 1408], F32, kind="ExternalInput").ap()
    swu_h = nc.dram_tensor("swu", [DM, 1408], F32, kind="ExternalInput").ap()
    swd_h = nc.dram_tensor("swd", [1408, DM], F32, kind="ExternalInput").ap()
    y_h = nc.dram_tensor("y", [T, DM], F32, kind="ExternalOutput").ap()
    crows_d = nc.dram_tensor("crows_d", [NEXP, T], BF16, kind="Internal").ap()
    tap_out = {}
    for tn, shape, dt in taps:
        tap_out[tn] = nc.dram_tensor("tap_" + tn, shape, dt, kind="ExternalOutput").ap()

    snap = {}
    TEN = {}

    with ExitStack() as st:
        def sb(n_, shape, dt):
            h = st.enter_context(nc.sbuf_tensor(n_, shape, dt))
            TEN[n_] = h
            return h

        x_sb = sb("x_sb", [128, NKC * T], F32)
        xn_sb = sb("xn_sb", [128, NKC * T], BF16)
        stage = sb("stage", [128, 2 * DM], F32)
        wga = sb("wga", [128, 2 * DM], BF16)
        wua = sb("wua", [128, 2 * DM], BF16)
        eye_b = sb("eye_b", [128, 128], BF16)
        eye_f = sb("eye_f", [128, 128], F32)
        ones_b = sb("ones_b", [1, 128], BF16)
        ones_fc = sb("ones_fc", [128, 1], F32)
        ones_fr = sb("ones_fr", [1, 128], F32)
        ks_col = sb("ks_col", [128, 8], F32)
        pro_box = {}
        gwT = sb("gwT", [128, NKC * 9], BF16)
        epsc = sb("epsc", [1, 1], F32)
        lg_sb = sb("lg_sb", [128, NTC * 9], F32)
        ctm = sb("ctm", [128, NTC * 9], F32)
        cm1 = sb("cm1", [128, NTC], F32)
        cm2 = sb("cm2", [128, NTC], F32)
        ci1 = sb("ci1", [128, NTC], F32)
        ci2 = sb("ci2", [128, NTC], F32)
        cdc = sb("cdc", [128, NTC], F32)
        sg1c = sb("sg1c", [128, NTC], F32)
        sgsh = sb("sgsh", [128, NTC], F32)
        c1a = sb("c1a", [128, NTC], F32)
        c2a = sb("c2a", [128, NTC], F32)
        m16a = stage[:, 0:128]
        m16b = stage[:, 128:256]
        m16c = stage[:, 256:384]
        iota16 = sb("iota16", [128, 128], F32)
        iotaB16 = sb("iotaB16", [128, 128], F32)
        c_t9 = sb("c_t9", [9, 2 * 128], BF16)
        rowp = sb("rowp", [1, 2 * T], F32)
        r_mean = rowp[0:1, 0:T]
        r_tmp = rowp[0:1, T:2 * T]
        zr = rowp[0:1, 0:512]
        zrb = sb("zrb", [1, 512], BF16)
        m_bf = sb("m_bf", [128, 2 * 16], BF16)
        m_t = sb("m_t", [16, 2 * 128], BF16)
        mm4 = sb("mm4", [128, 4], F32)

        psA = st.enter_context(nc.psum_tensor("psA", [128, 2048], F32))
        psB = st.enter_context(nc.psum_tensor("psB", [128, 2048], F32))

        def pb(i, p0=0, p1=128, c0=0, c1=512):
            t_ = psA if i < 4 else psB
            base = (i % 4) * 512
            return t_[p0:p1, base + c0:base + c1]

        dw = SC(st.enter_context(nc.semaphore("dw")))
        dx = SC(st.enter_context(nc.semaphore("dx")))
        dwg = [SC(st.enter_context(nc.semaphore(f"dwg{i}"))) for i in range(2)]
        dwd = SC(st.enter_context(nc.semaphore("dwd")))
        dxx = [SC(st.enter_context(nc.semaphore(f"dxx{i}"))) for i in range(2)]
        dxc = [SC(st.enter_context(nc.semaphore(f"dxc{i}"))) for i in range(2)]
        dxl = [SC(st.enter_context(nc.semaphore(f"dxl{i}"))) for i in range(2)]
        dxm = [SC(st.enter_context(nc.semaphore(f"dxm{i}"))) for i in range(2)]
        pe = SC(st.enter_context(nc.semaphore("pe")))
        ve = SC(st.enter_context(nc.semaphore("ve")))
        se = SC(st.enter_context(nc.semaphore("se")))
        ge = SC(st.enter_context(nc.semaphore("ge")))
        ALLSC = [dw, dx, pe, ve, se, ge, dwd] + dwg + dxx + dxc + dxl + dxm

        blk = st.enter_context(nc.Block())

        ctx = {"real": False, "pass": 0}
        bankuse = {}

        def scoped(stk, name, shape, dt):
            h = stk.enter_context(nc.sbuf_tensor(f"{name}_p{ctx['pass']}", shape, dt))
            TEN[name] = h
            return h

        def on(ename, f):
            if ctx["real"]:
                getattr(blk, ename)(f)
            else:
                f(_FE)

        def ev(name):
            return snap.get(name, 0)

        def setev(name, val):
            snap[name] = val

        def bank_claim(eng, bi):
            if bi in bankuse:
                sem, name = bankuse[bi]
                eng.wait_ge(sem.h, ev(name))

        def bank_release(bi, sem, name):
            bankuse[bi] = (sem, name)

        def barrier():
            pev, vev, sev = pe.n, ve.n, se.n

            def _t(t):
                t.wait_ge(ve.h, vev)
                t.wait_ge(se.h, sev)
            on("tensor", _t)

            def _v(v):
                v.wait_ge(pe.h, pev)
                v.wait_ge(se.h, sev)
            on("vector", _v)

            def _s(s):
                s.wait_ge(pe.h, pev)
                s.wait_ge(ve.h, vev)
            on("scalar", _s)

        def xv(kc, f0, f1):
            return x_sb[:, kc * T + f0: kc * T + f1]

        def xnv(kc, f0, f1):
            return xn_sb[:, kc * T + f0: kc * T + f1]

        def emit_prologue():
            ks_st = pro_box["ks_st"]
            gsw_sb = pro_box["gsw_sb"]
            def _g(g):
                ge.inc(g.memset(eye_b[:], 0.0))
                ge.inc(g.memset(eye_f[:], 0.0))
                g.wait_ge(ge.h, ge.n)
                g.affine_select(out=eye_b[:], in_=eye_b[:], compare_op=ALU.not_equal,
                                fill=1.0, base=0, pattern=[[-1, 128]], channel_multiplier=1)
                g.affine_select(out=eye_f[:], in_=eye_f[:], compare_op=ALU.not_equal,
                                fill=1.0, base=0, pattern=[[-1, 128]], channel_multiplier=1)
                g.memset(ones_b[:], 1.0)
                g.memset(ones_fc[:], 1.0)
                g.memset(ones_fr[:], 1.0)
                g.memset(epsc[:], 1e-5)
                g.iota(iota16[:].rearrange("p (tc e) -> p tc e", e=8),
                       pattern=[[0, 16], [1, 8]], base=0, channel_multiplier=0,
                       allow_small_or_imprecise_dtypes=True)
                ge.inc(g.iota(iotaB16[:].rearrange("p (tc e) -> p tc e", e=8),
                              pattern=[[0, 16], [1, 8]], base=int(BIG), channel_multiplier=0,
                              allow_small_or_imprecise_dtypes=True))
                setev("consts", ge.n)
            on("gpsimd", _g)

            def _sy(sy):
                sy.dma_start(ks_st[:], ks_h).then_inc(dx.h, 16)
                sy.dma_start(gsw_sb[0:8, :], gw_h).then_inc(dx.h, 16)
                sy.dma_start(gsw_sb[8:9, :], sgw_h).then_inc(dx.h, 16)
                dx.n += 48
                setev("hdr_dx", dx.n)
                for tc in range(NTC):
                    sy.wait_ge(pe.h, ev(("xtr", tc - 2, NKC - 1)))
                    sm = dxx[tc % 2]
                    sy.dma_start(stage[:, (tc % 2) * DM:(tc % 2 + 1) * DM],
                                 x_h[tc * 128:(tc + 1) * 128, :]).then_inc(sm.h, 16)
                    sm.n += 16
                    setev(("xdma", tc), sm.n)
            on("sync", _sy)

            def _t(t):
                t.wait_ge(ge.h, ev("consts"))
                for tc in range(NTC):
                    t.wait_ge(dxx[tc % 2].h, ev(("xdma", tc)))
                    for kc in range(NKC):
                        bi = kc % 4
                        bank_claim(t, bi)
                        pe.inc(t.transpose(pb(bi, 0, 128, 0, 128),
                                           stage[:, (tc % 2) * DM + kc * 128:(tc % 2) * DM + (kc + 1) * 128],
                                           eye_f[:]))
                        setev(("xtr", tc, kc), pe.n)
                        bank_release(bi, ve, ("xev", tc, kc))
                t.wait_ge(dx.h, ev("hdr_dx"))
                bank_claim(t, 4)
                pe.inc(t.transpose(pb(4, 0, 128, 0, 8), ks_st[:], eye_f[0:8, 0:8]))
                setev("kstr", pe.n)
                bank_release(4, ve, "ksev")
                bank_claim(t, 5)
                for kc in range(NKC):
                    t.wait_ge(ve.h, ev(("gwev", kc - 1)))
                    pe.inc(t.transpose(pb(5, 0, 128, kc * 9, kc * 9 + 9),
                                       gsw_sb[:, kc * 128:(kc + 1) * 128], eye_f[0:9, 0:9]))
                    setev(("gwtr", kc), pe.n)
                bank_release(5, ve, ("gwev", NKC - 1))
            on("tensor", _t)

            def _v(v):
                for tc in range(NTC):
                    for kc in range(NKC):
                        bi = kc % 4
                        v.wait_ge(pe.h, ev(("xtr", tc, kc)))
                        ve.inc(v.tensor_copy(xv(kc, tc * 128, (tc + 1) * 128), pb(bi, 0, 128, 0, 128)))
                        setev(("xev", tc, kc), ve.n)
                v.wait_ge(pe.h, ev("kstr"))
                ve.inc(v.tensor_scalar(ks_col[:], pb(4, 0, 128, 0, 8), 0.125, None, ALU.mult))
                setev("ksev", ve.n)
                for kc in range(NKC):
                    v.wait_ge(pe.h, ev(("gwtr", kc)))
                    ve.inc(v.tensor_copy(gwT[:, kc * 9:(kc + 1) * 9], pb(5, 0, 128, kc * 9, kc * 9 + 9)))
                    setev(("gwev", kc), ve.n)
            on("vector", _v)
            barrier()

        def emit_ln(li):
            def _s(s):
                for kc in range(NKC):
                    for ft in range(FT):
                        idx = kc * FT + ft
                        q = (idx % 4) * 512
                        s.wait_ge(pe.h, ev(("sqmm", li, idx - 4)))
                        se.inc(s.activation(stage[:, q:q + 512],
                                            xv(kc, ft * 512, (ft + 1) * 512), AF.Square))
                        setev(("sq", li, idx), se.n)
            on("scalar", _s)

            def _t(t):
                for bi in range(8):
                    bank_claim(t, bi)
                for kc in range(NKC):
                    for ft in range(FT):
                        idx = kc * FT + ft
                        q = (idx % 4) * 512
                        t.matmul(pb(ft, 0, 1), ones_fc[:], xv(kc, ft * 512, (ft + 1) * 512),
                                 start=(kc == 0), stop=(kc == NKC - 1))
                        t.wait_ge(se.h, ev(("sq", li, idx)))
                        inst = t.matmul(pb(4 + ft, 0, 1), ones_fc[:], stage[:, q:q + 512],
                                        start=(kc == 0), stop=(kc == NKC - 1))
                        setev(("sqmm", li, idx), pe.inc(inst))
                setev(("stats", li), pe.n)
                for bi in range(8):
                    bank_release(bi, ve, ("lnst", li))
            on("tensor", _t)

            def _v(v):
                v.wait_ge(pe.h, ev(("stats", li)))
                for ft in range(FT):
                    v.tensor_scalar(r_mean[0:1, ft * 512:(ft + 1) * 512], pb(ft, 0, 1),
                                    1.0 / DM, None, ALU.mult)
                    ve.inc(v.tensor_scalar(r_tmp[0:1, ft * 512:(ft + 1) * 512], pb(4 + ft, 0, 1),
                                           1.0 / DM, None, ALU.mult))
                setev(("lnst", li), ve.n)
                v.wait_ge(ve.h, ve.n)
                ve.inc(v.tensor_mul(psA[0:1, 0:2048], r_mean[:], r_mean[:]))
                v.wait_ge(ve.h, ve.n)
                ve.inc(v.tensor_sub(r_tmp[:], r_tmp[:], psA[0:1, 0:2048]))
                setev(("var", li), ve.n)
            on("vector", _v)

            def _s2(s):
                s.wait_ge(ve.h, ev(("var", li)))
                se.inc(s.activation(r_tmp[:], r_tmp[:], AF.Sqrt, bias=epsc[0:1, 0:1]))
                setev(("std", li), se.n)
            on("scalar", _s2)

            def _v2(v):
                v.wait_ge(se.h, ev(("std", li)))
                ve.inc(v.reciprocal(r_tmp[:], r_tmp[:]))
                v.wait_ge(ve.h, ve.n)
                ve.inc(v.tensor_mul(r_mean[:], r_mean[:], r_tmp[:]))
                setev(("ab", li), ve.n)
            on("vector", _v2)

            def _t2(t):
                t.wait_ge(ve.h, ev(("ab", li)))
                for ft in range(FT):
                    bank_claim(t, ft)
                    bank_claim(t, 4 + ft)
                    t.matmul(pb(ft, 0, 128), ones_fr[:], r_tmp[0:1, ft * 512:(ft + 1) * 512],
                             start=True, stop=True)
                    inst = t.matmul(pb(4 + ft, 0, 128), ones_fr[:], r_mean[0:1, ft * 512:(ft + 1) * 512],
                                    start=True, stop=True)
                    setev(("bc", li, ft), pe.inc(inst))
                    bank_release(ft, ve, ("xnev", li))
                    bank_release(4 + ft, ve, ("xnev", li))
            on("tensor", _t2)

            def _v3(v):
                for ft in range(FT):
                    v.wait_ge(pe.h, ev(("bc", li, ft)))
                for kc in range(NKC):
                    for ft in range(FT):
                        ve.inc(v.tensor_mul(stage[:, 0:512], xv(kc, ft * 512, (ft + 1) * 512),
                                            pb(ft, 0, 128)))
                        v.wait_ge(ve.h, ve.n)
                        ve.inc(v.tensor_sub(xnv(kc, ft * 512, (ft + 1) * 512),
                                            stage[:, 0:512], pb(4 + ft, 0, 128)))
                        v.wait_ge(ve.h, ve.n)
                setev(("xnev", li), ve.n)
            on("vector", _v3)
            barrier()

        def emit_gating(gi):
            L3 = lg_sb[:].rearrange("p (tc e) -> p tc e", e=9)
            L8 = L3[:, :, 0:8]
            C3 = ctm[:].rearrange("p (tc e) -> p tc e", e=9)
            i3 = iota16[:].rearrange("p (tc e) -> p tc e", e=8)

            def r3(x):
                return x[:].rearrange("p (tc e) -> p tc e", e=8)

            def bc(x):
                return x[:].rearrange("p (tc one) -> p tc one", one=1).to_broadcast([128, NTC, 8])

            for tc in range(NTC):
                def _t(t, tc=tc):
                    bi = 6 + tc % 2
                    bank_claim(t, bi)
                    for kc in range(NKC):
                        inst = t.matmul(pb(bi, 0, 128, 0, 9),
                                        xn_sb[:, kc * T + tc * 128: kc * T + (tc + 1) * 128],
                                        gwT[:, kc * 9:(kc + 1) * 9],
                                        start=(kc == 0), stop=(kc == NKC - 1))
                    setev(("lg", gi, tc), pe.inc(inst))
                    bank_release(bi, ve, ("lgev", gi, tc))
                on("tensor", _t)

                def _v(v, tc=tc):
                    bi = 6 + tc % 2
                    v.wait_ge(pe.h, ev(("lg", gi, tc)))
                    ve.inc(v.tensor_copy(lg_sb[:, tc * 9:(tc + 1) * 9], pb(bi, 0, 128, 0, 9)))
                    setev(("lgev", gi, tc), ve.n)
                on("vector", _v)

            def _vc(v):
                def vsb(inst):
                    ve.inc(inst)
                    v.wait_ge(ve.h, ve.n)

                def ts16(dst, src_of, scol, op):
                    for tc in range(NTC):
                        inst = v.tensor_scalar(dst[:, tc * 8:(tc + 1) * 8], src_of(tc),
                                               scol[:, tc:tc + 1], None, op)
                    vsb(inst)
                v.wait_ge(ve.h, ve.n)
                vsb(v.tensor_reduce(cm1[:], L8, axis=AX.X, op=ALU.max))
                ts16(m16a, lambda tc: lg_sb[:, tc * 9:tc * 9 + 8], cm1, ALU.is_equal)
                vsb(v.scalar_tensor_tensor(m16b[:], m16a[:], -BIG, iotaB16[:], ALU.mult, ALU.add))
                vsb(v.tensor_reduce(ci1[:], r3(m16b), axis=AX.X, op=ALU.min))
                ts16(m16a, lambda tc: iota16[:, tc * 8:(tc + 1) * 8], ci1, ALU.is_equal)
                vsb(v.scalar_tensor_tensor(r3(m16b), r3(m16a), -BIG, L8, ALU.mult, ALU.add))
                vsb(v.tensor_reduce(cm2[:], r3(m16b), axis=AX.X, op=ALU.max))
                ts16(m16c, lambda tc: m16b[:, tc * 8:(tc + 1) * 8], cm2, ALU.is_equal)
                vsb(v.scalar_tensor_tensor(m16c[:], m16c[:], -BIG, iotaB16[:], ALU.mult, ALU.add))
                vsb(v.tensor_reduce(ci2[:], r3(m16c), axis=AX.X, op=ALU.min))
                vsb(v.tensor_sub(cdc[:], cm1[:], cm2[:]))
                setev(("cmb1", gi), ve.n)
            on("vector", _vc)

            def _sg(s):
                s.wait_ge(ve.h, ev(("cmb1", gi)))
                s.activation(sg1c[:], cdc[:], AF.Sigmoid)
                se.inc(s.activation(sgsh[:], L3[:, :, 8], AF.Sigmoid))
                setev(("sig", gi), se.n)
            on("scalar", _sg)

            def _vc2(v):
                def vsb(inst):
                    ve.inc(inst)
                    v.wait_ge(ve.h, ve.n)

                def ts16(dst, src_of, scol, op):
                    for tc in range(NTC):
                        inst = v.tensor_scalar(dst[:, tc * 8:(tc + 1) * 8], src_of(tc),
                                               scol[:, tc:tc + 1], None, op)
                    vsb(inst)
                v.wait_ge(se.h, ev(("sig", gi)))
                ts16(m16b, lambda tc: iota16[:, tc * 8:(tc + 1) * 8], ci2, ALU.is_equal)
                vsb(v.tensor_scalar(c1a[:], sg1c[:], 0.5, None, ALU.mult))
                vsb(v.tensor_scalar(c2a[:], sg1c[:], -0.5, 0.5, ALU.mult, ALU.add))
                ts16(m16a, lambda tc: m16a[:, tc * 8:(tc + 1) * 8], c1a, ALU.mult)
                ts16(m16b, lambda tc: m16b[:, tc * 8:(tc + 1) * 8], c2a, ALU.mult)
                vsb(v.tensor_tensor(C3[:, :, 0:8], r3(m16a), r3(m16b), ALU.add))
                vsb(v.tensor_scalar(C3[:, :, 8], sgsh[:], 0.5, None, ALU.mult))
                setev(("cmb", gi), ve.n)
            on("vector", _vc2)

            for tc in range(NTC):
                def _t2(t, tc=tc):
                    bi = 4 + tc % 2
                    t.wait_ge(ve.h, ev(("cmb", gi)))
                    bank_claim(t, bi)
                    pe.inc(t.transpose(pb(bi, 0, 9, 0, 128), ctm[:, tc * 9:(tc + 1) * 9], eye_f[:]))
                    setev(("ctr", gi, tc), pe.n)
                    bank_release(bi, ve, ("ctev", gi, tc))
                on("tensor", _t2)

                def _v3(v, tc=tc):
                    bi = 4 + tc % 2
                    v.wait_ge(pe.h, ev(("ctr", gi, tc)))
                    v.wait_ge(dxc[tc % 2].h, ev(("crdma", gi, tc - 2)))
                    ve.inc(v.tensor_copy(c_t9[:, (tc % 2) * 128:(tc % 2 + 1) * 128],
                                         pb(bi, 0, 9, 0, 128)))
                    setev(("ctev", gi, tc), ve.n)
                on("vector", _v3)

                def _sy(sy, tc=tc):
                    sy.wait_ge(ve.h, ev(("ctev", gi, tc)))
                    sm = dxc[tc % 2]
                    sy.dma_start(
                        crows_d[:, tc * 128:(tc + 1) * 128],
                        c_t9[:, (tc % 2) * 128:(tc % 2 + 1) * 128]).then_inc(sm.h, 16)
                    sm.n += 16
                    setev(("crdma", gi, tc), sm.n)
                on("sync", _sy)

            barrier()

        def chunk_src(c, which):
            e, ic = c // ICH, c % ICH
            if which == "g":
                base = wg_h[e] if e < 8 else swg_h
            else:
                base = wu_h[e] if e < 8 else swu_h
            return base[:, ic * 128:(ic + 1) * 128].rearrange("(kc p) m -> p kc m", p=128)

        def wd_src(c):
            e, ic = c // ICH, c % ICH
            base = wd_h[e] if e < 8 else swd_h
            return base[ic * 128:(ic + 1) * 128, :]

        def prev_ft(c, ft, k=2):
            if ft >= k:
                return (c, ft - k)
            return (c - 1, ft + FT - k)

        def emit_ffn(fi, cb_sb, crow_sb, h_sb, wd_sb, gs_st, us_st):
            parts = []
            c0 = 0
            while c0 < NCH:
                parts.append((c0, min(PART, NCH - c0)))
                c0 += PART
            LA = 2
            setev(("ffnpre", fi), pe.n)

            def cbv(e, ft):
                return cb_sb[:, (e % 2) * T + ft * 512:(e % 2) * T + (ft + 1) * 512]

            def _syc(sy):
                sy.wait_ge(dxc[0].h, ev(("crdma", fi, NTC - 2)))
                sy.wait_ge(dxc[1].h, ev(("crdma", fi, NTC - 1)))
                for e in range(NEXP):
                    if e >= 2:
                        sy.wait_ge(pe.h, ev(("cbmm", fi, e - 2, FT - 1)))
                    sm = dxl[e % 2]
                    sy.dma_start(crow_sb[0:1, (e % 2) * T:(e % 2 + 1) * T],
                                 crows_d[e:e + 1, :]).then_inc(sm.h, 16)
                    sm.n += 16
                    setev(("crl", fi, e), sm.n)
            on("sync", _syc)

            def _g(g):
                part_of = {}
                for pi, (pc0, pn) in enumerate(parts):
                    part_of[pc0 + min(1, pn - 1)] = pi
                for c in range(NCH):
                    g.wait_ge(pe.h, ev(("gudone", fi, c - LA)) if c >= LA else ev(("ffnpre", fi)))
                    sm = dwg[c % 2]
                    g.dma_start(wga[:, (c % 2) * DM:(c % 2 + 1) * DM]
                                .rearrange("p (kc m) -> p kc m", kc=8),
                                chunk_src(c, "g")).then_inc(sm.h, 16)
                    g.dma_start(wua[:, (c % 2) * DM:(c % 2 + 1) * DM]
                                .rearrange("p (kc m) -> p kc m", kc=8),
                                chunk_src(c, "u")).then_inc(sm.h, 16)
                    sm.n += 32
                    setev(("gudma", fi, c), sm.n)
                    if c in part_of:
                        pi = part_of[c]
                        pc0, pn = parts[pi]
                        g.wait_ge(pe.h, ev(("ddone", fi, pi - 1)) if pi > 0 else ev(("ffnpre", fi)))
                        for s_ in range(pn):
                            g.dma_start(wd_sb[:, s_ * DM:(s_ + 1) * DM], wd_src(pc0 + s_)).then_inc(dwd.h, 16)
                            dwd.n += 16
                        setev(("wddma", fi, pi), dwd.n)
            on("gpsimd", _g)

            def _t(t):
                for pi, (pc0, pn) in enumerate(parts):
                    for s_ in range(pn):
                        c = pc0 + s_
                        e = c // ICH
                        if c % ICH == 0:
                            t.wait_ge(dxl[e % 2].h, ev(("crl", fi, e)))
                            for ft in range(FT):
                                bi = 4 + ft
                                bank_claim(t, bi)
                                inst = t.matmul(pb(bi), ones_b[:],
                                                crow_sb[0:1, (e % 2) * T + ft * 512:(e % 2) * T + (ft + 1) * 512],
                                                start=True, stop=True)
                                setev(("cbmm", fi, e, ft), pe.inc(inst))
                                bank_release(bi, ve, ("cbev", fi, e, ft))
                        t.wait_ge(dwg[c % 2].h, ev(("gudma", fi, c)))
                        for ft in range(FT):
                            gb = ft % 2
                            ub = 2 + ft % 2
                            bank_claim(t, gb)
                            for kc in range(NKC):
                                inst = t.matmul(pb(gb),
                                                wga[:, (c % 2) * DM + kc * 128:(c % 2) * DM + (kc + 1) * 128],
                                                xnv(kc, ft * 512, (ft + 1) * 512),
                                                start=(kc == 0), stop=(kc == NKC - 1))
                            setev(("gmm", fi, c, ft), pe.inc(inst))
                            bank_release(gb, se, ("silu", fi, c, ft))
                            bank_claim(t, ub)
                            for kc in range(NKC):
                                inst = t.matmul(pb(ub),
                                                wua[:, (c % 2) * DM + kc * 128:(c % 2) * DM + (kc + 1) * 128],
                                                xnv(kc, ft * 512, (ft + 1) * 512),
                                                start=(kc == 0), stop=(kc == NKC - 1))
                            setev(("umm", fi, c, ft), pe.inc(inst))
                            bank_release(ub, ve, ("usc", fi, c, ft))
                        setev(("gudone", fi, c), pe.n)
                    t.wait_ge(dwd.h, ev(("wddma", fi, pi)))
                    for tb in range(FT):
                        t.wait_ge(ve.h, ev(("hevq", fi, pi, tb)))
                        for og in range(4):
                            b0 = 4 + 2 * (og % 2)
                            bank_claim(t, b0)
                            bank_claim(t, b0 + 1)
                            for kc in range(pn):
                                for o2 in range(2):
                                    o = og * 2 + o2
                                    inst = t.matmul(pb(b0 + o2),
                                                    wd_sb[:, kc * DM + o * 128: kc * DM + (o + 1) * 128],
                                                    h_sb[:, kc * T + tb * 512: kc * T + (tb + 1) * 512],
                                                    start=(kc == 0), stop=(kc == pn - 1))
                            setev(("dmm", fi, pi, tb, og), pe.inc(inst))
                            bank_release(b0, ve, ("dev", fi, pi, tb, og))
                            bank_release(b0 + 1, ve, ("dev", fi, pi, tb, og))
                    setev(("ddone", fi, pi), pe.n)
            on("tensor", _t)

            def _s(s):
                for pi, (pc0, pn) in enumerate(parts):
                    for s_ in range(pn):
                        c = pc0 + s_
                        for ft in range(FT):
                            s.wait_ge(pe.h, ev(("gmm", fi, c, ft)))
                            s.wait_ge(ve.h, ev(("hev", fi) + prev_ft(c, ft)))
                            se.inc(s.activation(gs_st[:, (ft % 2) * 512:(ft % 2 + 1) * 512],
                                                pb(ft % 2), AF.Silu))
                            s.wait_ge(se.h, se.n)
                            se.inc(s.copy(sgsh[0:1, 0:2],
                                          gs_st[0:1, (ft % 2) * 512:(ft % 2) * 512 + 2]))
                            setev(("silu", fi, c, ft), se.n)
            on("scalar", _s)

            def _v(v):
                for pi, (pc0, pn) in enumerate(parts):
                    for s_ in range(pn):
                        c = pc0 + s_
                        e = c // ICH
                        if c % ICH == 0:
                            for ft in range(FT):
                                v.wait_ge(pe.h, ev(("cbmm", fi, e, ft)))
                                ve.inc(v.tensor_copy(cbv(e, ft), pb(4 + ft)))
                                setev(("cbev", fi, e, ft), ve.n)
                            v.wait_ge(ve.h, ve.n)
                        for ft in range(FT):
                            v.wait_ge(pe.h, ev(("umm", fi, c, ft)))
                            ve.inc(v.tensor_mul(us_st[:, (ft % 2) * 512:(ft % 2 + 1) * 512],
                                                pb(2 + ft % 2),
                                                cbv(e, ft)))
                            setev(("usc", fi, c, ft), ve.n)
                            v.wait_ge(ve.h, ve.n)
                            v.wait_ge(se.h, ev(("silu", fi, c, ft)))
                            ve.inc(v.tensor_mul(h_sb[:, s_ * T + ft * 512: s_ * T + (ft + 1) * 512],
                                                gs_st[:, (ft % 2) * 512:(ft % 2 + 1) * 512],
                                                us_st[:, (ft % 2) * 512:(ft % 2 + 1) * 512]))
                            setev(("hev", fi, c, ft), ve.n)
                            if s_ == pn - 1:
                                v.wait_ge(ve.h, ve.n)
                                ve.inc(v.tensor_copy(c1a[:, 0:2],
                                                     h_sb[:, s_ * T + ft * 512 + 510:
                                                          s_ * T + ft * 512 + 512]))
                                setev(("hevq", fi, pi, ft), ve.n)
                    for tb in range(FT):
                        for og in range(4):
                            b0 = 4 + 2 * (og % 2)
                            v.wait_ge(pe.h, ev(("dmm", fi, pi, tb, og)))
                            for o2 in range(2):
                                o = og * 2 + o2
                                inst = v.tensor_add(xv(o, tb * 512, (tb + 1) * 512),
                                                    xv(o, tb * 512, (tb + 1) * 512),
                                                    pb(b0 + o2))
                            ve.inc(inst)
                            setev(("dev", fi, pi, tb, og), ve.n)
            on("vector", _v)
            barrier()

        def emit_attention(AT, n_heads=NH, proj_only=False):
            (k_ext2, v_t, rhs_ext2, q_sb, v_fm, zb_sb, att_st, p_ts) = AT
            NQT, NJC, NIC = 16, 16, 4
            setev("attpre_pe", pe.n)
            setev("attpre_ve", ve.n)

            def _g(g):
                g.wait_ge(ve.h, ev("attpre_ve"))
                g.wait_ge(pe.h, ev("attpre_pe"))
                g.memset(v_t[:], 1.0)
                for kv in range(NKV):
                    g.memset(k_ext2[kv][64:128, :], 0.0)
                for kv in range(NKV):
                    g.memset(k_ext2[kv][64:65, :], 1.0)
                g.memset(rhs_ext2[0][64:128, :], 0.0)
                g.memset(rhs_ext2[1][64:128, :], 0.0)
                ge.inc(g.memset(m_bf[:], 0.0))
                setev("attc", ge.n)
                for mq in range(12):
                    g.wait_ge(pe.h, ev(("pmm", mq - 2, FT - 1)) if mq >= 2 else ev("attpre_pe"))
                    src = (wq_h[:, mq * 128:(mq + 1) * 128] if mq < 8
                           else wkv_h[:, (mq - 8) * 128:(mq - 7) * 128])
                    sm = dwg[mq % 2]
                    g.dma_start(wga[:, (mq % 2) * DM:(mq % 2 + 1) * DM]
                                .rearrange("p (kc m) -> p kc m", kc=8),
                                src.rearrange("(kc p) m -> p kc m", p=128)).then_inc(sm.h, 16)
                    sm.n += 16
                    setev(("wdma", mq), sm.n)
            on("gpsimd", _g)

            def _t(t):
                t.wait_ge(ge.h, ev("attc"))
                for mq in range(12):
                    t.wait_ge(dwg[mq % 2].h, ev(("wdma", mq)))
                    for ft in range(FT):
                        bi = ft % 4
                        bank_claim(t, bi)
                        for kc in range(NKC):
                            inst = t.matmul(pb(bi),
                                            wga[:, (mq % 2) * DM + kc * 128:(mq % 2) * DM + (kc + 1) * 128],
                                            xnv(kc, ft * 512, (ft + 1) * 512),
                                            start=(kc == 0), stop=(kc == NKC - 1))
                        setev(("pmm", mq, ft), pe.inc(inst))
                        bank_release(bi, ve, ("pev", mq, ft))
            on("tensor", _t)

            def _v(v):
                for mq in range(12):
                    for ft in range(FT):
                        bi = ft % 4
                        v.wait_ge(pe.h, ev(("pmm", mq, ft)))
                        if mq < 8:
                            inst = v.tensor_scalar(q_sb[:, mq * T + ft * 512: mq * T + (ft + 1) * 512],
                                                   pb(bi), ks_col[:, mq:mq + 1], None, ALU.mult)
                        elif mq < 10:
                            kk = mq - 8
                            v.tensor_copy(k_ext2[2 * kk][0:64, ft * 512:(ft + 1) * 512], pb(bi, 0, 64))
                            inst = v.tensor_copy(k_ext2[2 * kk + 1][0:64, ft * 512:(ft + 1) * 512],
                                                 pb(bi, 64, 128))
                        else:
                            vv = mq - 10
                            inst = v.tensor_copy(v_fm[:, vv * T + ft * 512: vv * T + (ft + 1) * 512],
                                                 pb(bi))
                        ve.inc(inst)
                        setev(("pev", mq, ft), ve.n)
            on("vector", _v)

            def _t2(t):
                for idx in range(32):
                    vv, jc = idx // 16, idx % 16
                    t.wait_ge(ve.h, ev(("pev", 10 + vv, (jc * 128) // 512)))
                    bank_claim(t, 6)
                    pe.inc(t.transpose(pb(6, 0, 128, 0, 64).bitcast(BF16),
                                       v_fm[:, vv * T + jc * 128: vv * T + (jc + 1) * 128],
                                       eye_b[:]))
                    setev(("vt", idx), pe.n)
                    bank_release(6, ve, ("vtev", idx))
            on("tensor", _t2)

            def _v2(v):
                for idx in range(32):
                    vv, jc = idx // 16, idx % 16
                    v.wait_ge(pe.h, ev(("vt", idx)))
                    src = pb(6, 0, 128, 0, 64).bitcast(BF16)
                    for hh in range(2):
                        kvh = 2 * vv + hh
                        inst = v.tensor_copy(v_t[:, kvh * 1040 + jc * 65: kvh * 1040 + jc * 65 + 64],
                                             src[:, hh * 64:(hh + 1) * 64])
                    ve.inc(inst)
                    setev(("vtev", idx), ve.n)
            on("vector", _v2)
            barrier()

            def prev_avp(h, ic, jp):
                if jp >= 2:
                    return ("avmm", h, ic, (jp - 2) * 2 + 1)
                if ic >= 1:
                    return ("avmm", h, ic - 1, (6 + jp) * 2 + 1)
                if h >= 1:
                    return ("avmm", h - 1, NIC - 1, (6 + jp) * 2 + 1)
                return None

            def head_passA_qt(h, qt):
                kv = h // 4

                if qt == 0:
                    def _vq(v):
                        if h >= 2:
                            v.wait_ge(pe.h, ev(("bdone", h - 2)))
                        v.wait_ge(ve.h, ve.n)
                        ve.inc(v.tensor_copy(rhs_ext2[h % 2][0:64, :],
                                             q_sb[(h % 2) * 64:(h % 2) * 64 + 64,
                                                  (h // 2) * T:(h // 2 + 1) * T]))
                        v.wait_ge(ve.h, ve.n)
                        ve.inc(v.memset(rhs_ext2[h % 2][64:65, :], 0.0))
                        setev(("qcp", h), ve.n)
                    on("vector", _vq)

                def _t(t):
                    if qt == 0:
                        t.wait_ge(ve.h, ev(("qcp", h)))
                    for jj in range(FT):
                        bi = jj % 2
                        bank_claim(t, bi)
                        inst = t.matmul(pb(bi),
                                        rhs_ext2[h % 2][0:128, qt * 128:(qt + 1) * 128],
                                        k_ext2[kv][0:128, jj * 512:(jj + 1) * 512],
                                        start=True, stop=True)
                        setev(("amm", h, qt, jj), pe.inc(inst))
                        bank_release(bi, ve, ("amax", h, qt, jj // 2))
                on("tensor", _t)

                def _v(v):
                    if qt == 0 and h >= 2:
                        v.wait_ge(pe.h, ev(("mt", h - 2)))
                    for jp in range(2):
                        v.wait_ge(pe.h, ev(("amm", h, qt, jp * 2 + 1)))
                        ve.inc(v.tensor_reduce(mm4[:, jp:jp + 1], psA[:, 0:1024],
                                               axis=AX.X, op=ALU.max))
                        setev(("amax", h, qt, jp), ve.n)
                    v.wait_ge(ve.h, ve.n)
                    ve.inc(v.tensor_reduce(m_bf[:, (h % 2) * 16 + qt:(h % 2) * 16 + qt + 1],
                                           mm4[:, 0:2], axis=AX.X, op=ALU.max, negate=True))
                    v.wait_ge(ve.h, ve.n)
                    setev(("mq", h, qt), ve.n)
                on("vector", _v)

            def head_mrow(h):
                def _t(t):
                    t.wait_ge(ve.h, ev(("mq", h, 15)))
                    bank_claim(t, 6)
                    pe.inc(t.transpose(pb(6, 0, 16, 0, 64).bitcast(BF16),
                                       m_bf[:, (h % 2) * 16:(h % 2) * 16 + 16], eye_b[:]))
                    setev(("mt", h), pe.n)
                    bank_release(6, ve, ("mtev", h))
                on("tensor", _t)

                def _v(v):
                    v.wait_ge(pe.h, ev(("mt", h)))
                    if h >= 2:
                        v.wait_ge(dxm[h % 2].h, ev(("mrow", h - 2)))
                    ve.inc(v.tensor_copy(m_t[:, (h % 2) * 128:(h % 2) * 128 + 128],
                                         pb(6, 0, 16, 0, 64).bitcast(BF16)))
                    setev(("mtev", h), ve.n)
                on("vector", _v)

                def _sy(sy):
                    sy.wait_ge(ve.h, ev(("mtev", h)))
                    sm = dxm[h % 2]
                    sy.dma_start(rhs_ext2[h % 2][64:65, :].rearrange("one (p j) -> one p j", p=16),
                                 m_t[:, (h % 2) * 128:(h % 2) * 128 + 128]).then_inc(sm.h, 16)
                    sm.n += 16
                    setev(("mrow", h), sm.n)
                on("sync", _sy)

            def head_passB(h):
                kv = h // 4

                def _t(t, ic, q4):
                    if ic == 0 and q4 == 0:
                        t.wait_ge(dxm[h % 2].h, ev(("mrow", h)))
                    av = 4 + ic % 2
                    if q4 == 0:
                        bank_claim(t, av)
                    for jp in range(q4 * 2, q4 * 2 + 2):
                        bank_claim(t, 2)
                        bank_claim(t, 3)
                        for j2 in range(2):
                            jc = jp * 2 + j2
                            inst = t.matmul(pb(2 + j2), k_ext2[kv][0:128, jc * 128:(jc + 1) * 128],
                                            rhs_ext2[h % 2][0:128, ic * 512:(ic + 1) * 512],
                                            start=True, stop=True)
                            setev(("st", h, ic, jc), pe.inc(inst))
                        bank_release(2, se, ("exp", h, ic, jp))
                        bank_release(3, se, ("exp", h, ic, jp))
                        if jp >= 1:
                            t.wait_ge(se.h, ev(("exp", h, ic, jp - 1)))
                            for j2 in range(2):
                                jc = (jp - 1) * 2 + j2
                                inst = t.matmul(pb(av, 0, 65),
                                                v_t[:, kv * 1040 + jc * 65: kv * 1040 + jc * 65 + 65],
                                                p_ts[(jp - 1) % 2][:, j2 * 512:(j2 + 1) * 512],
                                                start=(jc == 0), stop=False)
                                setev(("avmm", h, ic, jc), pe.inc(inst))
                    if q4 == 3:
                        jp = 7
                        t.wait_ge(se.h, ev(("exp", h, ic, jp)))
                        for j2 in range(2):
                            jc = jp * 2 + j2
                            inst = t.matmul(pb(av, 0, 65),
                                            v_t[:, kv * 1040 + jc * 65: kv * 1040 + jc * 65 + 65],
                                            p_ts[jp % 2][:, j2 * 512:(j2 + 1) * 512],
                                            start=False, stop=(j2 == 1))
                            setev(("avmm", h, ic, jc), pe.inc(inst))
                        setev(("av", h, ic), pe.n)
                        bank_release(av, ve, ("omul", h, ic))

                def _s(s, ic):
                    for jp in range(8):
                        s.wait_ge(pe.h, ev(("st", h, ic, jp * 2 + 1)))
                        pv = prev_avp(h, ic, jp)
                        if pv is not None:
                            s.wait_ge(pe.h, ev(pv))
                        se.inc(s.activation(p_ts[jp % 2][:], psA[:, 1024:2048], AF.Exp))
                        setev(("exp", h, ic, jp), se.n)
                    s.wait_ge(pe.h, ev(("av", h, ic)))
                    se.inc(s.activation(zr[:], pb(4 + ic % 2, 64, 65), AF.Ln))
                    s.wait_ge(se.h, se.n)
                    if ic >= 1:
                        s.wait_ge(pe.h, ev(("zbmm", h, ic - 1)))
                    elif h >= 1:
                        s.wait_ge(pe.h, ev(("zbmm", h - 1, NIC - 1)))
                    se.inc(s.activation(zrb[:], zr[:], AF.Exp, bias=0.0, scale=-1.0))
                    s.wait_ge(se.h, se.n)
                    setev(("zrb", h, ic), se.n)

                def _v(v, ic):
                    pass

                def _t2(t, ic):
                    t.wait_ge(se.h, ev(("zrb", h, ic)))
                    bank_claim(t, 7)
                    pe.inc(t.matmul(pb(7, 0, 64), ones_b[0:1, 0:64], zrb[:],
                                    start=True, stop=True))
                    setev(("zbmm", h, ic), pe.n)
                    bank_release(7, ve, ("zbcp", h, ic))

                def _v2(v, ic):
                    av = 4 + ic % 2
                    v.wait_ge(pe.h, ev(("zbmm", h, ic)))
                    ve.inc(v.tensor_copy(zb_sb[:], pb(7, 0, 64)))
                    setev(("zbcp", h, ic), ve.n)
                    v.wait_ge(ve.h, ve.n)
                    ve.inc(v.tensor_mul(att_st[0:64, :], pb(av, 0, 64), zb_sb[:]))
                    setev(("omul", h, ic), ve.n)
                    v.wait_ge(ve.h, ve.n)
                    half = att_st[0:64, :]
                    if h % 2 == 1:
                        ve.inc(v.tensor_copy(att_st[64:128, :], att_st[0:64, :]))
                        v.wait_ge(ve.h, ve.n)
                        half = att_st[64:128, :]
                    ve.inc(v.tensor_add(
                        x_sb[(h % 2) * 64:(h % 2) * 64 + 64,
                             (h // 2) * T + ic * 512:(h // 2) * T + (ic + 1) * 512],
                        x_sb[(h % 2) * 64:(h % 2) * 64 + 64,
                             (h // 2) * T + ic * 512:(h // 2) * T + (ic + 1) * 512],
                        half))
                    setev(("oev", h, ic), ve.n)

                return _t, _t2, _s, _v, _v2

            if proj_only:
                return

            def emit_B_unit(h, u, fns):
                _t, _t2, _s, _v, _v2 = fns
                ic, q4 = u // 4, u % 4
                if u == 0:
                    def _sall(s_):
                        for ic2 in range(NIC):
                            _s(s_, ic2)
                    on("scalar", _sall)
                on("tensor", lambda t: _t(t, ic, q4))
                if q4 == 3:
                    if ic >= 1:
                        on("tensor", lambda t: _t2(t, ic - 1))
                        on("vector", lambda v: _v2(v, ic - 1))
                    if ic == NIC - 1:
                        on("tensor", lambda t: _t2(t, NIC - 1))
                        on("vector", lambda v: _v2(v, NIC - 1))
                        setev(("bdone", h), snap[("av", h, NIC - 1)])

            bfns = {}
            for h in range(n_heads + 1):
                for u in range(16):
                    if h < n_heads:
                        head_passA_qt(h, u)
                    if h >= 1:
                        emit_B_unit(h - 1, u, bfns[h - 1])
                if h < n_heads:
                    head_mrow(h)
                    bfns[h] = head_passB(h)
            barrier()

        def emit_epilogue():
            def _t(t):
                for tc in range(NTC):
                    for kc in range(NKC):
                        bi = kc % 4
                        bank_claim(t, bi)
                        pe.inc(t.transpose(pb(bi, 0, 128, 0, 128),
                                           xv(kc, tc * 128, (tc + 1) * 128), eye_f[:]))
                        setev(("ytr", tc, kc), pe.n)
                        bank_release(bi, ve, ("yevk", tc, kc))
            on("tensor", _t)

            def _v(v):
                for tc in range(NTC):
                    v.wait_ge(dxx[tc % 2].h, ev(("yst", tc - 2)))
                    for kc in range(NKC):
                        bi = kc % 4
                        v.wait_ge(pe.h, ev(("ytr", tc, kc)))
                        ve.inc(v.tensor_copy(stage[:, (tc % 2) * DM + kc * 128:(tc % 2) * DM + (kc + 1) * 128],
                                             pb(bi, 0, 128, 0, 128)))
                        setev(("yevk", tc, kc), ve.n)
                    setev(("yev", tc), ve.n)
            on("vector", _v)

            def _sy(sy):
                for tc in range(NTC):
                    sy.wait_ge(ve.h, ev(("yev", tc)))
                    sm = dxx[tc % 2]
                    sy.dma_start(y_h[tc * 128:(tc + 1) * 128, :],
                                 stage[:, (tc % 2) * DM:(tc % 2 + 1) * DM]).then_inc(sm.h, 16)
                    sm.n += 16
                    setev(("yst", tc), sm.n)
                sy.wait_ge(dxx[0].h, dxx[0].n)
                sy.wait_ge(dxx[1].h, dxx[1].n)
                sy.wait_ge(dx.h, dx.n)
            on("sync", _sy)

        def emit_all():
            bankuse.clear()
            with ExitStack() as pst:
                pro_box["ks_st"] = scoped(pst, "ks_st", [8, 128], F32)
                pro_box["gsw_sb"] = scoped(pst, "gsw_sb", [9, DM], F32)
                emit_prologue()
            emit_ln(0)
            if upto == "ln1":
                return
            with ExitStack() as fst:
                emit_gating(0)
                if upto == "gate1":
                    return
                cb_sb = scoped(fst, "cb1", [128, 2 * T], BF16)
                crow_sb = scoped(fst, "crow1", [1, 2 * T], BF16)
                h_sb = scoped(fst, "h1", [128, PART * T], BF16)
                wd_sb = scoped(fst, "wd1", [128, PART * DM], BF16)
                gs_st = scoped(fst, "gs1", [128, 2 * 512], BF16)
                us_st = scoped(fst, "us1", [128, 2 * 512], BF16)
                emit_ffn(0, cb_sb, crow_sb, h_sb, wd_sb, gs_st, us_st)
            if upto == "ffn1":
                return
            emit_ln(1)
            if upto == "ln2":
                return
            with ExitStack() as ast_:
                def asb(n_, s_, d_):
                    return scoped(ast_, n_, s_, d_)
                k_ext2 = [asb(f"kext{i}", [128, T], BF16) for i in range(NKV)]
                v_t = asb("v_t", [128, NKV * 16 * 65], BF16)
                q_sb = asb("q_sb", [128, NKC * T], BF16)
                v_fm = asb("v_fm", [128, 2 * T], BF16)
                rhs_ext2 = [asb(f"rhse{i}", [128, T], BF16) for i in range(2)]
                att_st = stage[:, 0:512]
                zb_sb = stage[0:64, 512:1024]
                p_t = stage[:, 1024:1536].bitcast(BF16)
                p_t2 = stage[:, 1536:2048].bitcast(BF16)
                if upto.startswith("attn") and upto != "attn":
                    nhh = 0 if upto == "attn0" else int(upto[4:])
                    emit_attention((k_ext2, v_t, rhs_ext2, q_sb, v_fm, zb_sb, att_st, (p_t, p_t2)),
                                   n_heads=max(nhh, 1), proj_only=(nhh == 0))
                    return
                emit_attention((k_ext2, v_t, rhs_ext2, q_sb, v_fm, zb_sb, att_st, (p_t, p_t2)))
            if upto == "attn":
                return
            emit_ln(2)
            if upto == "ln3":
                return
            with ExitStack() as fst:
                emit_gating(1)
                cb_sb = scoped(fst, "cb2", [128, 2 * T], BF16)
                crow_sb = scoped(fst, "crow2", [1, 2 * T], BF16)
                h_sb = scoped(fst, "h2", [128, PART * T], BF16)
                wd_sb = scoped(fst, "wd2", [128, PART * DM], BF16)
                gs_st = scoped(fst, "gs2", [128, 2 * 512], BF16)
                us_st = scoped(fst, "us2", [128, 2 * 512], BF16)
                emit_ffn(1, cb_sb, crow_sb, h_sb, wd_sb, gs_st, us_st)

        def emit_taps():
            if not taps:
                return

            def _sy(sy):
                sy.wait_ge(ve.h, ve.n)
                sy.wait_ge(se.h, se.n)
                sy.wait_ge(pe.h, pe.n)
                sy.wait_ge(ge.h, ge.n)
                for tn, shape, dt in taps:
                    src = crows_d if tn == "crows_d" else TEN[tn][0:shape[0], 0:shape[1]]
                    sy.dma_start(tap_out[tn], src).then_inc(dx.h, 16)
                    dx.n += 16
                sy.wait_ge(dx.h, dx.n)
            on("sync", _sy)

        ctx["real"] = False
        ctx["pass"] = 1
        emit_all()
        if upto == "all":
            emit_epilogue()
        emit_taps()
        pass1 = [s_.n for s_ in ALLSC]
        for s_ in ALLSC:
            s_.n = 0
        ctx["real"] = True
        ctx["pass"] = 2
        emit_all()
        if upto == "all":
            emit_epilogue()
        emit_taps()
        pass2 = [s_.n for s_ in ALLSC]
        assert pass1 == pass2, (pass1, pass2)
    return nc


def _in_maps(L0_attn_x, L0_attn_Wq, L0_attn_Wkv, L0_attn_k_scale,
             L0_ffn_gate_w, L0_ffn_w_gate, L0_ffn_w_up, L0_ffn_w_down,
             L0_ffn_sw_gate, L0_ffn_sw_up, L0_ffn_sw_down, L0_ffn_shared_gate_w):
    shared = {
        "wq": np.ascontiguousarray(L0_attn_Wq, np.float32),
        "wkv": np.ascontiguousarray(L0_attn_Wkv, np.float32),
        "ks": np.ascontiguousarray(np.asarray(L0_attn_k_scale, np.float32).reshape(8, 128)),
        "gw": np.ascontiguousarray(L0_ffn_gate_w, np.float32),
        "sgw": np.ascontiguousarray(np.asarray(L0_ffn_shared_gate_w, np.float32).reshape(1, DM)),
        "wg": np.ascontiguousarray(L0_ffn_w_gate, np.float32),
        "wu": np.ascontiguousarray(L0_ffn_w_up, np.float32),
        "wd": np.ascontiguousarray(L0_ffn_w_down, np.float32),
        "swg": np.ascontiguousarray(L0_ffn_sw_gate, np.float32),
        "swu": np.ascontiguousarray(L0_ffn_sw_up, np.float32),
        "swd": np.ascontiguousarray(L0_ffn_sw_down, np.float32),
    }
    x = np.asarray(L0_attn_x, np.float32)
    return [dict(shared, x=np.ascontiguousarray(x[c])) for c in range(8)]


def kernel(**inputs):
    nc = build_program()
    in_maps = _in_maps(**inputs)
    res = run_bass_kernel_spmd(nc, in_maps, core_ids=list(range(8)))
    return np.stack([res.results[c]["y"] for c in range(8)], axis=0)


# version 10
# speedup vs baseline: 1.1933x; 1.1933x over previous
import numpy as np
import concourse.bass as bass
import concourse.mybir as mybir
from concourse.bass_utils import run_bass_kernel_spmd
from contextlib import ExitStack

F32 = mybir.dt.float32
BF16 = mybir.dt.bfloat16
AF = mybir.ActivationFunctionType
ALU = mybir.AluOpType
AX = mybir.AxisListType

T = 2048
NKC = 8
DM = 1024
NCH = 99
ICH = 11
NEXP = 9
NH = 16
NKV = 4
BIG = 30000.0
PART = 6
FT = 4
NTC = 16


class FakeInst:
    def then_inc(self, sem, k):
        return self


_FI = FakeInst()


class FakeEng:
    def wait_ge(self, *a, **k):
        pass

    def __getattr__(self, name):
        def op(*a, **k):
            return _FI
        return op


_FE = FakeEng()


class SC:
    __slots__ = ("h", "n")

    def __init__(self, h):
        self.h = h
        self.n = 0

    def inc(self, inst, k=1):
        inst.then_inc(self.h, k)
        self.n += k
        return self.n


def build_program(upto="all", taps=()):
    nc = bass.Bass("TRN2", target_bir_lowering=False)
    x_h = nc.dram_tensor("x", [T, DM], F32, kind="ExternalInput").ap()
    wq_h = nc.dram_tensor("wq", [DM, DM], F32, kind="ExternalInput").ap()
    wkv_h = nc.dram_tensor("wkv", [DM, 512], F32, kind="ExternalInput").ap()
    ks_h = nc.dram_tensor("ks", [8, 128], F32, kind="ExternalInput").ap()
    gw_h = nc.dram_tensor("gw", [8, DM], F32, kind="ExternalInput").ap()
    sgw_h = nc.dram_tensor("sgw", [1, DM], F32, kind="ExternalInput").ap()
    wg_h = nc.dram_tensor("wg", [8, DM, 1408], F32, kind="ExternalInput").ap()
    wu_h = nc.dram_tensor("wu", [8, DM, 1408], F32, kind="ExternalInput").ap()
    wd_h = nc.dram_tensor("wd", [8, 1408, DM], F32, kind="ExternalInput").ap()
    swg_h = nc.dram_tensor("swg", [DM, 1408], F32, kind="ExternalInput").ap()
    swu_h = nc.dram_tensor("swu", [DM, 1408], F32, kind="ExternalInput").ap()
    swd_h = nc.dram_tensor("swd", [1408, DM], F32, kind="ExternalInput").ap()
    y_h = nc.dram_tensor("y", [T, DM], F32, kind="ExternalOutput").ap()
    crows_d = nc.dram_tensor("crows_d", [NEXP, T], BF16, kind="Internal").ap()
    tap_out = {}
    for tn, shape, dt in taps:
        tap_out[tn] = nc.dram_tensor("tap_" + tn, shape, dt, kind="ExternalOutput").ap()

    snap = {}
    TEN = {}

    with ExitStack() as st:
        def sb(n_, shape, dt):
            h = st.enter_context(nc.sbuf_tensor(n_, shape, dt))
            TEN[n_] = h
            return h

        x_sb = sb("x_sb", [128, NKC * T], F32)
        xn_sb = sb("xn_sb", [128, NKC * T], BF16)
        stage = sb("stage", [128, 2 * DM], F32)
        wga = sb("wga", [128, 2 * DM], BF16)
        wua = sb("wua", [128, 2 * DM], BF16)
        eye_b = sb("eye_b", [128, 128], BF16)
        eye_f = sb("eye_f", [128, 128], F32)
        ones_b = sb("ones_b", [1, 128], BF16)
        ones_fc = sb("ones_fc", [128, 1], F32)
        ones_fr = sb("ones_fr", [1, 128], F32)
        ks_col = sb("ks_col", [128, 8], F32)
        pro_box = {}
        gwT = sb("gwT", [128, NKC * 9], BF16)
        epsc = sb("epsc", [1, 1], F32)
        lg_sb = sb("lg_sb", [128, NTC * 9], F32)
        ctm = sb("ctm", [128, NTC * 9], F32)
        cm1 = sb("cm1", [128, NTC], F32)
        cm2 = sb("cm2", [128, NTC], F32)
        ci1 = sb("ci1", [128, NTC], F32)
        ci2 = sb("ci2", [128, NTC], F32)
        cdc = sb("cdc", [128, NTC], F32)
        sg1c = sb("sg1c", [128, NTC], F32)
        sgsh = sb("sgsh", [128, NTC], F32)
        c1a = sb("c1a", [128, NTC], F32)
        c2a = sb("c2a", [128, NTC], F32)
        m16a = stage[:, 0:128]
        m16b = stage[:, 128:256]
        m16c = stage[:, 256:384]
        iota16 = sb("iota16", [128, 128], F32)
        iotaB16 = sb("iotaB16", [128, 128], F32)
        c_t9 = sb("c_t9", [9, 2 * 128], BF16)
        rowp = sb("rowp", [1, 2 * T], F32)
        r_mean = rowp[0:1, 0:T]
        r_tmp = rowp[0:1, T:2 * T]
        zr = rowp[0:1, 0:512]
        zrb = sb("zrb", [1, 512], BF16)
        m_bf = sb("m_bf", [128, 2 * 16], BF16)
        m_t = sb("m_t", [16, 2 * 128], BF16)
        mm4 = sb("mm4", [128, 4], F32)

        psA = st.enter_context(nc.psum_tensor("psA", [128, 2048], F32))
        psB = st.enter_context(nc.psum_tensor("psB", [128, 2048], F32))

        def pb(i, p0=0, p1=128, c0=0, c1=512):
            t_ = psA if i < 4 else psB
            base = (i % 4) * 512
            return t_[p0:p1, base + c0:base + c1]

        dw = SC(st.enter_context(nc.semaphore("dw")))
        dx = SC(st.enter_context(nc.semaphore("dx")))
        dwg = [SC(st.enter_context(nc.semaphore(f"dwg{i}"))) for i in range(2)]
        dwd = SC(st.enter_context(nc.semaphore("dwd")))
        dxx = [SC(st.enter_context(nc.semaphore(f"dxx{i}"))) for i in range(2)]
        dxc = [SC(st.enter_context(nc.semaphore(f"dxc{i}"))) for i in range(2)]
        dxl = [SC(st.enter_context(nc.semaphore(f"dxl{i}"))) for i in range(2)]
        dxm = [SC(st.enter_context(nc.semaphore(f"dxm{i}"))) for i in range(2)]
        pe = SC(st.enter_context(nc.semaphore("pe")))
        ve = SC(st.enter_context(nc.semaphore("ve")))
        se = SC(st.enter_context(nc.semaphore("se")))
        ge = SC(st.enter_context(nc.semaphore("ge")))
        ALLSC = [dw, dx, pe, ve, se, ge, dwd] + dwg + dxx + dxc + dxl + dxm

        blk = st.enter_context(nc.Block())

        ctx = {"real": False, "pass": 0}
        bankuse = {}

        def scoped(stk, name, shape, dt):
            h = stk.enter_context(nc.sbuf_tensor(f"{name}_p{ctx['pass']}", shape, dt))
            TEN[name] = h
            return h

        def on(ename, f):
            if ctx["real"]:
                getattr(blk, ename)(f)
            else:
                f(_FE)

        def ev(name):
            return snap.get(name, 0)

        def setev(name, val):
            snap[name] = val

        def bank_claim(eng, bi):
            if bi in bankuse:
                sem, name = bankuse[bi]
                eng.wait_ge(sem.h, ev(name))

        def bank_release(bi, sem, name):
            bankuse[bi] = (sem, name)

        def barrier():
            pev, vev, sev = pe.n, ve.n, se.n

            def _t(t):
                t.wait_ge(ve.h, vev)
                t.wait_ge(se.h, sev)
            on("tensor", _t)

            def _v(v):
                v.wait_ge(pe.h, pev)
                v.wait_ge(se.h, sev)
            on("vector", _v)

            def _s(s):
                s.wait_ge(pe.h, pev)
                s.wait_ge(ve.h, vev)
            on("scalar", _s)

        def xv(kc, f0, f1):
            return x_sb[:, kc * T + f0: kc * T + f1]

        def xnv(kc, f0, f1):
            return xn_sb[:, kc * T + f0: kc * T + f1]

        def emit_prologue():
            ks_st = pro_box["ks_st"]
            gsw_sb = pro_box["gsw_sb"]
            def _g(g):
                ge.inc(g.memset(eye_b[:], 0.0))
                ge.inc(g.memset(eye_f[:], 0.0))
                g.wait_ge(ge.h, ge.n)
                g.affine_select(out=eye_b[:], in_=eye_b[:], compare_op=ALU.not_equal,
                                fill=1.0, base=0, pattern=[[-1, 128]], channel_multiplier=1)
                g.affine_select(out=eye_f[:], in_=eye_f[:], compare_op=ALU.not_equal,
                                fill=1.0, base=0, pattern=[[-1, 128]], channel_multiplier=1)
                g.memset(ones_b[:], 1.0)
                g.memset(ones_fc[:], 1.0)
                g.memset(ones_fr[:], 1.0)
                g.memset(epsc[:], 1e-5)
                g.iota(iota16[:].rearrange("p (tc e) -> p tc e", e=8),
                       pattern=[[0, 16], [1, 8]], base=0, channel_multiplier=0,
                       allow_small_or_imprecise_dtypes=True)
                ge.inc(g.iota(iotaB16[:].rearrange("p (tc e) -> p tc e", e=8),
                              pattern=[[0, 16], [1, 8]], base=int(BIG), channel_multiplier=0,
                              allow_small_or_imprecise_dtypes=True))
                setev("consts", ge.n)
            on("gpsimd", _g)

            def _sy(sy):
                sy.dma_start(ks_st[:], ks_h).then_inc(dx.h, 16)
                sy.dma_start(gsw_sb[0:8, :], gw_h).then_inc(dx.h, 16)
                sy.dma_start(gsw_sb[8:9, :], sgw_h).then_inc(dx.h, 16)
                dx.n += 48
                setev("hdr_dx", dx.n)
                for tc in range(NTC):
                    sy.wait_ge(pe.h, ev(("xtr", tc - 2, NKC - 1)))
                    sm = dxx[tc % 2]
                    sy.dma_start(stage[:, (tc % 2) * DM:(tc % 2 + 1) * DM],
                                 x_h[tc * 128:(tc + 1) * 128, :]).then_inc(sm.h, 16)
                    sm.n += 16
                    setev(("xdma", tc), sm.n)
            on("sync", _sy)

            def _t(t):
                t.wait_ge(ge.h, ev("consts"))
                for tc in range(NTC):
                    t.wait_ge(dxx[tc % 2].h, ev(("xdma", tc)))
                    for kc in range(NKC):
                        bi = kc % 4
                        bank_claim(t, bi)
                        pe.inc(t.transpose(pb(bi, 0, 128, 0, 128),
                                           stage[:, (tc % 2) * DM + kc * 128:(tc % 2) * DM + (kc + 1) * 128],
                                           eye_f[:]))
                        setev(("xtr", tc, kc), pe.n)
                        bank_release(bi, ve, ("xev", tc, kc))
                t.wait_ge(dx.h, ev("hdr_dx"))
                bank_claim(t, 4)
                pe.inc(t.transpose(pb(4, 0, 128, 0, 8), ks_st[:], eye_f[0:8, 0:8]))
                setev("kstr", pe.n)
                bank_release(4, ve, "ksev")
                bank_claim(t, 5)
                for kc in range(NKC):
                    t.wait_ge(ve.h, ev(("gwev", kc - 1)))
                    pe.inc(t.transpose(pb(5, 0, 128, kc * 9, kc * 9 + 9),
                                       gsw_sb[:, kc * 128:(kc + 1) * 128], eye_f[0:9, 0:9]))
                    setev(("gwtr", kc), pe.n)
                bank_release(5, ve, ("gwev", NKC - 1))
            on("tensor", _t)

            def _v(v):
                for tc in range(NTC):
                    for kc in range(NKC):
                        bi = kc % 4
                        v.wait_ge(pe.h, ev(("xtr", tc, kc)))
                        ve.inc(v.tensor_copy(xv(kc, tc * 128, (tc + 1) * 128), pb(bi, 0, 128, 0, 128)))
                        setev(("xev", tc, kc), ve.n)
                v.wait_ge(pe.h, ev("kstr"))
                ve.inc(v.tensor_scalar(ks_col[:], pb(4, 0, 128, 0, 8), 0.125, None, ALU.mult))
                setev("ksev", ve.n)
                for kc in range(NKC):
                    v.wait_ge(pe.h, ev(("gwtr", kc)))
                    ve.inc(v.tensor_copy(gwT[:, kc * 9:(kc + 1) * 9], pb(5, 0, 128, kc * 9, kc * 9 + 9)))
                    setev(("gwev", kc), ve.n)
            on("vector", _v)
            barrier()

        def emit_ln(li):
            def _s(s):
                for kc in range(NKC):
                    for ft in range(FT):
                        idx = kc * FT + ft
                        q = (idx % 4) * 512
                        s.wait_ge(pe.h, ev(("sqmm", li, idx - 4)))
                        se.inc(s.activation(stage[:, q:q + 512],
                                            xv(kc, ft * 512, (ft + 1) * 512), AF.Square))
                        setev(("sq", li, idx), se.n)
            on("scalar", _s)

            def _t(t):
                for bi in range(8):
                    bank_claim(t, bi)
                for kc in range(NKC):
                    for ft in range(FT):
                        idx = kc * FT + ft
                        q = (idx % 4) * 512
                        t.matmul(pb(ft, 0, 1), ones_fc[:], xv(kc, ft * 512, (ft + 1) * 512),
                                 start=(kc == 0), stop=(kc == NKC - 1))
                        t.wait_ge(se.h, ev(("sq", li, idx)))
                        inst = t.matmul(pb(4 + ft, 0, 1), ones_fc[:], stage[:, q:q + 512],
                                        start=(kc == 0), stop=(kc == NKC - 1))
                        setev(("sqmm", li, idx), pe.inc(inst))
                setev(("stats", li), pe.n)
                for bi in range(8):
                    bank_release(bi, ve, ("lnst", li))
            on("tensor", _t)

            def _v(v):
                v.wait_ge(pe.h, ev(("stats", li)))
                for ft in range(FT):
                    v.tensor_scalar(r_mean[0:1, ft * 512:(ft + 1) * 512], pb(ft, 0, 1),
                                    1.0 / DM, None, ALU.mult)
                    ve.inc(v.tensor_scalar(r_tmp[0:1, ft * 512:(ft + 1) * 512], pb(4 + ft, 0, 1),
                                           1.0 / DM, None, ALU.mult))
                setev(("lnst", li), ve.n)
                v.wait_ge(ve.h, ve.n)
                ve.inc(v.tensor_mul(psA[0:1, 0:2048], r_mean[:], r_mean[:]))
                v.wait_ge(ve.h, ve.n)
                ve.inc(v.tensor_sub(r_tmp[:], r_tmp[:], psA[0:1, 0:2048]))
                setev(("var", li), ve.n)
            on("vector", _v)

            def _s2(s):
                s.wait_ge(ve.h, ev(("var", li)))
                se.inc(s.activation(r_tmp[:], r_tmp[:], AF.Sqrt, bias=epsc[0:1, 0:1]))
                setev(("std", li), se.n)
            on("scalar", _s2)

            def _v2(v):
                v.wait_ge(se.h, ev(("std", li)))
                ve.inc(v.reciprocal(r_tmp[:], r_tmp[:]))
                v.wait_ge(ve.h, ve.n)
                ve.inc(v.tensor_mul(r_mean[:], r_mean[:], r_tmp[:]))
                setev(("ab", li), ve.n)
            on("vector", _v2)

            def _t2(t):
                t.wait_ge(ve.h, ev(("ab", li)))
                for ft in range(FT):
                    bank_claim(t, ft)
                    bank_claim(t, 4 + ft)
                    t.matmul(pb(ft, 0, 128), ones_fr[:], r_tmp[0:1, ft * 512:(ft + 1) * 512],
                             start=True, stop=True)
                    inst = t.matmul(pb(4 + ft, 0, 128), ones_fr[:], r_mean[0:1, ft * 512:(ft + 1) * 512],
                                    start=True, stop=True)
                    setev(("bc", li, ft), pe.inc(inst))
                    bank_release(ft, ve, ("xnev", li))
                    bank_release(4 + ft, ve, ("xnev", li))
            on("tensor", _t2)

            def _v3(v):
                for ft in range(FT):
                    v.wait_ge(pe.h, ev(("bc", li, ft)))
                for kc in range(NKC):
                    for ft in range(FT):
                        ve.inc(v.tensor_mul(stage[:, 0:512], xv(kc, ft * 512, (ft + 1) * 512),
                                            pb(ft, 0, 128)))
                        v.wait_ge(ve.h, ve.n)
                        ve.inc(v.tensor_sub(xnv(kc, ft * 512, (ft + 1) * 512),
                                            stage[:, 0:512], pb(4 + ft, 0, 128)))
                        v.wait_ge(ve.h, ve.n)
                setev(("xnev", li), ve.n)
            on("vector", _v3)
            barrier()

        def emit_gating(gi):
            L3 = lg_sb[:].rearrange("p (tc e) -> p tc e", e=9)
            L8 = L3[:, :, 0:8]
            C3 = ctm[:].rearrange("p (tc e) -> p tc e", e=9)
            i3 = iota16[:].rearrange("p (tc e) -> p tc e", e=8)

            def r3(x):
                return x[:].rearrange("p (tc e) -> p tc e", e=8)

            def bc(x):
                return x[:].rearrange("p (tc one) -> p tc one", one=1).to_broadcast([128, NTC, 8])

            for tc in range(NTC):
                def _t(t, tc=tc):
                    bi = 6 + tc % 2
                    bank_claim(t, bi)
                    for kc in range(NKC):
                        inst = t.matmul(pb(bi, 0, 128, 0, 9),
                                        xn_sb[:, kc * T + tc * 128: kc * T + (tc + 1) * 128],
                                        gwT[:, kc * 9:(kc + 1) * 9],
                                        start=(kc == 0), stop=(kc == NKC - 1))
                    setev(("lg", gi, tc), pe.inc(inst))
                    bank_release(bi, ve, ("lgev", gi, tc))
                on("tensor", _t)

                def _v(v, tc=tc):
                    bi = 6 + tc % 2
                    v.wait_ge(pe.h, ev(("lg", gi, tc)))
                    ve.inc(v.tensor_copy(lg_sb[:, tc * 9:(tc + 1) * 9], pb(bi, 0, 128, 0, 9)))
                    setev(("lgev", gi, tc), ve.n)
                on("vector", _v)

            def _vc(v):
                def vsb(inst):
                    ve.inc(inst)
                    v.wait_ge(ve.h, ve.n)

                def ts16(dst, src_of, scol, op):
                    for tc in range(NTC):
                        inst = v.tensor_scalar(dst[:, tc * 8:(tc + 1) * 8], src_of(tc),
                                               scol[:, tc:tc + 1], None, op)
                    vsb(inst)
                v.wait_ge(ve.h, ve.n)
                vsb(v.tensor_reduce(cm1[:], L8, axis=AX.X, op=ALU.max))
                ts16(m16a, lambda tc: lg_sb[:, tc * 9:tc * 9 + 8], cm1, ALU.is_equal)
                vsb(v.scalar_tensor_tensor(m16b[:], m16a[:], -BIG, iotaB16[:], ALU.mult, ALU.add))
                vsb(v.tensor_reduce(ci1[:], r3(m16b), axis=AX.X, op=ALU.min))
                ts16(m16a, lambda tc: iota16[:, tc * 8:(tc + 1) * 8], ci1, ALU.is_equal)
                vsb(v.scalar_tensor_tensor(r3(m16b), r3(m16a), -BIG, L8, ALU.mult, ALU.add))
                vsb(v.tensor_reduce(cm2[:], r3(m16b), axis=AX.X, op=ALU.max))
                ts16(m16c, lambda tc: m16b[:, tc * 8:(tc + 1) * 8], cm2, ALU.is_equal)
                vsb(v.scalar_tensor_tensor(m16c[:], m16c[:], -BIG, iotaB16[:], ALU.mult, ALU.add))
                vsb(v.tensor_reduce(ci2[:], r3(m16c), axis=AX.X, op=ALU.min))
                vsb(v.tensor_sub(cdc[:], cm1[:], cm2[:]))
                setev(("cmb1", gi), ve.n)
            on("vector", _vc)

            def _sg(s):
                s.wait_ge(ve.h, ev(("cmb1", gi)))
                s.activation(sg1c[:], cdc[:], AF.Sigmoid)
                se.inc(s.activation(sgsh[:], L3[:, :, 8], AF.Sigmoid))
                setev(("sig", gi), se.n)
            on("scalar", _sg)

            def _vc2(v):
                def vsb(inst):
                    ve.inc(inst)
                    v.wait_ge(ve.h, ve.n)

                def ts16(dst, src_of, scol, op):
                    for tc in range(NTC):
                        inst = v.tensor_scalar(dst[:, tc * 8:(tc + 1) * 8], src_of(tc),
                                               scol[:, tc:tc + 1], None, op)
                    vsb(inst)
                v.wait_ge(se.h, ev(("sig", gi)))
                ts16(m16b, lambda tc: iota16[:, tc * 8:(tc + 1) * 8], ci2, ALU.is_equal)
                vsb(v.tensor_scalar(c1a[:], sg1c[:], 0.5, None, ALU.mult))
                vsb(v.tensor_scalar(c2a[:], sg1c[:], -0.5, 0.5, ALU.mult, ALU.add))
                ts16(m16a, lambda tc: m16a[:, tc * 8:(tc + 1) * 8], c1a, ALU.mult)
                ts16(m16b, lambda tc: m16b[:, tc * 8:(tc + 1) * 8], c2a, ALU.mult)
                vsb(v.tensor_tensor(C3[:, :, 0:8], r3(m16a), r3(m16b), ALU.add))
                vsb(v.tensor_scalar(C3[:, :, 8], sgsh[:], 0.5, None, ALU.mult))
                setev(("cmb", gi), ve.n)
            on("vector", _vc2)

            for tc in range(NTC):
                def _t2(t, tc=tc):
                    bi = 4 + tc % 2
                    t.wait_ge(ve.h, ev(("cmb", gi)))
                    bank_claim(t, bi)
                    pe.inc(t.transpose(pb(bi, 0, 9, 0, 128), ctm[:, tc * 9:(tc + 1) * 9], eye_f[:]))
                    setev(("ctr", gi, tc), pe.n)
                    bank_release(bi, ve, ("ctev", gi, tc))
                on("tensor", _t2)

                def _v3(v, tc=tc):
                    bi = 4 + tc % 2
                    v.wait_ge(pe.h, ev(("ctr", gi, tc)))
                    v.wait_ge(dxc[tc % 2].h, ev(("crdma", gi, tc - 2)))
                    ve.inc(v.tensor_copy(c_t9[:, (tc % 2) * 128:(tc % 2 + 1) * 128],
                                         pb(bi, 0, 9, 0, 128)))
                    setev(("ctev", gi, tc), ve.n)
                on("vector", _v3)

                def _sy(sy, tc=tc):
                    sy.wait_ge(ve.h, ev(("ctev", gi, tc)))
                    sm = dxc[tc % 2]
                    sy.dma_start(
                        crows_d[:, tc * 128:(tc + 1) * 128],
                        c_t9[:, (tc % 2) * 128:(tc % 2 + 1) * 128]).then_inc(sm.h, 16)
                    sm.n += 16
                    setev(("crdma", gi, tc), sm.n)
                on("sync", _sy)

            barrier()

        def chunk_src(c, which):
            e, ic = c // ICH, c % ICH
            if which == "g":
                base = wg_h[e] if e < 8 else swg_h
            else:
                base = wu_h[e] if e < 8 else swu_h
            return base[:, ic * 128:(ic + 1) * 128].rearrange("(kc p) m -> p kc m", p=128)

        def wd_src(c):
            e, ic = c // ICH, c % ICH
            base = wd_h[e] if e < 8 else swd_h
            return base[ic * 128:(ic + 1) * 128, :]

        def prev_ft(c, ft, k=2):
            if ft >= k:
                return (c, ft - k)
            return (c - 1, ft + FT - k)

        def emit_ffn(fi, cb_sb, crow_sb, h_sb, wd_sb, gs_st, us_st):
            parts = []
            c0 = 0
            while c0 < NCH:
                parts.append((c0, min(PART, NCH - c0)))
                c0 += PART
            LA = 2
            setev(("ffnpre", fi), pe.n)

            def cbv(e, ft):
                return cb_sb[:, (e % 2) * T + ft * 512:(e % 2) * T + (ft + 1) * 512]

            def _syc(sy):
                sy.wait_ge(dxc[0].h, ev(("crdma", fi, NTC - 2)))
                sy.wait_ge(dxc[1].h, ev(("crdma", fi, NTC - 1)))
                for e in range(NEXP):
                    if e >= 2:
                        sy.wait_ge(pe.h, ev(("cbmm", fi, e - 2, FT - 1)))
                    sm = dxl[e % 2]
                    sy.dma_start(crow_sb[0:1, (e % 2) * T:(e % 2 + 1) * T],
                                 crows_d[e:e + 1, :]).then_inc(sm.h, 16)
                    sm.n += 16
                    setev(("crl", fi, e), sm.n)
            on("sync", _syc)

            def _g(g):
                part_of = {}
                for pi, (pc0, pn) in enumerate(parts):
                    part_of[pc0 + min(1, pn - 1)] = pi
                for c in range(NCH):
                    g.wait_ge(pe.h, ev(("gudone", fi, c - LA)) if c >= LA else ev(("ffnpre", fi)))
                    sm = dwg[c % 2]
                    g.dma_start(wga[:, (c % 2) * DM:(c % 2 + 1) * DM]
                                .rearrange("p (kc m) -> p kc m", kc=8),
                                chunk_src(c, "g")).then_inc(sm.h, 16)
                    g.dma_start(wua[:, (c % 2) * DM:(c % 2 + 1) * DM]
                                .rearrange("p (kc m) -> p kc m", kc=8),
                                chunk_src(c, "u")).then_inc(sm.h, 16)
                    sm.n += 32
                    setev(("gudma", fi, c), sm.n)
                    if c in part_of:
                        pi = part_of[c]
                        pc0, pn = parts[pi]
                        g.wait_ge(pe.h, ev(("ddone", fi, pi - 1)) if pi > 0 else ev(("ffnpre", fi)))
                        for s_ in range(pn):
                            g.dma_start(wd_sb[:, s_ * DM:(s_ + 1) * DM], wd_src(pc0 + s_)).then_inc(dwd.h, 16)
                            dwd.n += 16
                        setev(("wddma", fi, pi), dwd.n)
            on("gpsimd", _g)

            def _t(t):
                for pi, (pc0, pn) in enumerate(parts):
                    for s_ in range(pn):
                        c = pc0 + s_
                        e = c // ICH
                        if c % ICH == 0:
                            t.wait_ge(dxl[e % 2].h, ev(("crl", fi, e)))
                            for ft in range(FT):
                                bi = 4 + ft
                                bank_claim(t, bi)
                                inst = t.matmul(pb(bi), ones_b[:],
                                                crow_sb[0:1, (e % 2) * T + ft * 512:(e % 2) * T + (ft + 1) * 512],
                                                start=True, stop=True)
                                setev(("cbmm", fi, e, ft), pe.inc(inst))
                                bank_release(bi, ve, ("cbev", fi, e, ft))
                        t.wait_ge(dwg[c % 2].h, ev(("gudma", fi, c)))
                        for ft in range(FT):
                            gb = ft % 2
                            ub = 2 + ft % 2
                            bank_claim(t, gb)
                            for kc in range(NKC):
                                inst = t.matmul(pb(gb),
                                                wga[:, (c % 2) * DM + kc * 128:(c % 2) * DM + (kc + 1) * 128],
                                                xnv(kc, ft * 512, (ft + 1) * 512),
                                                start=(kc == 0), stop=(kc == NKC - 1))
                            setev(("gmm", fi, c, ft), pe.inc(inst))
                            bank_release(gb, se, ("silu", fi, c, ft))
                            bank_claim(t, ub)
                            for kc in range(NKC):
                                inst = t.matmul(pb(ub),
                                                wua[:, (c % 2) * DM + kc * 128:(c % 2) * DM + (kc + 1) * 128],
                                                xnv(kc, ft * 512, (ft + 1) * 512),
                                                start=(kc == 0), stop=(kc == NKC - 1))
                            setev(("umm", fi, c, ft), pe.inc(inst))
                            bank_release(ub, ve, ("usc", fi, c, ft))
                        setev(("gudone", fi, c), pe.n)
                    t.wait_ge(dwd.h, ev(("wddma", fi, pi)))
                    for tb in range(FT):
                        t.wait_ge(ve.h, ev(("hevq", fi, pi, tb)))
                        for og in range(4):
                            b0 = 4 + 2 * (og % 2)
                            bank_claim(t, b0)
                            bank_claim(t, b0 + 1)
                            for kc in range(pn):
                                for o2 in range(2):
                                    o = og * 2 + o2
                                    inst = t.matmul(pb(b0 + o2),
                                                    wd_sb[:, kc * DM + o * 128: kc * DM + (o + 1) * 128],
                                                    h_sb[:, kc * T + tb * 512: kc * T + (tb + 1) * 512],
                                                    start=(kc == 0), stop=(kc == pn - 1))
                            setev(("dmm", fi, pi, tb, og), pe.inc(inst))
                            bank_release(b0, ve, ("dev", fi, pi, tb, og))
                            bank_release(b0 + 1, ve, ("dev", fi, pi, tb, og))
                    setev(("ddone", fi, pi), pe.n)
            on("tensor", _t)

            def _s(s):
                for pi, (pc0, pn) in enumerate(parts):
                    for s_ in range(pn):
                        c = pc0 + s_
                        for ft in range(FT):
                            s.wait_ge(pe.h, ev(("gmm", fi, c, ft)))
                            s.wait_ge(ve.h, ev(("hev", fi) + prev_ft(c, ft)))
                            se.inc(s.activation(gs_st[:, (ft % 2) * 512:(ft % 2 + 1) * 512],
                                                pb(ft % 2), AF.Silu))
                            setev(("silu", fi, c, ft), se.n)
            on("scalar", _s)

            def _v(v):
                for pi, (pc0, pn) in enumerate(parts):
                    for s_ in range(pn):
                        c = pc0 + s_
                        e = c // ICH
                        if c % ICH == 0:
                            for ft in range(FT):
                                v.wait_ge(pe.h, ev(("cbmm", fi, e, ft)))
                                ve.inc(v.tensor_copy(cbv(e, ft), pb(4 + ft)))
                                setev(("cbev", fi, e, ft), ve.n)
                            v.wait_ge(ve.h, ve.n)
                        for ft in range(FT):
                            v.wait_ge(pe.h, ev(("umm", fi, c, ft)))
                            ve.inc(v.tensor_mul(us_st[:, (ft % 2) * 512:(ft % 2 + 1) * 512],
                                                pb(2 + ft % 2),
                                                cbv(e, ft)))
                            setev(("usc", fi, c, ft), ve.n)
                            v.wait_ge(ve.h, ve.n)
                            v.wait_ge(se.h, ev(("silu", fi, c, ft)))
                            ve.inc(v.tensor_mul(h_sb[:, s_ * T + ft * 512: s_ * T + (ft + 1) * 512],
                                                gs_st[:, (ft % 2) * 512:(ft % 2 + 1) * 512],
                                                us_st[:, (ft % 2) * 512:(ft % 2 + 1) * 512]))
                            setev(("hev", fi, c, ft), ve.n)
                            if s_ == pn - 1:
                                v.wait_ge(ve.h, ve.n)
                                ve.inc(v.tensor_copy(c1a[:, 0:2],
                                                     h_sb[:, s_ * T + ft * 512 + 510:
                                                          s_ * T + ft * 512 + 512]))
                                setev(("hevq", fi, pi, ft), ve.n)
                    for tb in range(FT):
                        for og in range(4):
                            b0 = 4 + 2 * (og % 2)
                            v.wait_ge(pe.h, ev(("dmm", fi, pi, tb, og)))
                            for o2 in range(2):
                                o = og * 2 + o2
                                inst = v.tensor_add(xv(o, tb * 512, (tb + 1) * 512),
                                                    xv(o, tb * 512, (tb + 1) * 512),
                                                    pb(b0 + o2))
                            ve.inc(inst)
                            setev(("dev", fi, pi, tb, og), ve.n)
            on("vector", _v)
            barrier()

        def emit_attention(AT, n_heads=NH, proj_only=False):
            (k_ext2, v_t, mk_rhse, q_sb, v_fm, zb_sb, att_st, p_ts, vfm_close) = AT
            rhs_ext2 = None
            NQT, NJC, NIC = 16, 16, 4
            setev("attpre_pe", pe.n)
            setev("attpre_ve", ve.n)

            def _g(g):
                g.wait_ge(ve.h, ev("attpre_ve"))
                g.wait_ge(pe.h, ev("attpre_pe"))
                g.memset(v_t[:], 0.0)
                g.memset(v_t[:].rearrange("p (j c) -> p j c", c=128)[:, :, 64:65], 1.0)
                for kv in range(NKV):
                    g.memset(k_ext2[kv][64:128, :], 0.0)
                for kv in range(NKV):
                    g.memset(k_ext2[kv][64:65, :], 1.0)
                ge.inc(g.memset(m_bf[:], 0.0))
                setev("attc", ge.n)
                for mq in range(12):
                    g.wait_ge(pe.h, ev(("pmm", mq - 2, FT - 1)) if mq >= 2 else ev("attpre_pe"))
                    src = (wq_h[:, mq * 128:(mq + 1) * 128] if mq < 8
                           else wkv_h[:, (mq - 8) * 128:(mq - 7) * 128])
                    sm = dwg[mq % 2]
                    g.dma_start(wga[:, (mq % 2) * DM:(mq % 2 + 1) * DM]
                                .rearrange("p (kc m) -> p kc m", kc=8),
                                src.rearrange("(kc p) m -> p kc m", p=128)).then_inc(sm.h, 16)
                    sm.n += 16
                    setev(("wdma", mq), sm.n)
            on("gpsimd", _g)

            def _t(t):
                t.wait_ge(ge.h, ev("attc"))
                for mq in range(12):
                    t.wait_ge(dwg[mq % 2].h, ev(("wdma", mq)))
                    for ft in range(FT):
                        bi = ft % 4
                        bank_claim(t, bi)
                        for kc in range(NKC):
                            inst = t.matmul(pb(bi),
                                            wga[:, (mq % 2) * DM + kc * 128:(mq % 2) * DM + (kc + 1) * 128],
                                            xnv(kc, ft * 512, (ft + 1) * 512),
                                            start=(kc == 0), stop=(kc == NKC - 1))
                        setev(("pmm", mq, ft), pe.inc(inst))
                        bank_release(bi, ve, ("pev", mq, ft))
            on("tensor", _t)

            def _v(v):
                for mq in range(12):
                    for ft in range(FT):
                        bi = ft % 4
                        v.wait_ge(pe.h, ev(("pmm", mq, ft)))
                        if mq < 8:
                            inst = v.tensor_scalar(q_sb[:, mq * T + ft * 512: mq * T + (ft + 1) * 512],
                                                   pb(bi), ks_col[:, mq:mq + 1], None, ALU.mult)
                        elif mq < 10:
                            kk = mq - 8
                            v.tensor_copy(k_ext2[2 * kk][0:64, ft * 512:(ft + 1) * 512], pb(bi, 0, 64))
                            inst = v.tensor_copy(k_ext2[2 * kk + 1][0:64, ft * 512:(ft + 1) * 512],
                                                 pb(bi, 64, 128))
                        else:
                            vv = mq - 10
                            inst = v.tensor_copy(v_fm[:, vv * T + ft * 512: vv * T + (ft + 1) * 512],
                                                 pb(bi))
                        ve.inc(inst)
                        setev(("pev", mq, ft), ve.n)
            on("vector", _v)

            def _t2(t):
                for idx in range(32):
                    vv, jc = idx // 16, idx % 16
                    t.wait_ge(ve.h, ev(("pev", 10 + vv, (jc * 128) // 512)))
                    bank_claim(t, 6)
                    pe.inc(t.transpose(pb(6, 0, 128, 0, 64).bitcast(BF16),
                                       v_fm[:, vv * T + jc * 128: vv * T + (jc + 1) * 128],
                                       eye_b[:]))
                    setev(("vt", idx), pe.n)
                    bank_release(6, ve, ("vtev", idx))
            on("tensor", _t2)

            def _v2(v):
                for idx in range(32):
                    vv, jc = idx // 16, idx % 16
                    v.wait_ge(pe.h, ev(("vt", idx)))
                    src = pb(6, 0, 128, 0, 64).bitcast(BF16)
                    for hh in range(2):
                        kvh = 2 * vv + hh
                        inst = v.tensor_copy(v_t[:, kvh * 2048 + jc * 128: kvh * 2048 + jc * 128 + 64],
                                             src[:, hh * 64:(hh + 1) * 64])
                    ve.inc(inst)
                    setev(("vtev", idx), ve.n)
            on("vector", _v2)
            barrier()
            vfm_close()
            rhs_ext2 = mk_rhse()

            def prev_avp(h, ic, jp):
                if jp >= 2:
                    return ("avmm", h, ic, (jp - 2) * 2 + 1)
                if ic >= 1:
                    return ("avmm", h, ic - 1, (6 + jp) * 2 + 1)
                if h >= 1:
                    return ("avmm", h - 1, NIC - 1, (6 + jp) * 2 + 1)
                return None

            def head_passA_qt(h, qt):
                kv = h // 4

                if qt == 0:
                    def _vq(v):
                        if h >= 2:
                            v.wait_ge(pe.h, ev(("bdone", h - 2)))
                        v.wait_ge(ve.h, ve.n)
                        ve.inc(v.tensor_copy(rhs_ext2[h % 2][0:64, :],
                                             q_sb[(h % 2) * 64:(h % 2) * 64 + 64,
                                                  (h // 2) * T:(h // 2 + 1) * T]))
                        v.wait_ge(ve.h, ve.n)
                        ve.inc(v.memset(rhs_ext2[h % 2][64:65, :] if h >= 2
                                        else rhs_ext2[h % 2][64:128, :], 0.0))
                        setev(("qcp", h), ve.n)
                    on("vector", _vq)

                def _t(t):
                    if qt == 0:
                        t.wait_ge(ve.h, ev(("qcp", h)))
                    for jj in range(FT):
                        bi = jj % 2
                        bank_claim(t, bi)
                        inst = t.matmul(pb(bi),
                                        rhs_ext2[h % 2][0:128, qt * 128:(qt + 1) * 128],
                                        k_ext2[kv][0:128, jj * 512:(jj + 1) * 512],
                                        start=True, stop=True)
                        setev(("amm", h, qt, jj), pe.inc(inst))
                        bank_release(bi, ve, ("amax", h, qt, jj // 2))
                on("tensor", _t)

                def _v(v):
                    if qt == 0 and h >= 2:
                        v.wait_ge(pe.h, ev(("mt", h - 2)))
                    for jp in range(2):
                        v.wait_ge(pe.h, ev(("amm", h, qt, jp * 2 + 1)))
                        ve.inc(v.tensor_reduce(mm4[:, jp:jp + 1], psA[:, 0:1024],
                                               axis=AX.X, op=ALU.max))
                        setev(("amax", h, qt, jp), ve.n)
                    v.wait_ge(ve.h, ve.n)
                    ve.inc(v.tensor_reduce(m_bf[:, (h % 2) * 16 + qt:(h % 2) * 16 + qt + 1],
                                           mm4[:, 0:2], axis=AX.X, op=ALU.max, negate=True))
                    v.wait_ge(ve.h, ve.n)
                    setev(("mq", h, qt), ve.n)
                on("vector", _v)

            def head_mrow(h):
                def _t(t):
                    t.wait_ge(ve.h, ev(("mq", h, 15)))
                    bank_claim(t, 6)
                    pe.inc(t.transpose(pb(6, 0, 16, 0, 64).bitcast(BF16),
                                       m_bf[:, (h % 2) * 16:(h % 2) * 16 + 16], eye_b[:]))
                    setev(("mt", h), pe.n)
                    bank_release(6, ve, ("mtev", h))
                on("tensor", _t)

                def _v(v):
                    v.wait_ge(pe.h, ev(("mt", h)))
                    if h >= 2:
                        v.wait_ge(dxm[h % 2].h, ev(("mrow", h - 2)))
                    ve.inc(v.tensor_copy(m_t[:, (h % 2) * 128:(h % 2) * 128 + 128],
                                         pb(6, 0, 16, 0, 64).bitcast(BF16)))
                    setev(("mtev", h), ve.n)
                on("vector", _v)

                def _sy(sy):
                    sy.wait_ge(ve.h, ev(("mtev", h)))
                    sm = dxm[h % 2]
                    sy.dma_start(rhs_ext2[h % 2][64:65, :].rearrange("one (p j) -> one p j", p=16),
                                 m_t[:, (h % 2) * 128:(h % 2) * 128 + 128]).then_inc(sm.h, 16)
                    sm.n += 16
                    setev(("mrow", h), sm.n)
                on("sync", _sy)

            def head_passB(h):
                kv = h // 4

                def _t(t, ic, q4):
                    if ic == 0 and q4 == 0:
                        t.wait_ge(dxm[h % 2].h, ev(("mrow", h)))
                    av = 4 + ic % 2
                    if q4 == 0:
                        bank_claim(t, av)
                    for jp in range(q4 * 2, q4 * 2 + 2):
                        bank_claim(t, 2)
                        bank_claim(t, 3)
                        for j2 in range(2):
                            jc = jp * 2 + j2
                            inst = t.matmul(pb(2 + j2), k_ext2[kv][0:128, jc * 128:(jc + 1) * 128],
                                            rhs_ext2[h % 2][0:128, ic * 512:(ic + 1) * 512],
                                            start=True, stop=True)
                            setev(("st", h, ic, jc), pe.inc(inst))
                        bank_release(2, se, ("exp", h, ic, jp))
                        bank_release(3, se, ("exp", h, ic, jp))
                        if jp >= 1:
                            t.wait_ge(se.h, ev(("exp", h, ic, jp - 1)))
                            for j2 in range(2):
                                jc = (jp - 1) * 2 + j2
                                inst = t.matmul(pb(av),
                                                v_t[:, kv * 2048 + jc * 128: kv * 2048 + (jc + 1) * 128],
                                                p_ts[(jp - 1) % 2][:, j2 * 512:(j2 + 1) * 512],
                                                start=(jc == 0), stop=False)
                                setev(("avmm", h, ic, jc), pe.inc(inst))
                    if q4 == 3:
                        jp = 7
                        t.wait_ge(se.h, ev(("exp", h, ic, jp)))
                        for j2 in range(2):
                            jc = jp * 2 + j2
                            inst = t.matmul(pb(av),
                                            v_t[:, kv * 2048 + jc * 128: kv * 2048 + (jc + 1) * 128],
                                            p_ts[jp % 2][:, j2 * 512:(j2 + 1) * 512],
                                            start=False, stop=(j2 == 1))
                            setev(("avmm", h, ic, jc), pe.inc(inst))
                        setev(("av", h, ic), pe.n)
                        bank_release(av, ve, ("omul", h, ic))

                def _s(s, ic):
                    for jp in range(8):
                        s.wait_ge(pe.h, ev(("st", h, ic, jp * 2 + 1)))
                        pv = prev_avp(h, ic, jp)
                        if pv is not None:
                            s.wait_ge(pe.h, ev(pv))
                        se.inc(s.activation(p_ts[jp % 2][:], psA[:, 1024:2048], AF.Exp))
                        setev(("exp", h, ic, jp), se.n)
                    s.wait_ge(pe.h, ev(("av", h, ic)))
                    se.inc(s.activation(zr[:], pb(4 + ic % 2, 64, 65), AF.Ln))
                    s.wait_ge(se.h, se.n)
                    if ic >= 1:
                        s.wait_ge(pe.h, ev(("zbmm", h, ic - 1)))
                    elif h >= 1:
                        s.wait_ge(pe.h, ev(("zbmm", h - 1, NIC - 1)))
                    se.inc(s.activation(zrb[:], zr[:], AF.Exp, bias=0.0, scale=-1.0))
                    s.wait_ge(se.h, se.n)
                    setev(("zrb", h, ic), se.n)

                def _v(v, ic):
                    pass

                def _t2(t, ic):
                    t.wait_ge(se.h, ev(("zrb", h, ic)))
                    bank_claim(t, 7)
                    pe.inc(t.matmul(pb(7, 0, 64), ones_b[0:1, 0:64], zrb[:],
                                    start=True, stop=True))
                    setev(("zbmm", h, ic), pe.n)
                    bank_release(7, ve, ("zbcp", h, ic))

                def _v2(v, ic):
                    av = 4 + ic % 2
                    v.wait_ge(pe.h, ev(("zbmm", h, ic)))
                    ve.inc(v.tensor_copy(zb_sb[:], pb(7, 0, 64)))
                    setev(("zbcp", h, ic), ve.n)
                    v.wait_ge(ve.h, ve.n)
                    ve.inc(v.tensor_mul(att_st[0:64, :], pb(av, 0, 64), zb_sb[:]))
                    setev(("omul", h, ic), ve.n)
                    v.wait_ge(ve.h, ve.n)
                    half = att_st[0:64, :]
                    if h % 2 == 1:
                        ve.inc(v.tensor_copy(att_st[64:128, :], att_st[0:64, :]))
                        v.wait_ge(ve.h, ve.n)
                        half = att_st[64:128, :]
                    ve.inc(v.tensor_add(
                        x_sb[(h % 2) * 64:(h % 2) * 64 + 64,
                             (h // 2) * T + ic * 512:(h // 2) * T + (ic + 1) * 512],
                        x_sb[(h % 2) * 64:(h % 2) * 64 + 64,
                             (h // 2) * T + ic * 512:(h // 2) * T + (ic + 1) * 512],
                        half))
                    setev(("oev", h, ic), ve.n)

                return _t, _t2, _s, _v, _v2

            if proj_only:
                return

            def emit_B_unit(h, u, fns):
                _t, _t2, _s, _v, _v2 = fns
                ic, q4 = u // 4, u % 4
                if u == 0:
                    def _sall(s_):
                        for ic2 in range(NIC):
                            _s(s_, ic2)
                    on("scalar", _sall)
                on("tensor", lambda t: _t(t, ic, q4))
                if q4 == 3:
                    if ic >= 1:
                        on("tensor", lambda t: _t2(t, ic - 1))
                        on("vector", lambda v: _v2(v, ic - 1))
                    if ic == NIC - 1:
                        on("tensor", lambda t: _t2(t, NIC - 1))
                        on("vector", lambda v: _v2(v, NIC - 1))
                        setev(("bdone", h), snap[("av", h, NIC - 1)])

            bfns = {}
            for h in range(n_heads + 1):
                for u in range(16):
                    if h < n_heads:
                        head_passA_qt(h, u)
                    if h >= 1:
                        emit_B_unit(h - 1, u, bfns[h - 1])
                if h < n_heads:
                    head_mrow(h)
                    bfns[h] = head_passB(h)
            barrier()

        def emit_epilogue():
            def _t(t):
                for tc in range(NTC):
                    for kc in range(NKC):
                        bi = kc % 4
                        bank_claim(t, bi)
                        pe.inc(t.transpose(pb(bi, 0, 128, 0, 128),
                                           xv(kc, tc * 128, (tc + 1) * 128), eye_f[:]))
                        setev(("ytr", tc, kc), pe.n)
                        bank_release(bi, ve, ("yevk", tc, kc))
            on("tensor", _t)

            def _v(v):
                for tc in range(NTC):
                    v.wait_ge(dxx[tc % 2].h, ev(("yst", tc - 2)))
                    for kc in range(NKC):
                        bi = kc % 4
                        v.wait_ge(pe.h, ev(("ytr", tc, kc)))
                        ve.inc(v.tensor_copy(stage[:, (tc % 2) * DM + kc * 128:(tc % 2) * DM + (kc + 1) * 128],
                                             pb(bi, 0, 128, 0, 128)))
                        setev(("yevk", tc, kc), ve.n)
                    setev(("yev", tc), ve.n)
            on("vector", _v)

            def _sy(sy):
                for tc in range(NTC):
                    sy.wait_ge(ve.h, ev(("yev", tc)))
                    sm = dxx[tc % 2]
                    sy.dma_start(y_h[tc * 128:(tc + 1) * 128, :],
                                 stage[:, (tc % 2) * DM:(tc % 2 + 1) * DM]).then_inc(sm.h, 16)
                    sm.n += 16
                    setev(("yst", tc), sm.n)
                sy.wait_ge(dxx[0].h, dxx[0].n)
                sy.wait_ge(dxx[1].h, dxx[1].n)
                sy.wait_ge(dx.h, dx.n)
            on("sync", _sy)

        def emit_all():
            bankuse.clear()
            with ExitStack() as pst:
                pro_box["ks_st"] = scoped(pst, "ks_st", [8, 128], F32)
                pro_box["gsw_sb"] = scoped(pst, "gsw_sb", [9, DM], F32)
                emit_prologue()
            emit_ln(0)
            if upto == "ln1":
                return
            with ExitStack() as fst:
                emit_gating(0)
                if upto == "gate1":
                    return
                cb_sb = scoped(fst, "cb1", [128, 2 * T], BF16)
                crow_sb = scoped(fst, "crow1", [1, 2 * T], BF16)
                h_sb = scoped(fst, "h1", [128, PART * T], BF16)
                wd_sb = scoped(fst, "wd1", [128, PART * DM], BF16)
                gs_st = scoped(fst, "gs1", [128, 2 * 512], BF16)
                us_st = scoped(fst, "us1", [128, 2 * 512], BF16)
                emit_ffn(0, cb_sb, crow_sb, h_sb, wd_sb, gs_st, us_st)
            if upto == "ffn1":
                return
            emit_ln(1)
            if upto == "ln2":
                return
            with ExitStack() as ast_:
                def asb(n_, s_, d_):
                    return scoped(ast_, n_, s_, d_)
                k_ext2 = [asb(f"kext{i}", [128, T], BF16) for i in range(NKV)]
                v_t = asb("v_t", [128, NKV * 16 * 128], BF16)
                q_sb = asb("q_sb", [128, NKC * T], BF16)
                vf_st = ExitStack()
                v_fm = TEN["v_fm"] = vf_st.enter_context(
                    nc.sbuf_tensor(f"v_fm_p{ctx['pass']}", [128, 2 * T], BF16))

                def mk_rhse():
                    return [asb(f"rhse{i}", [128, T], BF16) for i in range(2)]
                att_st = stage[:, 0:512]
                zb_sb = stage[0:64, 512:1024]
                p_t = stage[:, 1024:1536].bitcast(BF16)
                p_t2 = stage[:, 1536:2048].bitcast(BF16)
                if upto.startswith("attn") and upto != "attn":
                    nhh = 0 if upto == "attn0" else int(upto[4:])
                    emit_attention((k_ext2, v_t, mk_rhse, q_sb, v_fm, zb_sb, att_st, (p_t, p_t2),
                                    vf_st.close),
                                   n_heads=max(nhh, 1), proj_only=(nhh == 0))
                    return
                emit_attention((k_ext2, v_t, mk_rhse, q_sb, v_fm, zb_sb, att_st, (p_t, p_t2), vf_st.close))
            if upto == "attn":
                return
            emit_ln(2)
            if upto == "ln3":
                return
            with ExitStack() as fst:
                emit_gating(1)
                cb_sb = scoped(fst, "cb2", [128, 2 * T], BF16)
                crow_sb = scoped(fst, "crow2", [1, 2 * T], BF16)
                h_sb = scoped(fst, "h2", [128, PART * T], BF16)
                wd_sb = scoped(fst, "wd2", [128, PART * DM], BF16)
                gs_st = scoped(fst, "gs2", [128, 2 * 512], BF16)
                us_st = scoped(fst, "us2", [128, 2 * 512], BF16)
                emit_ffn(1, cb_sb, crow_sb, h_sb, wd_sb, gs_st, us_st)

        def emit_taps():
            if not taps:
                return

            def _sy(sy):
                sy.wait_ge(ve.h, ve.n)
                sy.wait_ge(se.h, se.n)
                sy.wait_ge(pe.h, pe.n)
                sy.wait_ge(ge.h, ge.n)
                for tn, shape, dt in taps:
                    src = crows_d if tn == "crows_d" else TEN[tn][0:shape[0], 0:shape[1]]
                    sy.dma_start(tap_out[tn], src).then_inc(dx.h, 16)
                    dx.n += 16
                sy.wait_ge(dx.h, dx.n)
            on("sync", _sy)

        ctx["real"] = False
        ctx["pass"] = 1
        emit_all()
        if upto == "all":
            emit_epilogue()
        emit_taps()
        pass1 = [s_.n for s_ in ALLSC]
        for s_ in ALLSC:
            s_.n = 0
        ctx["real"] = True
        ctx["pass"] = 2
        emit_all()
        if upto == "all":
            emit_epilogue()
        emit_taps()
        pass2 = [s_.n for s_ in ALLSC]
        assert pass1 == pass2, (pass1, pass2)
    return nc


def _in_maps(L0_attn_x, L0_attn_Wq, L0_attn_Wkv, L0_attn_k_scale,
             L0_ffn_gate_w, L0_ffn_w_gate, L0_ffn_w_up, L0_ffn_w_down,
             L0_ffn_sw_gate, L0_ffn_sw_up, L0_ffn_sw_down, L0_ffn_shared_gate_w):
    shared = {
        "wq": np.ascontiguousarray(L0_attn_Wq, np.float32),
        "wkv": np.ascontiguousarray(L0_attn_Wkv, np.float32),
        "ks": np.ascontiguousarray(np.asarray(L0_attn_k_scale, np.float32).reshape(8, 128)),
        "gw": np.ascontiguousarray(L0_ffn_gate_w, np.float32),
        "sgw": np.ascontiguousarray(np.asarray(L0_ffn_shared_gate_w, np.float32).reshape(1, DM)),
        "wg": np.ascontiguousarray(L0_ffn_w_gate, np.float32),
        "wu": np.ascontiguousarray(L0_ffn_w_up, np.float32),
        "wd": np.ascontiguousarray(L0_ffn_w_down, np.float32),
        "swg": np.ascontiguousarray(L0_ffn_sw_gate, np.float32),
        "swu": np.ascontiguousarray(L0_ffn_sw_up, np.float32),
        "swd": np.ascontiguousarray(L0_ffn_sw_down, np.float32),
    }
    x = np.asarray(L0_attn_x, np.float32)
    return [dict(shared, x=np.ascontiguousarray(x[c])) for c in range(8)]


def kernel(**inputs):
    nc = build_program()
    in_maps = _in_maps(**inputs)
    res = run_bass_kernel_spmd(nc, in_maps, core_ids=list(range(8)))
    return np.stack([res.results[c]["y"] for c in range(8)], axis=0)


# version 11
# speedup vs baseline: 1.1982x; 1.0041x over previous
import numpy as np
import concourse.bass as bass
import concourse.mybir as mybir
from concourse.bass_utils import run_bass_kernel_spmd
from contextlib import ExitStack

F32 = mybir.dt.float32
BF16 = mybir.dt.bfloat16
AF = mybir.ActivationFunctionType
ALU = mybir.AluOpType
AX = mybir.AxisListType

T = 2048
NKC = 8
DM = 1024
NCH = 99
ICH = 11
NEXP = 9
NH = 16
NKV = 4
BIG = 30000.0
PART = 6
FT = 4
NTC = 16


class FakeInst:
    def then_inc(self, sem, k):
        return self


_FI = FakeInst()


class FakeEng:
    def wait_ge(self, *a, **k):
        pass

    def __getattr__(self, name):
        def op(*a, **k):
            return _FI
        return op


_FE = FakeEng()


class SC:
    __slots__ = ("h", "n")

    def __init__(self, h):
        self.h = h
        self.n = 0

    def inc(self, inst, k=1):
        inst.then_inc(self.h, k)
        self.n += k
        return self.n


def build_program(upto="all", taps=()):
    nc = bass.Bass("TRN2", target_bir_lowering=False)
    x_h = nc.dram_tensor("x", [T, DM], F32, kind="ExternalInput").ap()
    wq_h = nc.dram_tensor("wq", [DM, DM], F32, kind="ExternalInput").ap()
    wkv_h = nc.dram_tensor("wkv", [DM, 512], F32, kind="ExternalInput").ap()
    ks_h = nc.dram_tensor("ks", [8, 128], F32, kind="ExternalInput").ap()
    gw_h = nc.dram_tensor("gw", [8, DM], F32, kind="ExternalInput").ap()
    sgw_h = nc.dram_tensor("sgw", [1, DM], F32, kind="ExternalInput").ap()
    wg_h = nc.dram_tensor("wg", [8, DM, 1408], F32, kind="ExternalInput").ap()
    wu_h = nc.dram_tensor("wu", [8, DM, 1408], F32, kind="ExternalInput").ap()
    wd_h = nc.dram_tensor("wd", [8, 1408, DM], F32, kind="ExternalInput").ap()
    swg_h = nc.dram_tensor("swg", [DM, 1408], F32, kind="ExternalInput").ap()
    swu_h = nc.dram_tensor("swu", [DM, 1408], F32, kind="ExternalInput").ap()
    swd_h = nc.dram_tensor("swd", [1408, DM], F32, kind="ExternalInput").ap()
    y_h = nc.dram_tensor("y", [T, DM], F32, kind="ExternalOutput").ap()
    crows_d = nc.dram_tensor("crows_d", [NEXP, T], BF16, kind="Internal").ap()
    tap_out = {}
    for tn, shape, dt in taps:
        tap_out[tn] = nc.dram_tensor("tap_" + tn, shape, dt, kind="ExternalOutput").ap()

    snap = {}
    TEN = {}

    with ExitStack() as st:
        def sb(n_, shape, dt):
            h = st.enter_context(nc.sbuf_tensor(n_, shape, dt))
            TEN[n_] = h
            return h

        x_sb = sb("x_sb", [128, NKC * T], F32)
        xn_sb = sb("xn_sb", [128, NKC * T], BF16)
        stage = sb("stage", [128, 2 * DM], F32)
        wga = sb("wga", [128, 2 * DM], BF16)
        wua = sb("wua", [128, 2 * DM], BF16)
        eye_b = sb("eye_b", [128, 128], BF16)
        eye_f = sb("eye_f", [128, 128], F32)
        ones_b = sb("ones_b", [1, 128], BF16)
        ones_fc = sb("ones_fc", [128, 1], F32)
        ones_fr = sb("ones_fr", [1, 128], F32)
        ks_col = sb("ks_col", [128, 8], F32)
        pro_box = {}
        gwT = sb("gwT", [128, NKC * 9], BF16)
        epsc = sb("epsc", [1, 1], F32)
        lg_sb = sb("lg_sb", [128, NTC * 9], F32)
        ctm = sb("ctm", [128, NTC * 9], F32)
        cm1 = sb("cm1", [128, NTC], F32)
        cm2 = sb("cm2", [128, NTC], F32)
        ci1 = sb("ci1", [128, NTC], F32)
        ci2 = sb("ci2", [128, NTC], F32)
        cdc = sb("cdc", [128, NTC], F32)
        sg1c = sb("sg1c", [128, NTC], F32)
        sgsh = sb("sgsh", [128, NTC], F32)
        c1a = sb("c1a", [128, NTC], F32)
        c2a = sb("c2a", [128, NTC], F32)
        m16a = stage[:, 0:128]
        m16b = stage[:, 128:256]
        m16c = stage[:, 256:384]
        iota16 = sb("iota16", [128, 128], F32)
        iotaB16 = sb("iotaB16", [128, 128], F32)
        c_t9 = sb("c_t9", [9, 2 * 128], BF16)
        rowp = sb("rowp", [1, 2 * T], F32)
        r_mean = rowp[0:1, 0:T]
        r_tmp = rowp[0:1, T:2 * T]
        zr = rowp[0:1, 0:512]
        zrb = sb("zrb", [1, 512], BF16)
        m_bf = sb("m_bf", [128, 2 * 16], BF16)
        m_t = sb("m_t", [16, 2 * 128], BF16)
        mm4 = sb("mm4", [128, 4], F32)

        psA = st.enter_context(nc.psum_tensor("psA", [128, 2048], F32))
        psB = st.enter_context(nc.psum_tensor("psB", [128, 2048], F32))

        def pb(i, p0=0, p1=128, c0=0, c1=512):
            t_ = psA if i < 4 else psB
            base = (i % 4) * 512
            return t_[p0:p1, base + c0:base + c1]

        dw = SC(st.enter_context(nc.semaphore("dw")))
        dx = SC(st.enter_context(nc.semaphore("dx")))
        dwg = [SC(st.enter_context(nc.semaphore(f"dwg{i}"))) for i in range(2)]
        dwd = SC(st.enter_context(nc.semaphore("dwd")))
        dxx = [SC(st.enter_context(nc.semaphore(f"dxx{i}"))) for i in range(2)]
        dxc = [SC(st.enter_context(nc.semaphore(f"dxc{i}"))) for i in range(2)]
        dxl = [SC(st.enter_context(nc.semaphore(f"dxl{i}"))) for i in range(2)]
        dxm = [SC(st.enter_context(nc.semaphore(f"dxm{i}"))) for i in range(2)]
        pe = SC(st.enter_context(nc.semaphore("pe")))
        ve = SC(st.enter_context(nc.semaphore("ve")))
        se = SC(st.enter_context(nc.semaphore("se")))
        ge = SC(st.enter_context(nc.semaphore("ge")))
        ALLSC = [dw, dx, pe, ve, se, ge, dwd] + dwg + dxx + dxc + dxl + dxm

        blk = st.enter_context(nc.Block())

        ctx = {"real": False, "pass": 0}
        bankuse = {}

        def scoped(stk, name, shape, dt):
            h = stk.enter_context(nc.sbuf_tensor(f"{name}_p{ctx['pass']}", shape, dt))
            TEN[name] = h
            return h

        def on(ename, f):
            if ctx["real"]:
                getattr(blk, ename)(f)
            else:
                f(_FE)

        def ev(name):
            return snap.get(name, 0)

        def setev(name, val):
            snap[name] = val

        def bank_claim(eng, bi):
            if bi in bankuse:
                sem, name = bankuse[bi]
                eng.wait_ge(sem.h, ev(name))

        def bank_release(bi, sem, name):
            bankuse[bi] = (sem, name)

        def barrier():
            pev, vev, sev = pe.n, ve.n, se.n

            def _t(t):
                t.wait_ge(ve.h, vev)
                t.wait_ge(se.h, sev)
            on("tensor", _t)

            def _v(v):
                v.wait_ge(pe.h, pev)
                v.wait_ge(se.h, sev)
            on("vector", _v)

            def _s(s):
                s.wait_ge(pe.h, pev)
                s.wait_ge(ve.h, vev)
            on("scalar", _s)

        def xv(kc, f0, f1):
            return x_sb[:, kc * T + f0: kc * T + f1]

        def xnv(kc, f0, f1):
            return xn_sb[:, kc * T + f0: kc * T + f1]

        def emit_prologue():
            ks_st = pro_box["ks_st"]
            gsw_sb = pro_box["gsw_sb"]
            def _g(g):
                ge.inc(g.memset(eye_b[:], 0.0))
                ge.inc(g.memset(eye_f[:], 0.0))
                g.wait_ge(ge.h, ge.n)
                g.affine_select(out=eye_b[:], in_=eye_b[:], compare_op=ALU.not_equal,
                                fill=1.0, base=0, pattern=[[-1, 128]], channel_multiplier=1)
                g.affine_select(out=eye_f[:], in_=eye_f[:], compare_op=ALU.not_equal,
                                fill=1.0, base=0, pattern=[[-1, 128]], channel_multiplier=1)
                g.memset(ones_b[:], 1.0)
                g.memset(ones_fc[:], 1.0)
                g.memset(ones_fr[:], 1.0)
                g.memset(epsc[:], 1e-5)
                g.iota(iota16[:].rearrange("p (tc e) -> p tc e", e=8),
                       pattern=[[0, 16], [1, 8]], base=0, channel_multiplier=0,
                       allow_small_or_imprecise_dtypes=True)
                ge.inc(g.iota(iotaB16[:].rearrange("p (tc e) -> p tc e", e=8),
                              pattern=[[0, 16], [1, 8]], base=int(BIG), channel_multiplier=0,
                              allow_small_or_imprecise_dtypes=True))
                setev("consts", ge.n)
            on("gpsimd", _g)

            def _sy(sy):
                sy.dma_start(ks_st[:], ks_h).then_inc(dx.h, 16)
                sy.dma_start(gsw_sb[0:8, :], gw_h).then_inc(dx.h, 16)
                sy.dma_start(gsw_sb[8:9, :], sgw_h).then_inc(dx.h, 16)
                dx.n += 48
                setev("hdr_dx", dx.n)
                for tc in range(NTC):
                    sy.wait_ge(pe.h, ev(("xtr", tc - 2, NKC - 1)))
                    sm = dxx[tc % 2]
                    sy.dma_start(stage[:, (tc % 2) * DM:(tc % 2 + 1) * DM],
                                 x_h[tc * 128:(tc + 1) * 128, :]).then_inc(sm.h, 16)
                    sm.n += 16
                    setev(("xdma", tc), sm.n)
            on("sync", _sy)

            def _t(t):
                t.wait_ge(ge.h, ev("consts"))
                for tc in range(NTC):
                    t.wait_ge(dxx[tc % 2].h, ev(("xdma", tc)))
                    for kc in range(NKC):
                        bi = kc % 4
                        bank_claim(t, bi)
                        pe.inc(t.transpose(pb(bi, 0, 128, 0, 128),
                                           stage[:, (tc % 2) * DM + kc * 128:(tc % 2) * DM + (kc + 1) * 128],
                                           eye_f[:]))
                        setev(("xtr", tc, kc), pe.n)
                        bank_release(bi, ve, ("xev", tc, kc))
                t.wait_ge(dx.h, ev("hdr_dx"))
                bank_claim(t, 4)
                pe.inc(t.transpose(pb(4, 0, 128, 0, 8), ks_st[:], eye_f[0:8, 0:8]))
                setev("kstr", pe.n)
                bank_release(4, ve, "ksev")
                bank_claim(t, 5)
                for kc in range(NKC):
                    t.wait_ge(ve.h, ev(("gwev", kc - 1)))
                    pe.inc(t.transpose(pb(5, 0, 128, kc * 9, kc * 9 + 9),
                                       gsw_sb[:, kc * 128:(kc + 1) * 128], eye_f[0:9, 0:9]))
                    setev(("gwtr", kc), pe.n)
                bank_release(5, ve, ("gwev", NKC - 1))
            on("tensor", _t)

            def _v(v):
                for tc in range(NTC):
                    for kc in range(NKC):
                        bi = kc % 4
                        v.wait_ge(pe.h, ev(("xtr", tc, kc)))
                        ve.inc(v.tensor_copy(xv(kc, tc * 128, (tc + 1) * 128), pb(bi, 0, 128, 0, 128)))
                        setev(("xev", tc, kc), ve.n)
                v.wait_ge(pe.h, ev("kstr"))
                ve.inc(v.tensor_scalar(ks_col[:], pb(4, 0, 128, 0, 8), 0.125, None, ALU.mult))
                setev("ksev", ve.n)
                for kc in range(NKC):
                    v.wait_ge(pe.h, ev(("gwtr", kc)))
                    ve.inc(v.tensor_copy(gwT[:, kc * 9:(kc + 1) * 9], pb(5, 0, 128, kc * 9, kc * 9 + 9)))
                    setev(("gwev", kc), ve.n)
            on("vector", _v)
            barrier()

        def emit_ln(li):
            def _s(s):
                for kc in range(NKC):
                    for ft in range(FT):
                        idx = kc * FT + ft
                        q = (idx % 4) * 512
                        s.wait_ge(pe.h, ev(("sqmm", li, idx - 4)))
                        se.inc(s.activation(stage[:, q:q + 512],
                                            xv(kc, ft * 512, (ft + 1) * 512), AF.Square))
                        setev(("sq", li, idx), se.n)
            on("scalar", _s)

            def _t(t):
                for bi in range(8):
                    bank_claim(t, bi)
                for kc in range(NKC):
                    for ft in range(FT):
                        idx = kc * FT + ft
                        q = (idx % 4) * 512
                        t.matmul(pb(ft, 0, 1), ones_fc[:], xv(kc, ft * 512, (ft + 1) * 512),
                                 start=(kc == 0), stop=(kc == NKC - 1))
                        t.wait_ge(se.h, ev(("sq", li, idx)))
                        inst = t.matmul(pb(4 + ft, 0, 1), ones_fc[:], stage[:, q:q + 512],
                                        start=(kc == 0), stop=(kc == NKC - 1))
                        setev(("sqmm", li, idx), pe.inc(inst))
                setev(("stats", li), pe.n)
                for bi in range(8):
                    bank_release(bi, ve, ("lnst", li))
            on("tensor", _t)

            def _v(v):
                v.wait_ge(pe.h, ev(("stats", li)))
                for ft in range(FT):
                    v.tensor_scalar(r_mean[0:1, ft * 512:(ft + 1) * 512], pb(ft, 0, 1),
                                    1.0 / DM, None, ALU.mult)
                    ve.inc(v.tensor_scalar(r_tmp[0:1, ft * 512:(ft + 1) * 512], pb(4 + ft, 0, 1),
                                           1.0 / DM, None, ALU.mult))
                setev(("lnst", li), ve.n)
                v.wait_ge(ve.h, ve.n)
                ve.inc(v.tensor_mul(psA[0:1, 0:2048], r_mean[:], r_mean[:]))
                v.wait_ge(ve.h, ve.n)
                ve.inc(v.tensor_sub(r_tmp[:], r_tmp[:], psA[0:1, 0:2048]))
                setev(("var", li), ve.n)
            on("vector", _v)

            def _s2(s):
                s.wait_ge(ve.h, ev(("var", li)))
                se.inc(s.activation(r_tmp[:], r_tmp[:], AF.Sqrt, bias=epsc[0:1, 0:1]))
                setev(("std", li), se.n)
            on("scalar", _s2)

            def _v2(v):
                v.wait_ge(se.h, ev(("std", li)))
                ve.inc(v.reciprocal(r_tmp[:], r_tmp[:]))
                v.wait_ge(ve.h, ve.n)
                ve.inc(v.tensor_mul(r_mean[:], r_mean[:], r_tmp[:]))
                setev(("ab", li), ve.n)
            on("vector", _v2)

            def _t2(t):
                t.wait_ge(ve.h, ev(("ab", li)))
                for ft in range(FT):
                    bank_claim(t, ft)
                    bank_claim(t, 4 + ft)
                    t.matmul(pb(ft, 0, 128), ones_fr[:], r_tmp[0:1, ft * 512:(ft + 1) * 512],
                             start=True, stop=True)
                    inst = t.matmul(pb(4 + ft, 0, 128), ones_fr[:], r_mean[0:1, ft * 512:(ft + 1) * 512],
                                    start=True, stop=True)
                    setev(("bc", li, ft), pe.inc(inst))
                    bank_release(ft, ve, ("xnev", li))
                    bank_release(4 + ft, ve, ("xnev", li))
            on("tensor", _t2)

            def _v3(v):
                for ft in range(FT):
                    v.wait_ge(pe.h, ev(("bc", li, ft)))
                for kc in range(NKC):
                    for ft in range(FT):
                        ve.inc(v.tensor_mul(stage[:, 0:512], xv(kc, ft * 512, (ft + 1) * 512),
                                            pb(ft, 0, 128)))
                        v.wait_ge(ve.h, ve.n)
                        ve.inc(v.tensor_sub(xnv(kc, ft * 512, (ft + 1) * 512),
                                            stage[:, 0:512], pb(4 + ft, 0, 128)))
                        v.wait_ge(ve.h, ve.n)
                setev(("xnev", li), ve.n)
            on("vector", _v3)
            barrier()

        def emit_gating(gi):
            L3 = lg_sb[:].rearrange("p (tc e) -> p tc e", e=9)
            L8 = L3[:, :, 0:8]
            C3 = ctm[:].rearrange("p (tc e) -> p tc e", e=9)
            i3 = iota16[:].rearrange("p (tc e) -> p tc e", e=8)

            def r3(x):
                return x[:].rearrange("p (tc e) -> p tc e", e=8)

            def bc(x):
                return x[:].rearrange("p (tc one) -> p tc one", one=1).to_broadcast([128, NTC, 8])

            for tc in range(NTC):
                def _t(t, tc=tc):
                    bi = 6 + tc % 2
                    bank_claim(t, bi)
                    for kc in range(NKC):
                        inst = t.matmul(pb(bi, 0, 128, 0, 9),
                                        xn_sb[:, kc * T + tc * 128: kc * T + (tc + 1) * 128],
                                        gwT[:, kc * 9:(kc + 1) * 9],
                                        start=(kc == 0), stop=(kc == NKC - 1))
                    setev(("lg", gi, tc), pe.inc(inst))
                    bank_release(bi, ve, ("lgev", gi, tc))
                on("tensor", _t)

                def _v(v, tc=tc):
                    bi = 6 + tc % 2
                    v.wait_ge(pe.h, ev(("lg", gi, tc)))
                    ve.inc(v.tensor_copy(lg_sb[:, tc * 9:(tc + 1) * 9], pb(bi, 0, 128, 0, 9)))
                    setev(("lgev", gi, tc), ve.n)
                on("vector", _v)

            def _vc(v):
                def vsb(inst):
                    ve.inc(inst)
                    v.wait_ge(ve.h, ve.n)

                def ts16(dst, src_of, scol, op):
                    for tc in range(NTC):
                        inst = v.tensor_scalar(dst[:, tc * 8:(tc + 1) * 8], src_of(tc),
                                               scol[:, tc:tc + 1], None, op)
                    vsb(inst)
                v.wait_ge(ve.h, ve.n)
                vsb(v.tensor_reduce(cm1[:], L8, axis=AX.X, op=ALU.max))
                ts16(m16a, lambda tc: lg_sb[:, tc * 9:tc * 9 + 8], cm1, ALU.is_equal)
                vsb(v.scalar_tensor_tensor(m16b[:], m16a[:], -BIG, iotaB16[:], ALU.mult, ALU.add))
                vsb(v.tensor_reduce(ci1[:], r3(m16b), axis=AX.X, op=ALU.min))
                ts16(m16a, lambda tc: iota16[:, tc * 8:(tc + 1) * 8], ci1, ALU.is_equal)
                vsb(v.scalar_tensor_tensor(r3(m16b), r3(m16a), -BIG, L8, ALU.mult, ALU.add))
                vsb(v.tensor_reduce(cm2[:], r3(m16b), axis=AX.X, op=ALU.max))
                ts16(m16c, lambda tc: m16b[:, tc * 8:(tc + 1) * 8], cm2, ALU.is_equal)
                vsb(v.scalar_tensor_tensor(m16c[:], m16c[:], -BIG, iotaB16[:], ALU.mult, ALU.add))
                vsb(v.tensor_reduce(ci2[:], r3(m16c), axis=AX.X, op=ALU.min))
                vsb(v.tensor_sub(cdc[:], cm1[:], cm2[:]))
                setev(("cmb1", gi), ve.n)
            on("vector", _vc)

            def _sg(s):
                s.wait_ge(ve.h, ev(("cmb1", gi)))
                s.activation(sg1c[:], cdc[:], AF.Sigmoid)
                se.inc(s.activation(sgsh[:], L3[:, :, 8], AF.Sigmoid))
                setev(("sig", gi), se.n)
            on("scalar", _sg)

            def _vc2(v):
                def vsb(inst):
                    ve.inc(inst)
                    v.wait_ge(ve.h, ve.n)

                def ts16(dst, src_of, scol, op):
                    for tc in range(NTC):
                        inst = v.tensor_scalar(dst[:, tc * 8:(tc + 1) * 8], src_of(tc),
                                               scol[:, tc:tc + 1], None, op)
                    vsb(inst)
                v.wait_ge(se.h, ev(("sig", gi)))
                ts16(m16b, lambda tc: iota16[:, tc * 8:(tc + 1) * 8], ci2, ALU.is_equal)
                vsb(v.tensor_scalar(c1a[:], sg1c[:], 0.5, None, ALU.mult))
                vsb(v.tensor_scalar(c2a[:], sg1c[:], -0.5, 0.5, ALU.mult, ALU.add))
                ts16(m16a, lambda tc: m16a[:, tc * 8:(tc + 1) * 8], c1a, ALU.mult)
                ts16(m16b, lambda tc: m16b[:, tc * 8:(tc + 1) * 8], c2a, ALU.mult)
                vsb(v.tensor_tensor(C3[:, :, 0:8], r3(m16a), r3(m16b), ALU.add))
                vsb(v.tensor_scalar(C3[:, :, 8], sgsh[:], 0.5, None, ALU.mult))
                setev(("cmb", gi), ve.n)
            on("vector", _vc2)

            for tc in range(NTC):
                def _t2(t, tc=tc):
                    bi = 4 + tc % 2
                    t.wait_ge(ve.h, ev(("cmb", gi)))
                    bank_claim(t, bi)
                    pe.inc(t.transpose(pb(bi, 0, 9, 0, 128), ctm[:, tc * 9:(tc + 1) * 9], eye_f[:]))
                    setev(("ctr", gi, tc), pe.n)
                    bank_release(bi, ve, ("ctev", gi, tc))
                on("tensor", _t2)

                def _v3(v, tc=tc):
                    bi = 4 + tc % 2
                    v.wait_ge(pe.h, ev(("ctr", gi, tc)))
                    v.wait_ge(dxc[tc % 2].h, ev(("crdma", gi, tc - 2)))
                    ve.inc(v.tensor_copy(c_t9[:, (tc % 2) * 128:(tc % 2 + 1) * 128],
                                         pb(bi, 0, 9, 0, 128)))
                    setev(("ctev", gi, tc), ve.n)
                on("vector", _v3)

                def _sy(sy, tc=tc):
                    sy.wait_ge(ve.h, ev(("ctev", gi, tc)))
                    sm = dxc[tc % 2]
                    sy.dma_start(
                        crows_d[:, tc * 128:(tc + 1) * 128],
                        c_t9[:, (tc % 2) * 128:(tc % 2 + 1) * 128]).then_inc(sm.h, 16)
                    sm.n += 16
                    setev(("crdma", gi, tc), sm.n)
                on("sync", _sy)

            barrier()

        def chunk_src(c, which):
            e, ic = c // ICH, c % ICH
            if which == "g":
                base = wg_h[e] if e < 8 else swg_h
            else:
                base = wu_h[e] if e < 8 else swu_h
            return base[:, ic * 128:(ic + 1) * 128].rearrange("(kc p) m -> p kc m", p=128)

        def wd_src(c):
            e, ic = c // ICH, c % ICH
            base = wd_h[e] if e < 8 else swd_h
            return base[ic * 128:(ic + 1) * 128, :]

        def prev_ft(c, ft, k=2):
            if ft >= k:
                return (c, ft - k)
            return (c - 1, ft + FT - k)

        def emit_ffn(fi, cb_sb, crow_sb, h_sb, wd_sb, gs_st, us_st):
            parts = []
            c0 = 0
            while c0 < NCH:
                parts.append((c0, min(PART, NCH - c0)))
                c0 += PART
            LA = 2
            setev(("ffnpre", fi), pe.n)

            def cbv(e, ft):
                return cb_sb[:, (e % 2) * T + ft * 512:(e % 2) * T + (ft + 1) * 512]

            def _syc(sy):
                sy.wait_ge(dxc[0].h, ev(("crdma", fi, NTC - 2)))
                sy.wait_ge(dxc[1].h, ev(("crdma", fi, NTC - 1)))
                for e in range(NEXP):
                    if e >= 2:
                        sy.wait_ge(pe.h, ev(("cbmm", fi, e - 2, FT - 1)))
                    sm = dxl[e % 2]
                    sy.dma_start(crow_sb[0:1, (e % 2) * T:(e % 2 + 1) * T],
                                 crows_d[e:e + 1, :]).then_inc(sm.h, 16)
                    sm.n += 16
                    setev(("crl", fi, e), sm.n)
            on("sync", _syc)

            def _g(g):
                part_of = {}
                for pi, (pc0, pn) in enumerate(parts):
                    part_of[pc0 + min(1, pn - 1)] = pi
                for c in range(NCH):
                    g.wait_ge(pe.h, ev(("gudone", fi, c - LA)) if c >= LA else ev(("ffnpre", fi)))
                    sm = dwg[c % 2]
                    g.dma_start(wga[:, (c % 2) * DM:(c % 2 + 1) * DM]
                                .rearrange("p (kc m) -> p kc m", kc=8),
                                chunk_src(c, "g")).then_inc(sm.h, 16)
                    g.dma_start(wua[:, (c % 2) * DM:(c % 2 + 1) * DM]
                                .rearrange("p (kc m) -> p kc m", kc=8),
                                chunk_src(c, "u")).then_inc(sm.h, 16)
                    sm.n += 32
                    setev(("gudma", fi, c), sm.n)
                    if c in part_of:
                        pi = part_of[c]
                        pc0, pn = parts[pi]
                        g.wait_ge(pe.h, ev(("ddone", fi, pi - 1)) if pi > 0 else ev(("ffnpre", fi)))
                        for s_ in range(pn):
                            g.dma_start(wd_sb[:, s_ * DM:(s_ + 1) * DM], wd_src(pc0 + s_)).then_inc(dwd.h, 16)
                            dwd.n += 16
                        setev(("wddma", fi, pi), dwd.n)
            on("gpsimd", _g)

            def _t(t):
                for pi, (pc0, pn) in enumerate(parts):
                    for s_ in range(pn):
                        c = pc0 + s_
                        e = c // ICH
                        if c % ICH == 0:
                            t.wait_ge(dxl[e % 2].h, ev(("crl", fi, e)))
                            for ft in range(FT):
                                bi = 4 + ft
                                bank_claim(t, bi)
                                inst = t.matmul(pb(bi), ones_b[:],
                                                crow_sb[0:1, (e % 2) * T + ft * 512:(e % 2) * T + (ft + 1) * 512],
                                                start=True, stop=True)
                                setev(("cbmm", fi, e, ft), pe.inc(inst))
                                bank_release(bi, ve, ("cbev", fi, e, ft))
                        t.wait_ge(dwg[c % 2].h, ev(("gudma", fi, c)))
                        for ft in range(FT):
                            gb = ft % 2
                            ub = 2 + ft % 2
                            bank_claim(t, gb)
                            for kc in range(NKC):
                                inst = t.matmul(pb(gb),
                                                wga[:, (c % 2) * DM + kc * 128:(c % 2) * DM + (kc + 1) * 128],
                                                xnv(kc, ft * 512, (ft + 1) * 512),
                                                start=(kc == 0), stop=(kc == NKC - 1))
                            setev(("gmm", fi, c, ft), pe.inc(inst))
                            bank_release(gb, se, ("silu", fi, c, ft))
                            bank_claim(t, ub)
                            for kc in range(NKC):
                                inst = t.matmul(pb(ub),
                                                wua[:, (c % 2) * DM + kc * 128:(c % 2) * DM + (kc + 1) * 128],
                                                xnv(kc, ft * 512, (ft + 1) * 512),
                                                start=(kc == 0), stop=(kc == NKC - 1))
                            setev(("umm", fi, c, ft), pe.inc(inst))
                            bank_release(ub, ve, ("usc", fi, c, ft))
                        setev(("gudone", fi, c), pe.n)
                    t.wait_ge(dwd.h, ev(("wddma", fi, pi)))
                    for tb in range(FT):
                        t.wait_ge(ve.h, ev(("hevq", fi, pi, tb)))
                        for og in range(4):
                            b0 = 4 + 2 * (og % 2)
                            bank_claim(t, b0)
                            bank_claim(t, b0 + 1)
                            for kc in range(pn):
                                for o2 in range(2):
                                    o = og * 2 + o2
                                    inst = t.matmul(pb(b0 + o2),
                                                    wd_sb[:, kc * DM + o * 128: kc * DM + (o + 1) * 128],
                                                    h_sb[:, kc * T + tb * 512: kc * T + (tb + 1) * 512],
                                                    start=(kc == 0), stop=(kc == pn - 1))
                            setev(("dmm", fi, pi, tb, og), pe.inc(inst))
                            bank_release(b0, ve, ("dev", fi, pi, tb, og))
                            bank_release(b0 + 1, ve, ("dev", fi, pi, tb, og))
                    setev(("ddone", fi, pi), pe.n)
            on("tensor", _t)

            def _s(s):
                for pi, (pc0, pn) in enumerate(parts):
                    for s_ in range(pn):
                        c = pc0 + s_
                        for ft in range(FT):
                            s.wait_ge(pe.h, ev(("gmm", fi, c, ft)))
                            s.wait_ge(ve.h, ev(("hev", fi) + prev_ft(c, ft)))
                            se.inc(s.activation(gs_st[:, (ft % 2) * 512:(ft % 2 + 1) * 512],
                                                pb(ft % 2), AF.Silu))
                            setev(("silu", fi, c, ft), se.n)
            on("scalar", _s)

            def _v(v):
                for pi, (pc0, pn) in enumerate(parts):
                    for s_ in range(pn):
                        c = pc0 + s_
                        e = c // ICH
                        if c % ICH == 0:
                            for ft in range(FT):
                                v.wait_ge(pe.h, ev(("cbmm", fi, e, ft)))
                                ve.inc(v.tensor_copy(cbv(e, ft), pb(4 + ft)))
                                setev(("cbev", fi, e, ft), ve.n)
                            v.wait_ge(ve.h, ve.n)
                        for ft in range(FT):
                            v.wait_ge(pe.h, ev(("umm", fi, c, ft)))
                            ve.inc(v.tensor_mul(us_st[:, (ft % 2) * 512:(ft % 2 + 1) * 512],
                                                pb(2 + ft % 2),
                                                cbv(e, ft)))
                            setev(("usc", fi, c, ft), ve.n)
                            v.wait_ge(ve.h, ve.n)
                            v.wait_ge(se.h, ev(("silu", fi, c, ft)))
                            ve.inc(v.tensor_mul(h_sb[:, s_ * T + ft * 512: s_ * T + (ft + 1) * 512],
                                                gs_st[:, (ft % 2) * 512:(ft % 2 + 1) * 512],
                                                us_st[:, (ft % 2) * 512:(ft % 2 + 1) * 512]))
                            setev(("hev", fi, c, ft), ve.n)
                            if s_ == pn - 1:
                                v.wait_ge(ve.h, ve.n)
                                ve.inc(v.tensor_copy(c1a[:, 0:2],
                                                     h_sb[:, s_ * T + ft * 512 + 510:
                                                          s_ * T + ft * 512 + 512]))
                                setev(("hevq", fi, pi, ft), ve.n)
                    for tb in range(FT):
                        for og in range(4):
                            b0 = 4 + 2 * (og % 2)
                            v.wait_ge(pe.h, ev(("dmm", fi, pi, tb, og)))
                            for o2 in range(2):
                                o = og * 2 + o2
                                inst = v.tensor_add(xv(o, tb * 512, (tb + 1) * 512),
                                                    xv(o, tb * 512, (tb + 1) * 512),
                                                    pb(b0 + o2))
                            ve.inc(inst)
                            setev(("dev", fi, pi, tb, og), ve.n)
            on("vector", _v)
            barrier()

        def emit_attention(AT, n_heads=NH, proj_only=False):
            (k_ext2, v_t, mk_rhse, q_sb, v_fm, zb_sb, att_st, p_ts, vfm_close) = AT
            rhs_ext2 = None
            NQT, NJC, NIC = 16, 16, 4
            setev("attpre_pe", pe.n)
            setev("attpre_ve", ve.n)

            def _g(g):
                g.wait_ge(ve.h, ev("attpre_ve"))
                g.wait_ge(pe.h, ev("attpre_pe"))
                g.memset(v_t[:], 0.0)
                g.memset(v_t[:].rearrange("p (j c) -> p j c", c=128)[:, :, 64:65], 1.0)
                for kv in range(NKV):
                    g.memset(k_ext2[kv][64:128, :], 0.0)
                for kv in range(NKV):
                    g.memset(k_ext2[kv][64:65, :], 1.0)
                ge.inc(g.memset(m_bf[:], 0.0))
                setev("attc", ge.n)
                for mq in range(12):
                    g.wait_ge(pe.h, ev(("pmm", mq - 2, FT - 1)) if mq >= 2 else ev("attpre_pe"))
                    src = (wq_h[:, mq * 128:(mq + 1) * 128] if mq < 8
                           else wkv_h[:, (mq - 8) * 128:(mq - 7) * 128])
                    sm = dwg[mq % 2]
                    g.dma_start(wga[:, (mq % 2) * DM:(mq % 2 + 1) * DM]
                                .rearrange("p (kc m) -> p kc m", kc=8),
                                src.rearrange("(kc p) m -> p kc m", p=128)).then_inc(sm.h, 16)
                    sm.n += 16
                    setev(("wdma", mq), sm.n)
            on("gpsimd", _g)

            def _t(t):
                t.wait_ge(ge.h, ev("attc"))
                for mq in range(12):
                    t.wait_ge(dwg[mq % 2].h, ev(("wdma", mq)))
                    for ft in range(FT):
                        bi = ft % 4
                        bank_claim(t, bi)
                        for kc in range(NKC):
                            inst = t.matmul(pb(bi),
                                            wga[:, (mq % 2) * DM + kc * 128:(mq % 2) * DM + (kc + 1) * 128],
                                            xnv(kc, ft * 512, (ft + 1) * 512),
                                            start=(kc == 0), stop=(kc == NKC - 1))
                        setev(("pmm", mq, ft), pe.inc(inst))
                        bank_release(bi, ve, ("pev", mq, ft))
            on("tensor", _t)

            def _v(v):
                for mq in range(12):
                    for ft in range(FT):
                        bi = ft % 4
                        v.wait_ge(pe.h, ev(("pmm", mq, ft)))
                        if mq < 8:
                            inst = v.tensor_scalar(q_sb[:, mq * T + ft * 512: mq * T + (ft + 1) * 512],
                                                   pb(bi), ks_col[:, mq:mq + 1], None, ALU.mult)
                        elif mq < 10:
                            kk = mq - 8
                            v.tensor_copy(k_ext2[2 * kk][0:64, ft * 512:(ft + 1) * 512], pb(bi, 0, 64))
                            inst = v.tensor_copy(k_ext2[2 * kk + 1][0:64, ft * 512:(ft + 1) * 512],
                                                 pb(bi, 64, 128))
                        else:
                            vv = mq - 10
                            inst = v.tensor_copy(v_fm[:, vv * T + ft * 512: vv * T + (ft + 1) * 512],
                                                 pb(bi))
                        ve.inc(inst)
                        setev(("pev", mq, ft), ve.n)
            on("vector", _v)

            def _t2(t):
                for idx in range(32):
                    vv, jc = idx // 16, idx % 16
                    t.wait_ge(ve.h, ev(("pev", 10 + vv, (jc * 128) // 512)))
                    bank_claim(t, 6)
                    pe.inc(t.transpose(pb(6, 0, 128, 0, 64).bitcast(BF16),
                                       v_fm[:, vv * T + jc * 128: vv * T + (jc + 1) * 128],
                                       eye_b[:]))
                    setev(("vt", idx), pe.n)
                    bank_release(6, ve, ("vtev", idx))
            on("tensor", _t2)

            def _v2(v):
                for idx in range(32):
                    vv, jc = idx // 16, idx % 16
                    v.wait_ge(pe.h, ev(("vt", idx)))
                    src = pb(6, 0, 128, 0, 64).bitcast(BF16)
                    for hh in range(2):
                        kvh = 2 * vv + hh
                        inst = v.tensor_copy(v_t[:, kvh * 2048 + jc * 128: kvh * 2048 + jc * 128 + 64],
                                             src[:, hh * 64:(hh + 1) * 64])
                    ve.inc(inst)
                    setev(("vtev", idx), ve.n)
            on("vector", _v2)
            barrier()
            vfm_close()
            rhs_ext2 = mk_rhse()

            def prev_avp(h, ic, jp):
                if jp >= 2:
                    return ("avmm", h, ic, (jp - 2) * 2 + 1)
                if ic >= 1:
                    return ("avmm", h, ic - 1, (6 + jp) * 2 + 1)
                if h >= 1:
                    return ("avmm", h - 1, NIC - 1, (6 + jp) * 2 + 1)
                return None

            def head_passA_qt(h, qt):
                kv = h // 4

                if qt == 0:
                    def _vq(v):
                        if h >= 2:
                            v.wait_ge(pe.h, ev(("bdone", h - 2)))
                        v.wait_ge(ve.h, ve.n)
                        ve.inc(v.tensor_copy(rhs_ext2[h % 2][0:64, :],
                                             q_sb[(h % 2) * 64:(h % 2) * 64 + 64,
                                                  (h // 2) * T:(h // 2 + 1) * T]))
                        v.wait_ge(ve.h, ve.n)
                        ve.inc(v.memset(rhs_ext2[h % 2][64:65, :] if h >= 2
                                        else rhs_ext2[h % 2][64:128, :], 0.0))
                        setev(("qcp", h), ve.n)
                    on("vector", _vq)

                def _t(t):
                    if qt == 0:
                        t.wait_ge(ve.h, ev(("qcp", h)))
                    for jj in range(FT):
                        bi = jj % 2
                        bank_claim(t, bi)
                        inst = t.matmul(pb(bi),
                                        rhs_ext2[h % 2][0:128, qt * 128:(qt + 1) * 128],
                                        k_ext2[kv][0:128, jj * 512:(jj + 1) * 512],
                                        start=True, stop=True)
                        setev(("amm", h, qt, jj), pe.inc(inst))
                        bank_release(bi, ve, ("amax", h, qt, jj // 2))
                on("tensor", _t)

                def _v(v):
                    if qt == 0 and h >= 2:
                        v.wait_ge(pe.h, ev(("mt", h - 2)))
                    for jp in range(2):
                        v.wait_ge(pe.h, ev(("amm", h, qt, jp * 2 + 1)))
                        ve.inc(v.tensor_reduce(mm4[:, jp:jp + 1], psA[:, 0:1024],
                                               axis=AX.X, op=ALU.max))
                        setev(("amax", h, qt, jp), ve.n)
                    v.wait_ge(ve.h, ve.n)
                    ve.inc(v.tensor_reduce(m_bf[:, (h % 2) * 16 + qt:(h % 2) * 16 + qt + 1],
                                           mm4[:, 0:2], axis=AX.X, op=ALU.max, negate=True))
                    v.wait_ge(ve.h, ve.n)
                    setev(("mq", h, qt), ve.n)
                on("vector", _v)

            def head_mrow(h):
                def _t(t):
                    t.wait_ge(ve.h, ev(("mq", h, 15)))
                    bank_claim(t, 6)
                    pe.inc(t.transpose(pb(6, 0, 16, 0, 64).bitcast(BF16),
                                       m_bf[:, (h % 2) * 16:(h % 2) * 16 + 16], eye_b[:]))
                    setev(("mt", h), pe.n)
                    bank_release(6, ve, ("mtev", h))
                on("tensor", _t)

                def _v(v):
                    v.wait_ge(pe.h, ev(("mt", h)))
                    if h >= 2:
                        v.wait_ge(dxm[h % 2].h, ev(("mrow", h - 2)))
                    ve.inc(v.tensor_copy(m_t[:, (h % 2) * 128:(h % 2) * 128 + 128],
                                         pb(6, 0, 16, 0, 64).bitcast(BF16)))
                    setev(("mtev", h), ve.n)
                on("vector", _v)

                def _sy(sy):
                    sy.wait_ge(ve.h, ev(("mtev", h)))
                    sm = dxm[h % 2]
                    sy.dma_start(rhs_ext2[h % 2][64:65, :].rearrange("one (p j) -> one p j", p=16),
                                 m_t[:, (h % 2) * 128:(h % 2) * 128 + 128]).then_inc(sm.h, 16)
                    sm.n += 16
                    setev(("mrow", h), sm.n)
                on("sync", _sy)

            def head_passB(h):
                kv = h // 4

                def _t(t, ic, q4):
                    if ic == 0 and q4 == 0:
                        t.wait_ge(dxm[h % 2].h, ev(("mrow", h)))
                    av = 4 + ic % 2
                    if q4 == 0:
                        bank_claim(t, av)
                    for jp in range(q4 * 2, q4 * 2 + 2):
                        bank_claim(t, 2)
                        bank_claim(t, 3)
                        for j2 in range(2):
                            jc = jp * 2 + j2
                            inst = t.matmul(pb(2 + j2), k_ext2[kv][0:128, jc * 128:(jc + 1) * 128],
                                            rhs_ext2[h % 2][0:128, ic * 512:(ic + 1) * 512],
                                            start=True, stop=True)
                            setev(("st", h, ic, jc), pe.inc(inst))
                        bank_release(2, se, ("exp", h, ic, jp))
                        bank_release(3, se, ("exp", h, ic, jp))
                        if jp >= 1:
                            t.wait_ge(se.h, ev(("exp", h, ic, jp - 1)))
                            for j2 in range(2):
                                jc = (jp - 1) * 2 + j2
                                inst = t.matmul(pb(av, 0, 65),
                                                v_t[:, kv * 2048 + jc * 128: kv * 2048 + jc * 128 + 65],
                                                p_ts[(jp - 1) % 2][:, j2 * 512:(j2 + 1) * 512],
                                                start=(jc == 0), stop=False)
                                setev(("avmm", h, ic, jc), pe.inc(inst))
                    if q4 == 3:
                        jp = 7
                        t.wait_ge(se.h, ev(("exp", h, ic, jp)))
                        for j2 in range(2):
                            jc = jp * 2 + j2
                            inst = t.matmul(pb(av, 0, 65),
                                            v_t[:, kv * 2048 + jc * 128: kv * 2048 + jc * 128 + 65],
                                            p_ts[jp % 2][:, j2 * 512:(j2 + 1) * 512],
                                            start=False, stop=(j2 == 1))
                            setev(("avmm", h, ic, jc), pe.inc(inst))
                        setev(("av", h, ic), pe.n)
                        bank_release(av, ve, ("omul", h, ic))

                def _s(s, ic):
                    for jp in range(8):
                        s.wait_ge(pe.h, ev(("st", h, ic, jp * 2 + 1)))
                        pv = prev_avp(h, ic, jp)
                        if pv is not None:
                            s.wait_ge(pe.h, ev(pv))
                        se.inc(s.activation(p_ts[jp % 2][:], psA[:, 1024:2048], AF.Exp))
                        setev(("exp", h, ic, jp), se.n)
                    s.wait_ge(pe.h, ev(("av", h, ic)))
                    se.inc(s.activation(zr[:], pb(4 + ic % 2, 64, 65), AF.Ln))
                    s.wait_ge(se.h, se.n)
                    if ic >= 1:
                        s.wait_ge(pe.h, ev(("zbmm", h, ic - 1)))
                    elif h >= 1:
                        s.wait_ge(pe.h, ev(("zbmm", h - 1, NIC - 1)))
                    se.inc(s.activation(zrb[:], zr[:], AF.Exp, bias=0.0, scale=-1.0))
                    s.wait_ge(se.h, se.n)
                    setev(("zrb", h, ic), se.n)

                def _v(v, ic):
                    pass

                def _t2(t, ic):
                    t.wait_ge(se.h, ev(("zrb", h, ic)))
                    bank_claim(t, 7)
                    pe.inc(t.matmul(pb(7, 0, 64), ones_b[0:1, 0:64], zrb[:],
                                    start=True, stop=True))
                    setev(("zbmm", h, ic), pe.n)
                    bank_release(7, ve, ("zbcp", h, ic))

                def _v2(v, ic):
                    av = 4 + ic % 2
                    v.wait_ge(pe.h, ev(("zbmm", h, ic)))
                    ve.inc(v.tensor_copy(zb_sb[:], pb(7, 0, 64)))
                    setev(("zbcp", h, ic), ve.n)
                    v.wait_ge(ve.h, ve.n)
                    ve.inc(v.tensor_mul(att_st[0:64, :], pb(av, 0, 64), zb_sb[:]))
                    setev(("omul", h, ic), ve.n)
                    v.wait_ge(ve.h, ve.n)
                    half = att_st[0:64, :]
                    if h % 2 == 1:
                        ve.inc(v.tensor_copy(att_st[64:128, :], att_st[0:64, :]))
                        v.wait_ge(ve.h, ve.n)
                        half = att_st[64:128, :]
                    ve.inc(v.tensor_add(
                        x_sb[(h % 2) * 64:(h % 2) * 64 + 64,
                             (h // 2) * T + ic * 512:(h // 2) * T + (ic + 1) * 512],
                        x_sb[(h % 2) * 64:(h % 2) * 64 + 64,
                             (h // 2) * T + ic * 512:(h // 2) * T + (ic + 1) * 512],
                        half))
                    setev(("oev", h, ic), ve.n)

                return _t, _t2, _s, _v, _v2

            if proj_only:
                return

            def emit_B_unit(h, u, fns):
                _t, _t2, _s, _v, _v2 = fns
                ic, q4 = u // 4, u % 4
                if u == 0:
                    def _sall(s_):
                        for ic2 in range(NIC):
                            _s(s_, ic2)
                    on("scalar", _sall)
                on("tensor", lambda t: _t(t, ic, q4))
                if q4 == 3:
                    if ic >= 1:
                        on("tensor", lambda t: _t2(t, ic - 1))
                        on("vector", lambda v: _v2(v, ic - 1))
                    if ic == NIC - 1:
                        on("tensor", lambda t: _t2(t, NIC - 1))
                        on("vector", lambda v: _v2(v, NIC - 1))
                        setev(("bdone", h), snap[("av", h, NIC - 1)])

            bfns = {}
            for h in range(n_heads + 1):
                for u in range(16):
                    if h < n_heads:
                        head_passA_qt(h, u)
                    if h >= 1:
                        emit_B_unit(h - 1, u, bfns[h - 1])
                if h < n_heads:
                    head_mrow(h)
                    bfns[h] = head_passB(h)
            barrier()

        def emit_epilogue():
            def _t(t):
                for tc in range(NTC):
                    for kc in range(NKC):
                        bi = kc % 4
                        bank_claim(t, bi)
                        pe.inc(t.transpose(pb(bi, 0, 128, 0, 128),
                                           xv(kc, tc * 128, (tc + 1) * 128), eye_f[:]))
                        setev(("ytr", tc, kc), pe.n)
                        bank_release(bi, ve, ("yevk", tc, kc))
            on("tensor", _t)

            def _v(v):
                for tc in range(NTC):
                    v.wait_ge(dxx[tc % 2].h, ev(("yst", tc - 2)))
                    for kc in range(NKC):
                        bi = kc % 4
                        v.wait_ge(pe.h, ev(("ytr", tc, kc)))
                        ve.inc(v.tensor_copy(stage[:, (tc % 2) * DM + kc * 128:(tc % 2) * DM + (kc + 1) * 128],
                                             pb(bi, 0, 128, 0, 128)))
                        setev(("yevk", tc, kc), ve.n)
                    setev(("yev", tc), ve.n)
            on("vector", _v)

            def _sy(sy):
                for tc in range(NTC):
                    sy.wait_ge(ve.h, ev(("yev", tc)))
                    sm = dxx[tc % 2]
                    sy.dma_start(y_h[tc * 128:(tc + 1) * 128, :],
                                 stage[:, (tc % 2) * DM:(tc % 2 + 1) * DM]).then_inc(sm.h, 16)
                    sm.n += 16
                    setev(("yst", tc), sm.n)
                sy.wait_ge(dxx[0].h, dxx[0].n)
                sy.wait_ge(dxx[1].h, dxx[1].n)
                sy.wait_ge(dx.h, dx.n)
            on("sync", _sy)

        def emit_all():
            bankuse.clear()
            with ExitStack() as pst:
                pro_box["ks_st"] = scoped(pst, "ks_st", [8, 128], F32)
                pro_box["gsw_sb"] = scoped(pst, "gsw_sb", [9, DM], F32)
                emit_prologue()
            emit_ln(0)
            if upto == "ln1":
                return
            with ExitStack() as fst:
                emit_gating(0)
                if upto == "gate1":
                    return
                cb_sb = scoped(fst, "cb1", [128, 2 * T], BF16)
                crow_sb = scoped(fst, "crow1", [1, 2 * T], BF16)
                h_sb = scoped(fst, "h1", [128, PART * T], BF16)
                wd_sb = scoped(fst, "wd1", [128, PART * DM], BF16)
                gs_st = scoped(fst, "gs1", [128, 2 * 512], BF16)
                us_st = scoped(fst, "us1", [128, 2 * 512], BF16)
                emit_ffn(0, cb_sb, crow_sb, h_sb, wd_sb, gs_st, us_st)
            if upto == "ffn1":
                return
            emit_ln(1)
            if upto == "ln2":
                return
            with ExitStack() as ast_:
                def asb(n_, s_, d_):
                    return scoped(ast_, n_, s_, d_)
                k_ext2 = [asb(f"kext{i}", [128, T], BF16) for i in range(NKV)]
                v_t = asb("v_t", [128, NKV * 16 * 128], BF16)
                q_sb = asb("q_sb", [128, NKC * T], BF16)
                vf_st = ExitStack()
                v_fm = TEN["v_fm"] = vf_st.enter_context(
                    nc.sbuf_tensor(f"v_fm_p{ctx['pass']}", [128, 2 * T], BF16))

                def mk_rhse():
                    return [asb(f"rhse{i}", [128, T], BF16) for i in range(2)]
                att_st = stage[:, 0:512]
                zb_sb = stage[0:64, 512:1024]
                p_t = stage[:, 1024:1536].bitcast(BF16)
                p_t2 = stage[:, 1536:2048].bitcast(BF16)
                if upto.startswith("attn") and upto != "attn":
                    nhh = 0 if upto == "attn0" else int(upto[4:])
                    emit_attention((k_ext2, v_t, mk_rhse, q_sb, v_fm, zb_sb, att_st, (p_t, p_t2),
                                    vf_st.close),
                                   n_heads=max(nhh, 1), proj_only=(nhh == 0))
                    return
                emit_attention((k_ext2, v_t, mk_rhse, q_sb, v_fm, zb_sb, att_st, (p_t, p_t2), vf_st.close))
            if upto == "attn":
                return
            emit_ln(2)
            if upto == "ln3":
                return
            with ExitStack() as fst:
                emit_gating(1)
                cb_sb = scoped(fst, "cb2", [128, 2 * T], BF16)
                crow_sb = scoped(fst, "crow2", [1, 2 * T], BF16)
                h_sb = scoped(fst, "h2", [128, PART * T], BF16)
                wd_sb = scoped(fst, "wd2", [128, PART * DM], BF16)
                gs_st = scoped(fst, "gs2", [128, 2 * 512], BF16)
                us_st = scoped(fst, "us2", [128, 2 * 512], BF16)
                emit_ffn(1, cb_sb, crow_sb, h_sb, wd_sb, gs_st, us_st)

        def emit_taps():
            if not taps:
                return

            def _sy(sy):
                sy.wait_ge(ve.h, ve.n)
                sy.wait_ge(se.h, se.n)
                sy.wait_ge(pe.h, pe.n)
                sy.wait_ge(ge.h, ge.n)
                for tn, shape, dt in taps:
                    src = crows_d if tn == "crows_d" else TEN[tn][0:shape[0], 0:shape[1]]
                    sy.dma_start(tap_out[tn], src).then_inc(dx.h, 16)
                    dx.n += 16
                sy.wait_ge(dx.h, dx.n)
            on("sync", _sy)

        ctx["real"] = False
        ctx["pass"] = 1
        emit_all()
        if upto == "all":
            emit_epilogue()
        emit_taps()
        pass1 = [s_.n for s_ in ALLSC]
        for s_ in ALLSC:
            s_.n = 0
        ctx["real"] = True
        ctx["pass"] = 2
        emit_all()
        if upto == "all":
            emit_epilogue()
        emit_taps()
        pass2 = [s_.n for s_ in ALLSC]
        assert pass1 == pass2, (pass1, pass2)
    return nc


def _in_maps(L0_attn_x, L0_attn_Wq, L0_attn_Wkv, L0_attn_k_scale,
             L0_ffn_gate_w, L0_ffn_w_gate, L0_ffn_w_up, L0_ffn_w_down,
             L0_ffn_sw_gate, L0_ffn_sw_up, L0_ffn_sw_down, L0_ffn_shared_gate_w):
    shared = {
        "wq": np.ascontiguousarray(L0_attn_Wq, np.float32),
        "wkv": np.ascontiguousarray(L0_attn_Wkv, np.float32),
        "ks": np.ascontiguousarray(np.asarray(L0_attn_k_scale, np.float32).reshape(8, 128)),
        "gw": np.ascontiguousarray(L0_ffn_gate_w, np.float32),
        "sgw": np.ascontiguousarray(np.asarray(L0_ffn_shared_gate_w, np.float32).reshape(1, DM)),
        "wg": np.ascontiguousarray(L0_ffn_w_gate, np.float32),
        "wu": np.ascontiguousarray(L0_ffn_w_up, np.float32),
        "wd": np.ascontiguousarray(L0_ffn_w_down, np.float32),
        "swg": np.ascontiguousarray(L0_ffn_sw_gate, np.float32),
        "swu": np.ascontiguousarray(L0_ffn_sw_up, np.float32),
        "swd": np.ascontiguousarray(L0_ffn_sw_down, np.float32),
    }
    x = np.asarray(L0_attn_x, np.float32)
    return [dict(shared, x=np.ascontiguousarray(x[c])) for c in range(8)]


def kernel(**inputs):
    nc = build_program()
    in_maps = _in_maps(**inputs)
    res = run_bass_kernel_spmd(nc, in_maps, core_ids=list(range(8)))
    return np.stack([res.results[c]["y"] for c in range(8)], axis=0)


# version 12
# speedup vs baseline: 1.3033x; 1.0877x over previous
import numpy as np
import concourse.bass as bass
import concourse.mybir as mybir
from concourse.bass_utils import run_bass_kernel_spmd
from contextlib import ExitStack

F32 = mybir.dt.float32
BF16 = mybir.dt.bfloat16
AF = mybir.ActivationFunctionType
ALU = mybir.AluOpType
AX = mybir.AxisListType

T = 2048
NKC = 8
DM = 1024
NCH = 99
ICH = 11
NEXP = 9
NH = 16
NKV = 4
BIG = 30000.0
PART = 6
FT = 4
NTC = 16


class FakeInst:
    def then_inc(self, sem, k):
        return self


_FI = FakeInst()


class FakeEng:
    def wait_ge(self, *a, **k):
        pass

    def __getattr__(self, name):
        def op(*a, **k):
            return _FI
        return op


_FE = FakeEng()


class SC:
    __slots__ = ("h", "n")

    def __init__(self, h):
        self.h = h
        self.n = 0

    def inc(self, inst, k=1):
        inst.then_inc(self.h, k)
        self.n += k
        return self.n


def build_program(upto="all", taps=()):
    nc = bass.Bass("TRN2", target_bir_lowering=False)
    x_h = nc.dram_tensor("x", [T, DM], F32, kind="ExternalInput").ap()
    wq_h = nc.dram_tensor("wq", [DM, DM], F32, kind="ExternalInput").ap()
    wkv_h = nc.dram_tensor("wkv", [DM, 512], F32, kind="ExternalInput").ap()
    ks_h = nc.dram_tensor("ks", [8, 128], F32, kind="ExternalInput").ap()
    gw_h = nc.dram_tensor("gw", [8, DM], F32, kind="ExternalInput").ap()
    sgw_h = nc.dram_tensor("sgw", [1, DM], F32, kind="ExternalInput").ap()
    wg_h = nc.dram_tensor("wg", [8, DM, 1408], F32, kind="ExternalInput").ap()
    wu_h = nc.dram_tensor("wu", [8, DM, 1408], F32, kind="ExternalInput").ap()
    wd_h = nc.dram_tensor("wd", [8, 1408, DM], F32, kind="ExternalInput").ap()
    swg_h = nc.dram_tensor("swg", [DM, 1408], F32, kind="ExternalInput").ap()
    swu_h = nc.dram_tensor("swu", [DM, 1408], F32, kind="ExternalInput").ap()
    swd_h = nc.dram_tensor("swd", [1408, DM], F32, kind="ExternalInput").ap()
    y_h = nc.dram_tensor("y", [T, DM], F32, kind="ExternalOutput").ap()
    crows_d = nc.dram_tensor("crows_d", [NEXP, T], BF16, kind="Internal").ap()
    tap_out = {}
    for tn, shape, dt in taps:
        tap_out[tn] = nc.dram_tensor("tap_" + tn, shape, dt, kind="ExternalOutput").ap()

    snap = {}
    TEN = {}

    with ExitStack() as st:
        def sb(n_, shape, dt):
            h = st.enter_context(nc.sbuf_tensor(n_, shape, dt))
            TEN[n_] = h
            return h

        x_sb = sb("x_sb", [128, NKC * T], F32)
        xn_sb = sb("xn_sb", [128, NKC * T], BF16)
        stage = sb("stage", [128, 2 * DM], F32)
        wga = sb("wga", [128, 2 * DM], BF16)
        wua = sb("wua", [128, 2 * DM], BF16)
        eye_b = sb("eye_b", [128, 128], BF16)
        eye_f = sb("eye_f", [128, 128], F32)
        ones_b = sb("ones_b", [1, 128], BF16)
        ones_fc = sb("ones_fc", [128, 1], F32)
        ones_fr = sb("ones_fr", [1, 128], F32)
        ks_col = sb("ks_col", [128, 8], F32)
        pro_box = {}
        gwT = sb("gwT", [128, NKC * 9], BF16)
        epsc = sb("epsc", [1, 1], F32)
        lg_sb = sb("lg_sb", [128, NTC * 9], F32)
        ctm = sb("ctm", [128, NTC * 9], F32)
        cm1 = sb("cm1", [128, NTC], F32)
        cm2 = sb("cm2", [128, NTC], F32)
        ci1 = sb("ci1", [128, NTC], F32)
        ci2 = sb("ci2", [128, NTC], F32)
        cdc = sb("cdc", [128, NTC], F32)
        sg1c = sb("sg1c", [128, NTC], F32)
        sgsh = sb("sgsh", [128, NTC], F32)
        c1a = sb("c1a", [128, NTC], F32)
        c2a = sb("c2a", [128, NTC], F32)
        m16a = stage[:, 0:128]
        m16b = stage[:, 128:256]
        m16c = stage[:, 256:384]
        iota16 = sb("iota16", [128, 128], F32)
        iotaB16 = sb("iotaB16", [128, 128], F32)
        c_t9 = sb("c_t9", [9, 2 * 128], BF16)
        rowp = sb("rowp", [1, 2 * T], F32)
        r_mean = rowp[0:1, 0:T]
        r_tmp = rowp[0:1, T:2 * T]
        zr = rowp[0:1, 0:512]
        zrb = sb("zrb", [1, 512], BF16)
        m_bf = sb("m_bf", [128, 2 * 16], BF16)
        m_t = sb("m_t", [16, 2 * 128], BF16)
        mm4 = sb("mm4", [128, 4], F32)

        psA = st.enter_context(nc.psum_tensor("psA", [128, 2048], F32))
        psB = st.enter_context(nc.psum_tensor("psB", [128, 2048], F32))

        def pb(i, p0=0, p1=128, c0=0, c1=512):
            t_ = psA if i < 4 else psB
            base = (i % 4) * 512
            return t_[p0:p1, base + c0:base + c1]

        dw = SC(st.enter_context(nc.semaphore("dw")))
        dx = SC(st.enter_context(nc.semaphore("dx")))
        dwg = [SC(st.enter_context(nc.semaphore(f"dwg{i}"))) for i in range(2)]
        dwd = SC(st.enter_context(nc.semaphore("dwd")))
        dxx = [SC(st.enter_context(nc.semaphore(f"dxx{i}"))) for i in range(2)]
        dxc = [SC(st.enter_context(nc.semaphore(f"dxc{i}"))) for i in range(2)]
        dxl = [SC(st.enter_context(nc.semaphore(f"dxl{i}"))) for i in range(2)]
        dxm = [SC(st.enter_context(nc.semaphore(f"dxm{i}"))) for i in range(2)]
        pe = SC(st.enter_context(nc.semaphore("pe")))
        ve = SC(st.enter_context(nc.semaphore("ve")))
        se = SC(st.enter_context(nc.semaphore("se")))
        ge = SC(st.enter_context(nc.semaphore("ge")))
        ALLSC = [dw, dx, pe, ve, se, ge, dwd] + dwg + dxx + dxc + dxl + dxm

        blk = st.enter_context(nc.Block())

        ctx = {"real": False, "pass": 0}
        bankuse = {}

        def scoped(stk, name, shape, dt):
            h = stk.enter_context(nc.sbuf_tensor(f"{name}_p{ctx['pass']}", shape, dt))
            TEN[name] = h
            return h

        def on(ename, f):
            if ctx["real"]:
                getattr(blk, ename)(f)
            else:
                f(_FE)

        def ev(name):
            return snap.get(name, 0)

        def setev(name, val):
            snap[name] = val

        def bank_claim(eng, bi):
            if bi in bankuse:
                sem, name = bankuse[bi]
                eng.wait_ge(sem.h, ev(name))

        def bank_release(bi, sem, name):
            bankuse[bi] = (sem, name)

        def barrier():
            pev, vev, sev = pe.n, ve.n, se.n

            def _t(t):
                t.wait_ge(ve.h, vev)
                t.wait_ge(se.h, sev)
            on("tensor", _t)

            def _v(v):
                v.wait_ge(pe.h, pev)
                v.wait_ge(se.h, sev)
            on("vector", _v)

            def _s(s):
                s.wait_ge(pe.h, pev)
                s.wait_ge(ve.h, vev)
            on("scalar", _s)

        def xv(kc, f0, f1):
            return x_sb[:, kc * T + f0: kc * T + f1]

        def xnv(kc, f0, f1):
            return xn_sb[:, kc * T + f0: kc * T + f1]

        def emit_prologue():
            ks_st = pro_box["ks_st"]
            gsw_sb = pro_box["gsw_sb"]
            def _g(g):
                ge.inc(g.memset(eye_b[:], 0.0))
                ge.inc(g.memset(eye_f[:], 0.0))
                g.wait_ge(ge.h, ge.n)
                g.affine_select(out=eye_b[:], in_=eye_b[:], compare_op=ALU.not_equal,
                                fill=1.0, base=0, pattern=[[-1, 128]], channel_multiplier=1)
                g.affine_select(out=eye_f[:], in_=eye_f[:], compare_op=ALU.not_equal,
                                fill=1.0, base=0, pattern=[[-1, 128]], channel_multiplier=1)
                g.memset(ones_b[:], 1.0)
                g.memset(ones_fc[:], 1.0)
                g.memset(ones_fr[:], 1.0)
                g.memset(epsc[:], 1e-5)
                g.iota(iota16[:].rearrange("p (tc e) -> p tc e", e=8),
                       pattern=[[0, 16], [1, 8]], base=0, channel_multiplier=0,
                       allow_small_or_imprecise_dtypes=True)
                ge.inc(g.iota(iotaB16[:].rearrange("p (tc e) -> p tc e", e=8),
                              pattern=[[0, 16], [1, 8]], base=int(BIG), channel_multiplier=0,
                              allow_small_or_imprecise_dtypes=True))
                setev("consts", ge.n)
            on("gpsimd", _g)

            def _sy(sy):
                sy.dma_start(ks_st[:], ks_h).then_inc(dx.h, 16)
                sy.dma_start(gsw_sb[0:8, :], gw_h).then_inc(dx.h, 16)
                sy.dma_start(gsw_sb[8:9, :], sgw_h).then_inc(dx.h, 16)
                dx.n += 48
                setev("hdr_dx", dx.n)
                for tc in range(NTC):
                    sy.wait_ge(pe.h, ev(("xtr", tc - 2, NKC - 1)))
                    sm = dxx[tc % 2]
                    sy.dma_start(stage[:, (tc % 2) * DM:(tc % 2 + 1) * DM],
                                 x_h[tc * 128:(tc + 1) * 128, :]).then_inc(sm.h, 16)
                    sm.n += 16
                    setev(("xdma", tc), sm.n)
            on("sync", _sy)

            def _t(t):
                t.wait_ge(ge.h, ev("consts"))
                for tc in range(NTC):
                    t.wait_ge(dxx[tc % 2].h, ev(("xdma", tc)))
                    for kc in range(NKC):
                        bi = kc % 4
                        bank_claim(t, bi)
                        pe.inc(t.transpose(pb(bi, 0, 128, 0, 128),
                                           stage[:, (tc % 2) * DM + kc * 128:(tc % 2) * DM + (kc + 1) * 128],
                                           eye_f[:]))
                        setev(("xtr", tc, kc), pe.n)
                        bank_release(bi, ve, ("xev", tc, kc))
                t.wait_ge(dx.h, ev("hdr_dx"))
                bank_claim(t, 4)
                pe.inc(t.transpose(pb(4, 0, 128, 0, 8), ks_st[:], eye_f[0:8, 0:8]))
                setev("kstr", pe.n)
                bank_release(4, ve, "ksev")
                bank_claim(t, 5)
                for kc in range(NKC):
                    t.wait_ge(ve.h, ev(("gwev", kc - 1)))
                    pe.inc(t.transpose(pb(5, 0, 128, kc * 9, kc * 9 + 9),
                                       gsw_sb[:, kc * 128:(kc + 1) * 128], eye_f[0:9, 0:9]))
                    setev(("gwtr", kc), pe.n)
                bank_release(5, ve, ("gwev", NKC - 1))
            on("tensor", _t)

            def _v(v):
                for tc in range(NTC):
                    for kc in range(NKC):
                        bi = kc % 4
                        v.wait_ge(pe.h, ev(("xtr", tc, kc)))
                        ve.inc(v.tensor_copy(xv(kc, tc * 128, (tc + 1) * 128), pb(bi, 0, 128, 0, 128)))
                        setev(("xev", tc, kc), ve.n)
                v.wait_ge(pe.h, ev("kstr"))
                ve.inc(v.tensor_scalar(ks_col[:], pb(4, 0, 128, 0, 8), 0.125, None, ALU.mult))
                setev("ksev", ve.n)
                for kc in range(NKC):
                    v.wait_ge(pe.h, ev(("gwtr", kc)))
                    ve.inc(v.tensor_copy(gwT[:, kc * 9:(kc + 1) * 9], pb(5, 0, 128, kc * 9, kc * 9 + 9)))
                    setev(("gwev", kc), ve.n)
            on("vector", _v)
            barrier()

        def emit_ln(li):
            def _s(s):
                for kc in range(NKC):
                    for ft in range(FT):
                        idx = kc * FT + ft
                        q = (idx % 4) * 512
                        s.wait_ge(pe.h, ev(("sqmm", li, idx - 4)))
                        se.inc(s.activation(stage[:, q:q + 512],
                                            xv(kc, ft * 512, (ft + 1) * 512), AF.Square))
                        setev(("sq", li, idx), se.n)
            on("scalar", _s)

            def _t(t):
                for bi in range(8):
                    bank_claim(t, bi)
                for kc in range(NKC):
                    for ft in range(FT):
                        idx = kc * FT + ft
                        q = (idx % 4) * 512
                        t.matmul(pb(ft, 0, 1), ones_fc[:], xv(kc, ft * 512, (ft + 1) * 512),
                                 start=(kc == 0), stop=(kc == NKC - 1))
                        t.wait_ge(se.h, ev(("sq", li, idx)))
                        inst = t.matmul(pb(4 + ft, 0, 1), ones_fc[:], stage[:, q:q + 512],
                                        start=(kc == 0), stop=(kc == NKC - 1))
                        setev(("sqmm", li, idx), pe.inc(inst))
                setev(("stats", li), pe.n)
                for bi in range(8):
                    bank_release(bi, ve, ("lnst", li))
            on("tensor", _t)

            def _v(v):
                v.wait_ge(pe.h, ev(("stats", li)))
                for ft in range(FT):
                    v.tensor_scalar(r_mean[0:1, ft * 512:(ft + 1) * 512], pb(ft, 0, 1),
                                    1.0 / DM, None, ALU.mult)
                    ve.inc(v.tensor_scalar(r_tmp[0:1, ft * 512:(ft + 1) * 512], pb(4 + ft, 0, 1),
                                           1.0 / DM, None, ALU.mult))
                setev(("lnst", li), ve.n)
                v.wait_ge(ve.h, ve.n)
                ve.inc(v.tensor_mul(psA[0:1, 0:2048], r_mean[:], r_mean[:]))
                v.wait_ge(ve.h, ve.n)
                ve.inc(v.tensor_sub(r_tmp[:], r_tmp[:], psA[0:1, 0:2048]))
                setev(("var", li), ve.n)
            on("vector", _v)

            def _s2(s):
                s.wait_ge(ve.h, ev(("var", li)))
                se.inc(s.activation(r_tmp[:], r_tmp[:], AF.Sqrt, bias=epsc[0:1, 0:1]))
                setev(("std", li), se.n)
            on("scalar", _s2)

            def _v2(v):
                v.wait_ge(se.h, ev(("std", li)))
                ve.inc(v.reciprocal(r_tmp[:], r_tmp[:]))
                v.wait_ge(ve.h, ve.n)
                ve.inc(v.tensor_mul(r_mean[:], r_mean[:], r_tmp[:]))
                setev(("ab", li), ve.n)
            on("vector", _v2)

            def _t2(t):
                t.wait_ge(ve.h, ev(("ab", li)))
                for ft in range(FT):
                    bank_claim(t, ft)
                    bank_claim(t, 4 + ft)
                    t.matmul(pb(ft, 0, 128), ones_fr[:], r_tmp[0:1, ft * 512:(ft + 1) * 512],
                             start=True, stop=True)
                    inst = t.matmul(pb(4 + ft, 0, 128), ones_fr[:], r_mean[0:1, ft * 512:(ft + 1) * 512],
                                    start=True, stop=True)
                    setev(("bc", li, ft), pe.inc(inst))
                    bank_release(ft, ve, ("xnev", li))
                    bank_release(4 + ft, ve, ("xnev", li))
            on("tensor", _t2)

            def _v3(v):
                for ft in range(FT):
                    v.wait_ge(pe.h, ev(("bc", li, ft)))
                for kc in range(NKC):
                    for ft in range(FT):
                        ve.inc(v.tensor_mul(stage[:, 0:512], xv(kc, ft * 512, (ft + 1) * 512),
                                            pb(ft, 0, 128)))
                        v.wait_ge(ve.h, ve.n)
                        ve.inc(v.tensor_sub(xnv(kc, ft * 512, (ft + 1) * 512),
                                            stage[:, 0:512], pb(4 + ft, 0, 128)))
                        v.wait_ge(ve.h, ve.n)
                setev(("xnev", li), ve.n)
            on("vector", _v3)
            barrier()

        def emit_gating(gi):
            L3 = lg_sb[:].rearrange("p (tc e) -> p tc e", e=9)
            L8 = L3[:, :, 0:8]
            C3 = ctm[:].rearrange("p (tc e) -> p tc e", e=9)
            i3 = iota16[:].rearrange("p (tc e) -> p tc e", e=8)

            def r3(x):
                return x[:].rearrange("p (tc e) -> p tc e", e=8)

            def bc(x):
                return x[:].rearrange("p (tc one) -> p tc one", one=1).to_broadcast([128, NTC, 8])

            for tc in range(NTC):
                def _t(t, tc=tc):
                    bi = 6 + tc % 2
                    bank_claim(t, bi)
                    for kc in range(NKC):
                        inst = t.matmul(pb(bi, 0, 128, 0, 9),
                                        xn_sb[:, kc * T + tc * 128: kc * T + (tc + 1) * 128],
                                        gwT[:, kc * 9:(kc + 1) * 9],
                                        start=(kc == 0), stop=(kc == NKC - 1))
                    setev(("lg", gi, tc), pe.inc(inst))
                    bank_release(bi, ve, ("lgev", gi, tc))
                on("tensor", _t)

                def _v(v, tc=tc):
                    bi = 6 + tc % 2
                    v.wait_ge(pe.h, ev(("lg", gi, tc)))
                    ve.inc(v.tensor_copy(lg_sb[:, tc * 9:(tc + 1) * 9], pb(bi, 0, 128, 0, 9)))
                    setev(("lgev", gi, tc), ve.n)
                on("vector", _v)

            def _vc(v):
                def vsb(inst):
                    ve.inc(inst)
                    v.wait_ge(ve.h, ve.n)

                def ts16(dst, src_of, scol, op):
                    for tc in range(NTC):
                        inst = v.tensor_scalar(dst[:, tc * 8:(tc + 1) * 8], src_of(tc),
                                               scol[:, tc:tc + 1], None, op)
                    vsb(inst)
                v.wait_ge(ve.h, ve.n)
                vsb(v.tensor_reduce(cm1[:], L8, axis=AX.X, op=ALU.max))
                ts16(m16a, lambda tc: lg_sb[:, tc * 9:tc * 9 + 8], cm1, ALU.is_equal)
                vsb(v.scalar_tensor_tensor(m16b[:], m16a[:], -BIG, iotaB16[:], ALU.mult, ALU.add))
                vsb(v.tensor_reduce(ci1[:], r3(m16b), axis=AX.X, op=ALU.min))
                ts16(m16a, lambda tc: iota16[:, tc * 8:(tc + 1) * 8], ci1, ALU.is_equal)
                vsb(v.scalar_tensor_tensor(r3(m16b), r3(m16a), -BIG, L8, ALU.mult, ALU.add))
                vsb(v.tensor_reduce(cm2[:], r3(m16b), axis=AX.X, op=ALU.max))
                ts16(m16c, lambda tc: m16b[:, tc * 8:(tc + 1) * 8], cm2, ALU.is_equal)
                vsb(v.scalar_tensor_tensor(m16c[:], m16c[:], -BIG, iotaB16[:], ALU.mult, ALU.add))
                vsb(v.tensor_reduce(ci2[:], r3(m16c), axis=AX.X, op=ALU.min))
                vsb(v.tensor_sub(cdc[:], cm1[:], cm2[:]))
                setev(("cmb1", gi), ve.n)
            on("vector", _vc)

            def _sg(s):
                s.wait_ge(ve.h, ev(("cmb1", gi)))
                s.activation(sg1c[:], cdc[:], AF.Sigmoid)
                se.inc(s.activation(sgsh[:], L3[:, :, 8], AF.Sigmoid))
                setev(("sig", gi), se.n)
            on("scalar", _sg)

            def _vc2(v):
                def vsb(inst):
                    ve.inc(inst)
                    v.wait_ge(ve.h, ve.n)

                def ts16(dst, src_of, scol, op):
                    for tc in range(NTC):
                        inst = v.tensor_scalar(dst[:, tc * 8:(tc + 1) * 8], src_of(tc),
                                               scol[:, tc:tc + 1], None, op)
                    vsb(inst)
                v.wait_ge(se.h, ev(("sig", gi)))
                ts16(m16b, lambda tc: iota16[:, tc * 8:(tc + 1) * 8], ci2, ALU.is_equal)
                vsb(v.tensor_scalar(c1a[:], sg1c[:], 0.5, None, ALU.mult))
                vsb(v.tensor_scalar(c2a[:], sg1c[:], -0.5, 0.5, ALU.mult, ALU.add))
                ts16(m16a, lambda tc: m16a[:, tc * 8:(tc + 1) * 8], c1a, ALU.mult)
                ts16(m16b, lambda tc: m16b[:, tc * 8:(tc + 1) * 8], c2a, ALU.mult)
                vsb(v.tensor_tensor(C3[:, :, 0:8], r3(m16a), r3(m16b), ALU.add))
                vsb(v.tensor_scalar(C3[:, :, 8], sgsh[:], 0.5, None, ALU.mult))
                setev(("cmb", gi), ve.n)
            on("vector", _vc2)

            for tc in range(NTC):
                def _t2(t, tc=tc):
                    bi = 4 + tc % 2
                    t.wait_ge(ve.h, ev(("cmb", gi)))
                    bank_claim(t, bi)
                    pe.inc(t.transpose(pb(bi, 0, 9, 0, 128), ctm[:, tc * 9:(tc + 1) * 9], eye_f[:]))
                    setev(("ctr", gi, tc), pe.n)
                    bank_release(bi, ve, ("ctev", gi, tc))
                on("tensor", _t2)

                def _v3(v, tc=tc):
                    bi = 4 + tc % 2
                    v.wait_ge(pe.h, ev(("ctr", gi, tc)))
                    v.wait_ge(dxc[tc % 2].h, ev(("crdma", gi, tc - 2)))
                    ve.inc(v.tensor_copy(c_t9[:, (tc % 2) * 128:(tc % 2 + 1) * 128],
                                         pb(bi, 0, 9, 0, 128)))
                    setev(("ctev", gi, tc), ve.n)
                on("vector", _v3)

                def _sy(sy, tc=tc):
                    sy.wait_ge(ve.h, ev(("ctev", gi, tc)))
                    sm = dxc[tc % 2]
                    sy.dma_start(
                        crows_d[:, tc * 128:(tc + 1) * 128],
                        c_t9[:, (tc % 2) * 128:(tc % 2 + 1) * 128]).then_inc(sm.h, 16)
                    sm.n += 16
                    setev(("crdma", gi, tc), sm.n)
                on("sync", _sy)

            barrier()

        def chunk_src(c, which):
            e, ic = c // ICH, c % ICH
            if which == "g":
                base = wg_h[e] if e < 8 else swg_h
            else:
                base = wu_h[e] if e < 8 else swu_h
            return base[:, ic * 128:(ic + 1) * 128].rearrange("(kc p) m -> p kc m", p=128)

        def wd_src(c):
            e, ic = c // ICH, c % ICH
            base = wd_h[e] if e < 8 else swd_h
            return base[ic * 128:(ic + 1) * 128, :]

        def prev_ft(c, ft, k=2):
            if ft >= k:
                return (c, ft - k)
            return (c - 1, ft + FT - k)

        def emit_ffn(fi, cb_sb, crow_sb, h_sb, wd_sb, gs_st, us_st):
            parts = []
            c0 = 0
            while c0 < NCH:
                parts.append((c0, min(PART, NCH - c0)))
                c0 += PART
            LA = 2
            setev(("ffnpre", fi), pe.n)

            def cbv(e, ft):
                return cb_sb[:, (e % 2) * T + ft * 512:(e % 2) * T + (ft + 1) * 512]

            def _syc(sy):
                sy.wait_ge(dxc[0].h, ev(("crdma", fi, NTC - 2)))
                sy.wait_ge(dxc[1].h, ev(("crdma", fi, NTC - 1)))
                for e in range(NEXP):
                    if e >= 2:
                        sy.wait_ge(pe.h, ev(("cbmm", fi, e - 2, FT - 1)))
                    sm = dxl[e % 2]
                    sy.dma_start(crow_sb[0:1, (e % 2) * T:(e % 2 + 1) * T],
                                 crows_d[e:e + 1, :]).then_inc(sm.h, 16)
                    sm.n += 16
                    setev(("crl", fi, e), sm.n)
            on("sync", _syc)

            def _g(g):
                part_of = {}
                for pi, (pc0, pn) in enumerate(parts):
                    part_of[pc0 + min(1, pn - 1)] = pi
                for c in range(NCH):
                    g.wait_ge(pe.h, ev(("gudone", fi, c - LA)) if c >= LA else ev(("ffnpre", fi)))
                    sm = dwg[c % 2]
                    g.dma_start(wga[:, (c % 2) * DM:(c % 2 + 1) * DM]
                                .rearrange("p (kc m) -> p kc m", kc=8),
                                chunk_src(c, "g")).then_inc(sm.h, 16)
                    g.dma_start(wua[:, (c % 2) * DM:(c % 2 + 1) * DM]
                                .rearrange("p (kc m) -> p kc m", kc=8),
                                chunk_src(c, "u")).then_inc(sm.h, 16)
                    sm.n += 32
                    setev(("gudma", fi, c), sm.n)
                    if c in part_of:
                        pi = part_of[c]
                        pc0, pn = parts[pi]
                        g.wait_ge(pe.h, ev(("ddone", fi, pi - 1)) if pi > 0 else ev(("ffnpre", fi)))
                        for s_ in range(pn):
                            g.dma_start(wd_sb[:, s_ * DM:(s_ + 1) * DM], wd_src(pc0 + s_)).then_inc(dwd.h, 16)
                            dwd.n += 16
                        setev(("wddma", fi, pi), dwd.n)
            on("gpsimd", _g)

            def _t(t):
                for pi, (pc0, pn) in enumerate(parts):
                    for s_ in range(pn):
                        c = pc0 + s_
                        e = c // ICH
                        if c % ICH == 0:
                            t.wait_ge(dxl[e % 2].h, ev(("crl", fi, e)))
                            for ft in range(FT):
                                bi = 4 + ft
                                bank_claim(t, bi)
                                inst = t.matmul(pb(bi), ones_b[:],
                                                crow_sb[0:1, (e % 2) * T + ft * 512:(e % 2) * T + (ft + 1) * 512],
                                                start=True, stop=True)
                                setev(("cbmm", fi, e, ft), pe.inc(inst))
                                bank_release(bi, ve, ("cbev", fi, e, ft))
                        t.wait_ge(dwg[c % 2].h, ev(("gudma", fi, c)))
                        for ft in range(FT):
                            gb = ft % 2
                            ub = 2 + ft % 2
                            bank_claim(t, gb)
                            for kc in range(NKC):
                                inst = t.matmul(pb(gb),
                                                wga[:, (c % 2) * DM + kc * 128:(c % 2) * DM + (kc + 1) * 128],
                                                xnv(kc, ft * 512, (ft + 1) * 512),
                                                start=(kc == 0), stop=(kc == NKC - 1))
                            setev(("gmm", fi, c, ft), pe.inc(inst))
                            bank_release(gb, se, ("silu", fi, c, ft))
                            bank_claim(t, ub)
                            for kc in range(NKC):
                                inst = t.matmul(pb(ub),
                                                wua[:, (c % 2) * DM + kc * 128:(c % 2) * DM + (kc + 1) * 128],
                                                xnv(kc, ft * 512, (ft + 1) * 512),
                                                start=(kc == 0), stop=(kc == NKC - 1))
                            setev(("umm", fi, c, ft), pe.inc(inst))
                            bank_release(ub, ve, ("usc", fi, c, ft))
                        setev(("gudone", fi, c), pe.n)
                    t.wait_ge(dwd.h, ev(("wddma", fi, pi)))
                    for tb in range(FT):
                        t.wait_ge(ve.h, ev(("hevq", fi, pi, tb)))
                        for og in range(4):
                            b0 = 4 + 2 * (og % 2)
                            bank_claim(t, b0)
                            bank_claim(t, b0 + 1)
                            for kc in range(pn):
                                for o2 in range(2):
                                    o = og * 2 + o2
                                    inst = t.matmul(pb(b0 + o2),
                                                    wd_sb[:, kc * DM + o * 128: kc * DM + (o + 1) * 128],
                                                    h_sb[:, kc * T + tb * 512: kc * T + (tb + 1) * 512],
                                                    start=(kc == 0), stop=(kc == pn - 1))
                            setev(("dmm", fi, pi, tb, og), pe.inc(inst))
                            bank_release(b0, ve, ("dev", fi, pi, tb, og))
                            bank_release(b0 + 1, ve, ("dev", fi, pi, tb, og))
                    setev(("ddone", fi, pi), pe.n)
            on("tensor", _t)

            def _s(s):
                for pi, (pc0, pn) in enumerate(parts):
                    for s_ in range(pn):
                        c = pc0 + s_
                        for ft in range(FT):
                            s.wait_ge(pe.h, ev(("gmm", fi, c, ft)))
                            s.wait_ge(ve.h, ev(("hev", fi) + prev_ft(c, ft)))
                            se.inc(s.activation(gs_st[:, (ft % 2) * 512:(ft % 2 + 1) * 512],
                                                pb(ft % 2), AF.Silu))
                            setev(("silu", fi, c, ft), se.n)
            on("scalar", _s)

            def _v(v):
                for pi, (pc0, pn) in enumerate(parts):
                    for s_ in range(pn):
                        c = pc0 + s_
                        e = c // ICH
                        if c % ICH == 0:
                            for ft in range(FT):
                                v.wait_ge(pe.h, ev(("cbmm", fi, e, ft)))
                                ve.inc(v.tensor_copy(cbv(e, ft), pb(4 + ft)))
                                setev(("cbev", fi, e, ft), ve.n)
                            v.wait_ge(ve.h, ve.n)
                        for ft in range(FT):
                            v.wait_ge(pe.h, ev(("umm", fi, c, ft)))
                            ve.inc(v.tensor_mul(us_st[:, (ft % 2) * 512:(ft % 2 + 1) * 512],
                                                pb(2 + ft % 2),
                                                cbv(e, ft)))
                            setev(("usc", fi, c, ft), ve.n)
                            v.wait_ge(ve.h, ve.n)
                            v.wait_ge(se.h, ev(("silu", fi, c, ft)))
                            ve.inc(v.tensor_mul(h_sb[:, s_ * T + ft * 512: s_ * T + (ft + 1) * 512],
                                                gs_st[:, (ft % 2) * 512:(ft % 2 + 1) * 512],
                                                us_st[:, (ft % 2) * 512:(ft % 2 + 1) * 512]))
                            setev(("hev", fi, c, ft), ve.n)
                            if s_ == pn - 1:
                                v.wait_ge(ve.h, ve.n)
                                ve.inc(v.tensor_copy(c1a[:, 0:2],
                                                     h_sb[:, s_ * T + ft * 512 + 510:
                                                          s_ * T + ft * 512 + 512]))
                                setev(("hevq", fi, pi, ft), ve.n)
                    for tb in range(FT):
                        for og in range(4):
                            b0 = 4 + 2 * (og % 2)
                            v.wait_ge(pe.h, ev(("dmm", fi, pi, tb, og)))
                            for o2 in range(2):
                                o = og * 2 + o2
                                inst = v.tensor_add(xv(o, tb * 512, (tb + 1) * 512),
                                                    xv(o, tb * 512, (tb + 1) * 512),
                                                    pb(b0 + o2))
                            ve.inc(inst)
                            setev(("dev", fi, pi, tb, og), ve.n)
            on("vector", _v)
            barrier()

        def emit_attention(AT, n_heads=NH, proj_only=False):
            (k_ext2, v_t, mk_rhse, q_sb, v_fm, zb_sb, att_st, p_ts, vfm_close) = AT
            rhs_ext2 = None
            NQT, NJC, NIC = 16, 16, 4
            setev("attpre_pe", pe.n)
            setev("attpre_ve", ve.n)

            def _g(g):
                g.wait_ge(ve.h, ev("attpre_ve"))
                g.wait_ge(pe.h, ev("attpre_pe"))
                ge.inc(g.memset(v_t[:], 0.0))
                for kv in range(NKV):
                    ge.inc(g.memset(k_ext2[kv][64:128, :], 0.0))
                g.wait_ge(ge.h, ge.n)
                g.memset(v_t[:].rearrange("p (j c) -> p j c", c=128)[:, :, 64:65], 1.0)
                for kv in range(NKV):
                    g.memset(k_ext2[kv][64:65, :], 1.0)
                ge.inc(g.memset(m_bf[:], 0.0))
                setev("attc", ge.n)
                for mq in range(12):
                    g.wait_ge(pe.h, ev(("pmm", mq - 2, FT - 1)) if mq >= 2 else ev("attpre_pe"))
                    src = (wq_h[:, mq * 128:(mq + 1) * 128] if mq < 8
                           else wkv_h[:, (mq - 8) * 128:(mq - 7) * 128])
                    sm = dwg[mq % 2]
                    g.dma_start(wga[:, (mq % 2) * DM:(mq % 2 + 1) * DM]
                                .rearrange("p (kc m) -> p kc m", kc=8),
                                src.rearrange("(kc p) m -> p kc m", p=128)).then_inc(sm.h, 16)
                    sm.n += 16
                    setev(("wdma", mq), sm.n)
            on("gpsimd", _g)

            def _t(t):
                t.wait_ge(ge.h, ev("attc"))
                for mq in range(12):
                    t.wait_ge(dwg[mq % 2].h, ev(("wdma", mq)))
                    for ft in range(FT):
                        bi = ft % 4
                        bank_claim(t, bi)
                        for kc in range(NKC):
                            inst = t.matmul(pb(bi),
                                            wga[:, (mq % 2) * DM + kc * 128:(mq % 2) * DM + (kc + 1) * 128],
                                            xnv(kc, ft * 512, (ft + 1) * 512),
                                            start=(kc == 0), stop=(kc == NKC - 1))
                        setev(("pmm", mq, ft), pe.inc(inst))
                        bank_release(bi, ve, ("pev", mq, ft))
            on("tensor", _t)

            def _v(v):
                for mq in range(12):
                    for ft in range(FT):
                        bi = ft % 4
                        v.wait_ge(pe.h, ev(("pmm", mq, ft)))
                        if mq < 8:
                            inst = v.tensor_scalar(q_sb[:, mq * T + ft * 512: mq * T + (ft + 1) * 512],
                                                   pb(bi), ks_col[:, mq:mq + 1], None, ALU.mult)
                        elif mq < 10:
                            kk = mq - 8
                            v.tensor_copy(k_ext2[2 * kk][0:64, ft * 512:(ft + 1) * 512], pb(bi, 0, 64))
                            inst = v.tensor_copy(k_ext2[2 * kk + 1][0:64, ft * 512:(ft + 1) * 512],
                                                 pb(bi, 64, 128))
                        else:
                            vv = mq - 10
                            inst = v.tensor_copy(v_fm[:, vv * T + ft * 512: vv * T + (ft + 1) * 512],
                                                 pb(bi))
                        ve.inc(inst)
                        setev(("pev", mq, ft), ve.n)
            on("vector", _v)

            def _t2(t):
                for idx in range(32):
                    vv, jc = idx // 16, idx % 16
                    t.wait_ge(ve.h, ev(("pev", 10 + vv, (jc * 128) // 512)))
                    bank_claim(t, 6)
                    pe.inc(t.transpose(pb(6, 0, 128, 0, 64).bitcast(BF16),
                                       v_fm[:, vv * T + jc * 128: vv * T + (jc + 1) * 128],
                                       eye_b[:]))
                    setev(("vt", idx), pe.n)
                    bank_release(6, ve, ("vtev", idx))
            on("tensor", _t2)

            def _v2(v):
                for idx in range(32):
                    vv, jc = idx // 16, idx % 16
                    v.wait_ge(pe.h, ev(("vt", idx)))
                    src = pb(6, 0, 128, 0, 64).bitcast(BF16)
                    for hh in range(2):
                        kvh = 2 * vv + hh
                        inst = v.tensor_copy(v_t[:, kvh * 2048 + jc * 128: kvh * 2048 + jc * 128 + 64],
                                             src[:, hh * 64:(hh + 1) * 64])
                    ve.inc(inst)
                    setev(("vtev", idx), ve.n)
            on("vector", _v2)
            barrier()
            vfm_close()
            rhs_ext2 = mk_rhse()

            def prev_avp(h, ic, jp):
                if jp >= 2:
                    return ("avmm", h, ic, (jp - 2) * 2 + 1)
                if ic >= 1:
                    return ("avmm", h, ic - 1, (6 + jp) * 2 + 1)
                if h >= 1:
                    return ("avmm", h - 1, NIC - 1, (6 + jp) * 2 + 1)
                return None

            def head_passA_qt(h, qt):
                kv = h // 4

                if qt == 0:
                    def _vq(v):
                        if h >= 2:
                            v.wait_ge(pe.h, ev(("bdone", h - 2)))
                        v.wait_ge(ve.h, ve.n)
                        ve.inc(v.tensor_copy(rhs_ext2[h % 2][0:64, :],
                                             q_sb[(h % 2) * 64:(h % 2) * 64 + 64,
                                                  (h // 2) * T:(h // 2 + 1) * T]))
                        v.wait_ge(ve.h, ve.n)
                        ve.inc(v.memset(rhs_ext2[h % 2][64:65, :] if h >= 2
                                        else rhs_ext2[h % 2][64:128, :], 0.0))
                        setev(("qcp", h), ve.n)
                    on("vector", _vq)

                ABANK = [0, 1, 6, 7]

                def _t(t):
                    if qt == 0:
                        t.wait_ge(ve.h, ev(("qcp", h)))
                    for jj in range(FT):
                        bi = ABANK[jj]
                        bank_claim(t, bi)
                        inst = t.matmul(pb(bi),
                                        rhs_ext2[h % 2][0:128, qt * 128:(qt + 1) * 128],
                                        k_ext2[kv][0:128, jj * 512:(jj + 1) * 512],
                                        start=True, stop=True)
                        setev(("amm", h, qt, jj), pe.inc(inst))
                        bank_release(bi, ve, ("amax", h, qt, jj // 2))
                on("tensor", _t)

                def _v(v):
                    if qt == 0 and h >= 2:
                        v.wait_ge(pe.h, ev(("mt", h - 2)))
                    for jp in range(2):
                        v.wait_ge(pe.h, ev(("amm", h, qt, jp * 2 + 1)))
                        src = psA[:, 0:1024] if jp == 0 else psB[:, 1024:2048]
                        ve.inc(v.tensor_reduce(mm4[:, jp:jp + 1], src,
                                               axis=AX.X, op=ALU.max))
                        setev(("amax", h, qt, jp), ve.n)
                    v.wait_ge(ve.h, ve.n)
                    ve.inc(v.tensor_reduce(m_bf[:, (h % 2) * 16 + qt:(h % 2) * 16 + qt + 1],
                                           mm4[:, 0:2], axis=AX.X, op=ALU.max, negate=True))
                    v.wait_ge(ve.h, ve.n)
                    setev(("mq", h, qt), ve.n)
                on("vector", _v)

            def head_mrow(h):
                def _t(t):
                    t.wait_ge(ve.h, ev(("mq", h, 15)))
                    bank_claim(t, 2)
                    pe.inc(t.transpose(pb(2, 0, 16, 0, 64).bitcast(BF16),
                                       m_bf[:, (h % 2) * 16:(h % 2) * 16 + 16], eye_b[:]))
                    setev(("mt", h), pe.n)
                    bank_release(2, ve, ("mtev", h))
                on("tensor", _t)

                def _v(v):
                    v.wait_ge(pe.h, ev(("mt", h)))
                    if h >= 2:
                        v.wait_ge(dxm[h % 2].h, ev(("mrow", h - 2)))
                    ve.inc(v.tensor_copy(m_t[:, (h % 2) * 128:(h % 2) * 128 + 128],
                                         pb(2, 0, 16, 0, 64).bitcast(BF16)))
                    setev(("mtev", h), ve.n)
                on("vector", _v)

                def _sy(sy):
                    sy.wait_ge(ve.h, ev(("mtev", h)))
                    sm = dxm[h % 2]
                    sy.dma_start(rhs_ext2[h % 2][64:65, :].rearrange("one (p j) -> one p j", p=16),
                                 m_t[:, (h % 2) * 128:(h % 2) * 128 + 128]).then_inc(sm.h, 16)
                    sm.n += 16
                    setev(("mrow", h), sm.n)
                on("sync", _sy)

            def head_passB(h):
                kv = h // 4

                def _t(t, ic, q4):
                    if ic == 0 and q4 == 0:
                        t.wait_ge(dxm[h % 2].h, ev(("mrow", h)))
                    av = 4 + ic % 2
                    if q4 == 0:
                        bank_claim(t, av)
                    for jp in range(q4 * 2, q4 * 2 + 2):
                        bank_claim(t, 2)
                        bank_claim(t, 3)
                        for j2 in range(2):
                            jc = jp * 2 + j2
                            inst = t.matmul(pb(2 + j2), k_ext2[kv][0:128, jc * 128:(jc + 1) * 128],
                                            rhs_ext2[h % 2][0:128, ic * 512:(ic + 1) * 512],
                                            start=True, stop=True)
                            setev(("st", h, ic, jc), pe.inc(inst))
                        bank_release(2, se, ("exp", h, ic, jp))
                        bank_release(3, se, ("exp", h, ic, jp))
                        if jp >= 1:
                            t.wait_ge(se.h, ev(("exp", h, ic, jp - 1)))
                            for j2 in range(2):
                                jc = (jp - 1) * 2 + j2
                                inst = t.matmul(pb(av, 0, 65),
                                                v_t[:, kv * 2048 + jc * 128: kv * 2048 + jc * 128 + 65],
                                                p_ts[(jp - 1) % 2][:, j2 * 512:(j2 + 1) * 512],
                                                start=(jc == 0), stop=False)
                                setev(("avmm", h, ic, jc), pe.inc(inst))
                    if q4 == 3:
                        jp = 7
                        t.wait_ge(se.h, ev(("exp", h, ic, jp)))
                        for j2 in range(2):
                            jc = jp * 2 + j2
                            inst = t.matmul(pb(av, 0, 65),
                                            v_t[:, kv * 2048 + jc * 128: kv * 2048 + jc * 128 + 65],
                                            p_ts[jp % 2][:, j2 * 512:(j2 + 1) * 512],
                                            start=False, stop=(j2 == 1))
                            setev(("avmm", h, ic, jc), pe.inc(inst))
                        setev(("av", h, ic), pe.n)
                        bank_release(av, ve, ("omul", h, ic))

                def _s(s, ic):
                    for jp in range(8):
                        s.wait_ge(pe.h, ev(("st", h, ic, jp * 2 + 1)))
                        pv = prev_avp(h, ic, jp)
                        if pv is not None:
                            s.wait_ge(pe.h, ev(pv))
                        se.inc(s.activation(p_ts[jp % 2][:], psA[:, 1024:2048], AF.Exp))
                        setev(("exp", h, ic, jp), se.n)
                    s.wait_ge(pe.h, ev(("av", h, ic)))
                    se.inc(s.activation(zr[:], pb(4 + ic % 2, 64, 65), AF.Ln))
                    s.wait_ge(se.h, se.n)
                    if ic >= 1:
                        s.wait_ge(pe.h, ev(("zbmm", h, ic - 1)))
                    elif h >= 1:
                        s.wait_ge(pe.h, ev(("zbmm", h - 1, NIC - 1)))
                    se.inc(s.activation(zrb[:], zr[:], AF.Exp, bias=0.0, scale=-1.0))
                    s.wait_ge(se.h, se.n)
                    setev(("zrb", h, ic), se.n)

                def _v(v, ic):
                    pass

                def _t2(t, ic):
                    t.wait_ge(se.h, ev(("zrb", h, ic)))
                    bank_claim(t, 3)
                    pe.inc(t.matmul(pb(3, 0, 64), ones_b[0:1, 0:64], zrb[:],
                                    start=True, stop=True))
                    setev(("zbmm", h, ic), pe.n)
                    bank_release(3, ve, ("zbcp", h, ic))

                def _v2(v, ic):
                    av = 4 + ic % 2
                    v.wait_ge(pe.h, ev(("zbmm", h, ic)))
                    ve.inc(v.tensor_copy(zb_sb[:], pb(3, 0, 64)))
                    setev(("zbcp", h, ic), ve.n)
                    v.wait_ge(ve.h, ve.n)
                    ve.inc(v.tensor_mul(att_st[0:64, :], pb(av, 0, 64), zb_sb[:]))
                    setev(("omul", h, ic), ve.n)
                    v.wait_ge(ve.h, ve.n)
                    half = att_st[0:64, :]
                    if h % 2 == 1:
                        ve.inc(v.tensor_copy(att_st[64:128, :], att_st[0:64, :]))
                        v.wait_ge(ve.h, ve.n)
                        half = att_st[64:128, :]
                    ve.inc(v.tensor_add(
                        x_sb[(h % 2) * 64:(h % 2) * 64 + 64,
                             (h // 2) * T + ic * 512:(h // 2) * T + (ic + 1) * 512],
                        x_sb[(h % 2) * 64:(h % 2) * 64 + 64,
                             (h // 2) * T + ic * 512:(h // 2) * T + (ic + 1) * 512],
                        half))
                    setev(("oev", h, ic), ve.n)

                return _t, _t2, _s, _v, _v2

            if proj_only:
                return

            def emit_B_unit(h, u, fns):
                _t, _t2, _s, _v, _v2 = fns
                ic, q4 = u // 4, u % 4
                if u == 0:
                    def _sall(s_):
                        for ic2 in range(NIC):
                            _s(s_, ic2)
                    on("scalar", _sall)
                on("tensor", lambda t: _t(t, ic, q4))
                if q4 == 3:
                    if ic >= 1:
                        on("tensor", lambda t: _t2(t, ic - 1))
                        on("vector", lambda v: _v2(v, ic - 1))
                    if ic == NIC - 1:
                        on("tensor", lambda t: _t2(t, NIC - 1))
                        on("vector", lambda v: _v2(v, NIC - 1))
                        setev(("bdone", h), snap[("av", h, NIC - 1)])

            bfns = {}
            for h in range(n_heads + 1):
                for u in range(16):
                    if h < n_heads:
                        head_passA_qt(h, u)
                    if h >= 1:
                        emit_B_unit(h - 1, u, bfns[h - 1])
                if h < n_heads:
                    head_mrow(h)
                    bfns[h] = head_passB(h)
            barrier()

        def emit_epilogue():
            def _t(t):
                for tc in range(NTC):
                    for kc in range(NKC):
                        bi = kc % 4
                        bank_claim(t, bi)
                        pe.inc(t.transpose(pb(bi, 0, 128, 0, 128),
                                           xv(kc, tc * 128, (tc + 1) * 128), eye_f[:]))
                        setev(("ytr", tc, kc), pe.n)
                        bank_release(bi, ve, ("yevk", tc, kc))
            on("tensor", _t)

            def _v(v):
                for tc in range(NTC):
                    v.wait_ge(dxx[tc % 2].h, ev(("yst", tc - 2)))
                    for kc in range(NKC):
                        bi = kc % 4
                        v.wait_ge(pe.h, ev(("ytr", tc, kc)))
                        ve.inc(v.tensor_copy(stage[:, (tc % 2) * DM + kc * 128:(tc % 2) * DM + (kc + 1) * 128],
                                             pb(bi, 0, 128, 0, 128)))
                        setev(("yevk", tc, kc), ve.n)
                    setev(("yev", tc), ve.n)
            on("vector", _v)

            def _sy(sy):
                for tc in range(NTC):
                    sy.wait_ge(ve.h, ev(("yev", tc)))
                    sm = dxx[tc % 2]
                    sy.dma_start(y_h[tc * 128:(tc + 1) * 128, :],
                                 stage[:, (tc % 2) * DM:(tc % 2 + 1) * DM]).then_inc(sm.h, 16)
                    sm.n += 16
                    setev(("yst", tc), sm.n)
                sy.wait_ge(dxx[0].h, dxx[0].n)
                sy.wait_ge(dxx[1].h, dxx[1].n)
                sy.wait_ge(dx.h, dx.n)
            on("sync", _sy)

        def emit_all():
            bankuse.clear()
            with ExitStack() as pst:
                pro_box["ks_st"] = scoped(pst, "ks_st", [8, 128], F32)
                pro_box["gsw_sb"] = scoped(pst, "gsw_sb", [9, DM], F32)
                emit_prologue()
            emit_ln(0)
            if upto == "ln1":
                return
            with ExitStack() as fst:
                emit_gating(0)
                if upto == "gate1":
                    return
                cb_sb = scoped(fst, "cb1", [128, 2 * T], BF16)
                crow_sb = scoped(fst, "crow1", [1, 2 * T], BF16)
                h_sb = scoped(fst, "h1", [128, PART * T], BF16)
                wd_sb = scoped(fst, "wd1", [128, PART * DM], BF16)
                gs_st = scoped(fst, "gs1", [128, 2 * 512], BF16)
                us_st = scoped(fst, "us1", [128, 2 * 512], BF16)
                emit_ffn(0, cb_sb, crow_sb, h_sb, wd_sb, gs_st, us_st)
            if upto == "ffn1":
                return
            emit_ln(1)
            if upto == "ln2":
                return
            with ExitStack() as ast_:
                def asb(n_, s_, d_):
                    return scoped(ast_, n_, s_, d_)
                k_ext2 = [asb(f"kext{i}", [128, T], BF16) for i in range(NKV)]
                v_t = asb("v_t", [128, NKV * 16 * 128], BF16)
                q_sb = asb("q_sb", [128, NKC * T], BF16)
                vf_st = ExitStack()
                v_fm = TEN["v_fm"] = vf_st.enter_context(
                    nc.sbuf_tensor(f"v_fm_p{ctx['pass']}", [128, 2 * T], BF16))

                def mk_rhse():
                    return [asb(f"rhse{i}", [128, T], BF16) for i in range(2)]
                att_st = stage[:, 0:512]
                zb_sb = stage[0:64, 512:1024]
                p_t = stage[:, 1024:1536].bitcast(BF16)
                p_t2 = stage[:, 1536:2048].bitcast(BF16)
                if upto.startswith("attn") and upto != "attn":
                    nhh = 0 if upto == "attn0" else int(upto[4:])
                    emit_attention((k_ext2, v_t, mk_rhse, q_sb, v_fm, zb_sb, att_st, (p_t, p_t2),
                                    vf_st.close),
                                   n_heads=max(nhh, 1), proj_only=(nhh == 0))
                    return
                emit_attention((k_ext2, v_t, mk_rhse, q_sb, v_fm, zb_sb, att_st, (p_t, p_t2), vf_st.close))
            if upto == "attn":
                return
            emit_ln(2)
            if upto == "ln3":
                return
            with ExitStack() as fst:
                emit_gating(1)
                cb_sb = scoped(fst, "cb2", [128, 2 * T], BF16)
                crow_sb = scoped(fst, "crow2", [1, 2 * T], BF16)
                h_sb = scoped(fst, "h2", [128, PART * T], BF16)
                wd_sb = scoped(fst, "wd2", [128, PART * DM], BF16)
                gs_st = scoped(fst, "gs2", [128, 2 * 512], BF16)
                us_st = scoped(fst, "us2", [128, 2 * 512], BF16)
                emit_ffn(1, cb_sb, crow_sb, h_sb, wd_sb, gs_st, us_st)

        def emit_taps():
            if not taps:
                return

            def _sy(sy):
                sy.wait_ge(ve.h, ve.n)
                sy.wait_ge(se.h, se.n)
                sy.wait_ge(pe.h, pe.n)
                sy.wait_ge(ge.h, ge.n)
                for tn, shape, dt in taps:
                    src = crows_d if tn == "crows_d" else TEN[tn][0:shape[0], 0:shape[1]]
                    sy.dma_start(tap_out[tn], src).then_inc(dx.h, 16)
                    dx.n += 16
                sy.wait_ge(dx.h, dx.n)
            on("sync", _sy)

        ctx["real"] = False
        ctx["pass"] = 1
        emit_all()
        if upto == "all":
            emit_epilogue()
        emit_taps()
        pass1 = [s_.n for s_ in ALLSC]
        for s_ in ALLSC:
            s_.n = 0
        ctx["real"] = True
        ctx["pass"] = 2
        emit_all()
        if upto == "all":
            emit_epilogue()
        emit_taps()
        pass2 = [s_.n for s_ in ALLSC]
        assert pass1 == pass2, (pass1, pass2)
    return nc


def _in_maps(L0_attn_x, L0_attn_Wq, L0_attn_Wkv, L0_attn_k_scale,
             L0_ffn_gate_w, L0_ffn_w_gate, L0_ffn_w_up, L0_ffn_w_down,
             L0_ffn_sw_gate, L0_ffn_sw_up, L0_ffn_sw_down, L0_ffn_shared_gate_w):
    shared = {
        "wq": np.ascontiguousarray(L0_attn_Wq, np.float32),
        "wkv": np.ascontiguousarray(L0_attn_Wkv, np.float32),
        "ks": np.ascontiguousarray(np.asarray(L0_attn_k_scale, np.float32).reshape(8, 128)),
        "gw": np.ascontiguousarray(L0_ffn_gate_w, np.float32),
        "sgw": np.ascontiguousarray(np.asarray(L0_ffn_shared_gate_w, np.float32).reshape(1, DM)),
        "wg": np.ascontiguousarray(L0_ffn_w_gate, np.float32),
        "wu": np.ascontiguousarray(L0_ffn_w_up, np.float32),
        "wd": np.ascontiguousarray(L0_ffn_w_down, np.float32),
        "swg": np.ascontiguousarray(L0_ffn_sw_gate, np.float32),
        "swu": np.ascontiguousarray(L0_ffn_sw_up, np.float32),
        "swd": np.ascontiguousarray(L0_ffn_sw_down, np.float32),
    }
    x = np.asarray(L0_attn_x, np.float32)
    return [dict(shared, x=np.ascontiguousarray(x[c])) for c in range(8)]


def kernel(**inputs):
    nc = build_program()
    in_maps = _in_maps(**inputs)
    res = run_bass_kernel_spmd(nc, in_maps, core_ids=list(range(8)))
    return np.stack([res.results[c]["y"] for c in range(8)], axis=0)
